# version 09
# baseline (speedup 1.0000x reference)
import contextlib
import numpy as np

import concourse.bass as bass
import concourse.mybir as mybir
from concourse.bass_utils import run_bass_kernel_spmd

F32 = mybir.dt.float32
F16 = mybir.dt.float16
BF16 = mybir.dt.bfloat16
FP8 = mybir.dt.float8e4
ALU = mybir.AluOpType
ACT = mybir.ActivationFunctionType
AX = mybir.AxisListType

D = 1024
DI = 1024
BC = 4
T = 2048
C = 64
NT = T // C
NTOK = BC * C
KC = D // 128
FF = 4096
OC = FF // 128
LN_EPS = 1e-5

ENGINES = ("sync", "tensor", "vector", "scalar", "gpsimd")


class Plan:

    def __init__(self, nc):
        self.nc = nc
        self.ops = {e: [] for e in ENGINES}
        self.cnt = {}
        self.sem_objs = {}
        self.last_wait = {e: {} for e in ENGINES}
        self.own = {"sync": None, "tensor": "PE", "vector": "V",
                    "scalar": "S", "gpsimd": "G"}

    def add_sem(self, name, obj):
        self.sem_objs[name] = obj
        self.cnt[name] = 0

    def emit(self, eng, fn, waits=(), inc=None, inc_n=1):
        rw = []
        for w in waits:
            if w is None:
                continue
            s, v = w
            if v <= 0:
                continue
            assert v <= self.cnt[s], f"wait on future {s}:{v} > {self.cnt[s]}"
            if s == self.own[eng]:
                continue
            if self.last_wait[eng].get(s, -1) >= v:
                continue
            rw.append((s, v))
            self.last_wait[eng][s] = v
        tok = None
        if inc is not None:
            self.cnt[inc] += inc_n
            tok = (inc, self.cnt[inc])
        self.ops[eng].append((fn, rw, (inc, inc_n) if inc else None))
        return tok

    def ck(self, sem):
        return (sem, self.cnt[sem])

    def build(self, block):
        sem_objs = self.sem_objs
        for eng in ENGINES:
            ops = self.ops[eng]
            if not ops:
                continue

            def make(ops):
                def run(e):
                    for fn, waits, inc in ops:
                        for s, v in waits:
                            e.wait_ge(sem_objs[s], v)
                        r = fn(e)
                        if inc is not None:
                            s, n = inc
                            r.then_inc(sem_objs[s], n)
                return run
            getattr(block, eng)(make(ops))


def build_program(dump=(), nta=NT, ntb=NT, pha=99):
    nc = bass.Bass("TRN2", target_bir_lowering=False,
                   detect_race_conditions=False)
    dump = set(dump)

    x_t = nc.dram_tensor("x_t", [D, NT, NTOK], F32, kind="ExternalInput").ap()
    w_in = {}
    for w in ("wq", "wk", "wv", "wg", "wo"):
        w_in[w] = nc.dram_tensor(w + "_t", [D, DI], F32, kind="ExternalInput").ap()
    wg1_t = nc.dram_tensor("wg1_t", [D, FF], F32, kind="ExternalInput").ap()
    wu1_t = nc.dram_tensor("wu1_t", [D, FF], F32, kind="ExternalInput").ap()
    wd1_t = nc.dram_tensor("wd1_t", [FF, D], F32, kind="ExternalInput").ap()
    b_in = {}
    for b in ("bq", "bk", "bv", "bg", "bo"):
        b_in[b] = nc.dram_tensor(b + "_c", [128, KC], F32, kind="ExternalInput").ap()
    gamma_c = nc.dram_tensor("gamma_c", [128, KC], F32, kind="ExternalInput").ap()
    beta_c = nc.dram_tensor("beta_c", [128, KC], F32, kind="ExternalInput").ap()
    alpha_in = nc.dram_tensor("alpha_c", [128, KC], F32, kind="ExternalInput").ap()
    bv_row = nc.dram_tensor("bv_row", [1, DI], F32, kind="ExternalInput").ap()
    mask_in = nc.dram_tensor("mask_t", [128, NTOK], F32, kind="ExternalInput").ap()
    onesc_in = nc.dram_tensor("ones_col", [128, 1], F32, kind="ExternalInput").ap()
    onesr_in = nc.dram_tensor("ones_row", [1, 128], F32, kind="ExternalInput").ap()
    ident_in = nc.dram_tensor("ident64", [64, 64], F32, kind="ExternalInput").ap()
    out_t = nc.dram_tensor("out_t", [D, NT, NTOK], F32, kind="ExternalOutput").ap()

    whi_s = nc.dram_tensor("whi_s", [5, 128, KC, DI], F16, kind="Internal").ap()
    wlo_s = nc.dram_tensor("wlo_s", [5, 128, KC, DI], BF16, kind="Internal").ap()
    tg_s = nc.dram_tensor("tg_s", [128, KC, FF], FP8, kind="Internal").ap()
    tu_s = nc.dram_tensor("tu_s", [128, KC, FF], FP8, kind="Internal").ap()
    td_s = nc.dram_tensor("td_s", [128, OC, D], FP8, kind="Internal").ap()
    i1_s = nc.dram_tensor("i1_s", [NT, 128, KC, NTOK], BF16, kind="Internal").ap()
    x2_s = nc.dram_tensor("x2_s", [NT, 128, KC, NTOK], F32, kind="Internal").ap()
    iv1_s = nc.dram_tensor("iv1_s", [NT, 1, NTOK], F32, kind="Internal").ap()

    dumps = {}
    def dump_out(name, shape, dtype=F32):
        if name in dump and name not in dumps:
            dumps[name] = nc.dram_tensor("dbg_" + name, shape, dtype,
                                         kind="ExternalOutput").ap()
        return dumps.get(name)

    es = contextlib.ExitStack()
    with es:
        p = Plan(nc)
        for s in ("dS", "dG", "PE", "V", "S", "G"):
            p.add_sem(s, es.enter_context(nc.semaphore(s)))

        def sb(name, shape, dt=F32):
            return es.enter_context(nc.sbuf_tensor(name, shape, dt))

        pb = [es.enter_context(nc.psum_tensor("pb%d" % i, [128, 512], F32))
              for i in range(8)]
        bias = {b: sb(b, [128, KC]) for b in ("bq", "bk", "bv", "bg", "bo")}
        gam = sb("gam", [128, KC]); bet = sb("bet", [128, KC])
        alpha = sb("alpha", [128, KC])
        a64 = sb("a64", [128, KC])
        mghat = sb("mghat", [128, 3])
        mask = sb("mask", [128, NTOK])
        ones_c = sb("ones_c", [128, 1])
        ones_r = sb("ones_r", [1, 128])
        ident = sb("ident", [128, 64])
        eps1 = sb("eps1", [1, 1])
        rows = sb("rows", [1, 512])
        rows2 = sb("rows2", [1, 512])
        negc = sb("negc", [128, 8])
        aq = sb("aq", [128, KC, 64])
        ak = sb("ak", [128, KC, 64])
        bvbc = sb("bvbc", [128, DI])

        def dma(eng, out, in_, waits=()):
            return p.emit(eng, lambda e, o=out, i=in_: e.dma_start(out=o, in_=i),
                          waits=waits,
                          inc="dS" if eng in ("sync", "scalar") else "dG", inc_n=16)

        for dst, src in [(mask[:, :], mask_in[:, :]),
                         (ones_c[:, :], onesc_in[:, :]),
                         (ones_r[:, :], onesr_in[:, :]),
                         (ident[0:64, :], ident_in[:, :]),
                         (ident[64:128, :], ident_in[:, :]),
                         (gam[:, :], gamma_c[:, :]),
                         (bet[:, :], beta_c[:, :]),
                         (alpha[:, :], alpha_in[:, :])]:
            dma("sync", dst, src)
        for b in bias:
            dma("sync", bias[b][:, :], b_in[b][:, :])
        t_consts = p.ck("dS")

        p.emit("vector", lambda e: e.memset(eps1[:, :], LN_EPS), waits=[], inc="V")
        for k in range(8):
            cval = -(127.5 - 2.0 ** -17)
            p.emit("vector", lambda e, k=k, c=cval: e.memset(negc[:, k:k + 1], c),
                   waits=[], inc="V")
        tok_alpha = p.emit("scalar", lambda e: e.activation(
            alpha[:, :], alpha[:, :], ACT.Sigmoid), waits=[t_consts], inc="S")

        esp = contextlib.ExitStack()
        def sbp(name, shape, dt=F32):
            return esp.enter_context(nc.sbuf_tensor(name, shape, dt))
        apow = sbp("apow", [128, 64])
        apneg = sbp("apneg", [128, 64])
        acols = sbp("acols", [128, 16])
        def vdrain():
            p.emit("vector", lambda e: e.drain(), waits=[], inc=None)

        for c in range(KC):
            al = alpha[:, c:c + 1]
            def em(fn, waits=()):
                tok = p.emit("vector", fn, waits=waits, inc="V")
                vdrain()
                return tok
            w0 = [tok_alpha]
            em(lambda e, al=al: e.tensor_tensor(acols[:, 0:1], al, al, ALU.mult), w0)
            for j in range(1, 5):
                em(lambda e, j=j: e.tensor_tensor(
                    acols[:, j:j + 1], acols[:, j - 1:j], acols[:, j - 1:j], ALU.mult))
            em(lambda e, c=c: e.tensor_tensor(
                a64[:, c:c + 1], acols[:, 4:5], acols[:, 4:5], ALU.mult))
            em(lambda e, al=al: e.reciprocal(acols[:, 5:6], al), w0)
            for j in range(6, 11):
                em(lambda e, j=j: e.tensor_tensor(
                    acols[:, j:j + 1], acols[:, j - 1:j], acols[:, j - 1:j], ALU.mult))
            em(lambda e: e.memset(apow[:, 0:1], 1.0))
            em(lambda e, al=al: e.tensor_copy(apow[:, 1:2], al), w0)
            for j in range(1, 6):
                n = 2 ** j
                em(lambda e, n=n, j=j: e.tensor_scalar(
                    apow[:, n:2 * n], apow[:, 0:n], acols[:, j - 1:j], None, ALU.mult))
            em(lambda e: e.memset(apneg[:, 0:1], 1.0))
            em(lambda e: e.tensor_copy(apneg[:, 1:2], acols[:, 5:6]))
            for j in range(1, 6):
                n = 2 ** j
                em(lambda e, n=n, j=j: e.tensor_scalar(
                    apneg[:, n:2 * n], apneg[:, 0:n], acols[:, 5 + j:6 + j], None,
                    ALU.mult))
            em(lambda e, c=c: e.tensor_scalar(
                aq[:, c, :], apow[:, :], acols[:, 10:11], None, ALU.mult))
            em(lambda e, c=c: e.tensor_scalar(
                ak[:, c, :], apneg[:, :], acols[:, 4:5], None, ALU.mult))
        tok_atiles = p.ck("V")

        bvrow_sb = sbp("bvrow_sb", [1, DI])
        t_bv = dma("sync", bvrow_sb[:, :], bv_row[:, :])
        t1 = p.emit("tensor", lambda e: e.matmul(
            pb[0][:, 0:512], ones_r[:, :], bvrow_sb[:, 0:512],
            start=True, stop=True), waits=[t_bv, t_consts], inc="PE")
        t2 = p.emit("tensor", lambda e: e.matmul(
            pb[1][:, 0:512], ones_r[:, :], bvrow_sb[:, 512:1024],
            start=True, stop=True), waits=[t_bv], inc="PE")
        p.emit("vector", lambda e: e.tensor_copy(bvbc[:, 0:512], pb[0][:, 0:512]),
               waits=[t1], inc="V")
        t_bvbc = p.emit("vector", lambda e: e.tensor_copy(
            bvbc[:, 512:1024], pb[1][:, 0:512]), waits=[t2], inc="V")

        wtmp = sbp("wtmp", [128, 2, DI])
        whtmp = sbp("whtmp", [128, 2, DI], F16)
        wltmp = sbp("wltmp", [128, 2, DI], BF16)
        sfree = [None, None]
        for wi, w in enumerate(("wq", "wk", "wv", "wg", "wo")):
            for k in range(KC):
                par = (wi * KC + k) % 2
                tin = dma("sync", wtmp[:, par, :], w_in[w][k * 128:(k + 1) * 128, :],
                          waits=[sfree[par]])
                th = p.emit("vector", lambda e, par=par: e.tensor_copy(
                    whtmp[:, par, :], wtmp[:, par, :]), waits=[tin], inc="V")
                tl = p.emit("vector", lambda e, par=par: e.tensor_tensor(
                    wltmp[:, par, :], wtmp[:, par, :], whtmp[:, par, :],
                    ALU.subtract), waits=[], inc="V")
                dma("sync", whi_s[wi, :, k, :], whtmp[:, par, :], waits=[th])
                sfree[par] = dma("sync", wlo_s[wi, :, k, :], wltmp[:, par, :],
                                 waits=[tl])

        ftmp = sbp("ftmp", [128, 2, FF])
        fb1 = sbp("fb1", [128, FF])
        ftern = sbp("ftern", [128, FF], FP8)
        redc = sbp("redc", [128, 4])
        foldt = sbp("foldt", [64, NTOK])
        for wi, (src, dst, kn, cols) in enumerate(
                [(wg1_t, tg_s, KC, FF), (wu1_t, tu_s, KC, FF), (wd1_t, td_s, OC, D)]):
            tz = p.emit("vector", lambda e: e.memset(redc[:, 0:1], 0.0),
                        waits=[], inc="V")
            lin = [None, None]
            for k in range(kn):
                par = k % 2
                tin = dma("sync", ftmp[:, par, 0:cols],
                          src[k * 128:(k + 1) * 128, :], waits=[lin[par]])
                p.emit("vector", lambda e, par=par, cols=cols: e.tensor_reduce(
                    redc[:, 1:2], ftmp[:, par, 0:cols], AX.X, ALU.add,
                    apply_absolute_value=True), waits=[tin, tz], inc="V")
                vdrain()
                lin[par] = p.emit("vector", lambda e: e.tensor_tensor(
                    redc[:, 0:1], redc[:, 0:1], redc[:, 1:2], ALU.add),
                    waits=[], inc="V")
                vdrain()
            for half in (64, 32):
                p.emit("vector", lambda e, h=half: e.tensor_copy(
                    foldt[0:h, 0:1], redc[h:2 * h, 0:1]), waits=[], inc="V")
                vdrain()
                p.emit("vector", lambda e, h=half: e.tensor_tensor(
                    redc[0:h, 0:1], redc[0:h, 0:1], foldt[0:h, 0:1], ALU.add),
                    waits=[], inc="V")
                vdrain()
            for half in (16, 8, 4, 2, 1):
                msk = [(i + half) % 32 for i in range(32)]
                p.emit("vector", lambda e, m=msk: e.stream_shuffle(
                    foldt[0:32, 0:1], redc[0:32, 0:1], m), waits=[], inc="V")
                vdrain()
                p.emit("vector", lambda e, h=half: e.tensor_tensor(
                    redc[0:h, 0:1], redc[0:h, 0:1], foldt[0:h, 0:1], ALU.add),
                    waits=[], inc="V")
                vdrain()
            p.emit("vector", lambda e, n=float(kn * 128 * cols): e.tensor_scalar(
                rows[0:1, 0:1], redc[0:1, 0:1], 1.0 / n, 1e-5, ALU.mult, ALU.max),
                waits=[], inc="V")
            vdrain()
            tsw = p.emit("vector", lambda e: e.reciprocal(
                rows[0:1, 1:2], rows[0:1, 0:1]), waits=[], inc="V")
            vdrain()
            tm1 = p.emit("tensor", lambda e: e.matmul(
                pb[0][:, 0:1], ones_r[:, :], rows[0:1, 0:1], start=True, stop=True),
                waits=[tsw, t_bvbc], inc="PE")
            tm2 = p.emit("tensor", lambda e: e.matmul(
                pb[0][:, 1:2], ones_r[:, :], rows[0:1, 1:2], start=True, stop=True),
                waits=[], inc="PE")
            p.emit("vector", lambda e, wi=wi: e.tensor_copy(
                mghat[:, wi:wi + 1], pb[0][:, 0:1]), waits=[tm1, tm2], inc="V")
            tswc = p.emit("vector", lambda e: e.tensor_copy(
                redc[:, 2:3], pb[0][:, 1:2]), waits=[], inc="V")
            lin = [None, None]
            for k in range(kn):
                par = k % 2
                tin = dma("sync", ftmp[:, par, 0:cols],
                          src[k * 128:(k + 1) * 128, :], waits=[lin[par], tswc])
                p.emit("vector", lambda e, par=par, cols=cols: e.tensor_scalar(
                    fb1[:, 0:cols], ftmp[:, par, 0:cols], redc[:, 2:3], 0.5,
                    ALU.mult, ALU.is_gt), waits=[tin], inc="V")
                p.emit("vector", lambda e, par=par, cols=cols: e.tensor_scalar(
                    ftmp[:, par, 0:cols], ftmp[:, par, 0:cols], redc[:, 2:3], -0.5,
                    ALU.mult, ALU.is_lt), waits=[], inc="V")
                t3 = p.emit("vector", lambda e, par=par, cols=cols: e.tensor_tensor(
                    ftern[:, 0:cols], fb1[:, 0:cols], ftmp[:, par, 0:cols],
                    ALU.subtract), waits=[], inc="V")
                lin[par] = dma("sync", dst[:, k, :], ftern[:, 0:cols], waits=[t3])
        t_prelude_dS = p.ck("dS")
        esp.close()

        barrier = {s: p.ck(s) for s in ("PE", "V", "S", "dG")}

        with contextlib.ExitStack() as esa:
            def sba(name, shape, dt=F32):
                return esa.enter_context(nc.sbuf_tensor(name, shape, dt))

            xsb = sba("xsb", [128, 1, KC, NTOK])
            sq = sba("sq", [128, 2, NTOK])
            xnhi = sba("xnhi", [128, KC, NTOK], F16)
            xnlo = sba("xnlo", [128, KC, NTOK], BF16)
            qf = sba("qf", [128, KC, NTOK])
            kf = sba("kf", [128, KC, NTOK])
            gs = sba("gs", [128, KC, NTOK])
            vtm = sba("vtm", [128, 2, DI])
            ktm = sba("ktm", [128, 2, DI])
            pmk = sba("pmk", [128, NTOK])
            o1sb = sba("o1sb", [128, NTOK])
            srm = sba("srm", [128, NTOK])
            ohi = sba("ohi", [128, KC, NTOK], F16)
            olo = sba("olo", [128, KC, NTOK], BF16)
            ssb = sba("ssb", [128, KC, NTOK])
            x2sb = sba("x2sb", [128, 2, KC, NTOK])
            i1sb = sba("i1sb", [128, KC, NTOK], BF16)
            accp = sba("accp", [128, NTOK])
            accn = sba("accn", [128, NTOK])
            am = sba("am", [128, NTOK])
            amf = sba("amf", [64, NTOK])
            rbc_sb = sba("rbc_sb", [128, NTOK])
            bsb = sba("bsb", [128, 512])
            ufin = sba("ufin", [128, 512])
            hp = [sba("hp0", [128, KC, NTOK]), sba("hp1", [128, KC, NTOK])]
            wbh = sba("wbh", [128, 2, KC, DI], F16)
            wbl = sba("wbl", [128, 2, KC, DI], BF16)

            th0 = p.emit("vector", lambda e: e.memset(hp[0][:, :, :], 0.0),
                         waits=[t_prelude_dS], inc="V")
            p.emit("vector", lambda e: e.memset(hp[1][:, :, :], 0.0),
                   waits=[], inc="V")

            WROW = (2, 0, 1, 3, 4)
            w_free = {}
            w_ready = {}

            def prefetch_w(m):
                if m >= nta * 5 or m in w_ready:
                    return
                par = m % 2
                wi = WROW[m % 5]
                p.emit("sync", lambda e, wi=wi, par=par: e.dma_start(
                    out=wbh[:, par, :, :], in_=whi_s[wi, :, :, :]),
                    waits=[w_free.get(m - 2), t_prelude_dS], inc="dS", inc_n=16)
                w_ready[m] = p.emit("sync", lambda e, wi=wi, par=par: e.dma_start(
                    out=wbl[:, par, :, :], in_=wlo_s[wi, :, :, :]),
                    waits=[], inc="dS", inc_n=16)

            x_free = {}
            x_ready = {}

            def prefetch_x(t):
                if t >= nta or t in x_ready:
                    return
                x_ready[t] = p.emit("sync", lambda e, t=t: e.dma_start(
                    out=xsb[:, 0, :, :],
                    in_=x_t[:, t, :].rearrange("(k p) n -> p k n", p=128),
                    ), waits=[x_free.get(t - 1), t_prelude_dS],
                    inc="dS", inc_n=16)

            prefetch_w(0); prefetch_w(1)
            prefetch_x(0)

            prev_store = None
            defer_ctx = {}

            def emit_deferred(dt):
                nonlocal prev_store, ti1_last, prev_iv, prev_st1
                ctx = defer_ctx.pop(dt)
                dpar = ctx["par"]
                x2c = lambda c: x2sb[:, dpar, c, :]
                fold_last = None
                for c in range(KC):
                    if c == 0:
                        p.emit("vector", lambda e, c=c: e.tensor_copy(
                            accp[:, :], x2c(c)), waits=[ctx["x2_toks"][0]],
                            inc="V")
                        fold_last = p.emit("vector", lambda e, c=c: e.tensor_copy(
                            accn[:, :], x2c(c)), waits=[], inc="V")
                    else:
                        p.emit("vector", lambda e, c=c: e.tensor_tensor(
                            accp[:, :], accp[:, :], x2c(c), ALU.max),
                            waits=[ctx["x2_toks"][c]], inc="V")
                        fold_last = p.emit("vector", lambda e, c=c: e.tensor_tensor(
                            accn[:, :], accn[:, :], x2c(c), ALU.min),
                            waits=[], inc="V")
                p.emit("vector", lambda e: e.scalar_tensor_tensor(
                    am[:, :], accn[:, :], -1.0, accp[:, :], ALU.mult, ALU.max),
                    waits=[fold_last], inc="V")
                for half in (64, 32):
                    p.emit("vector", lambda e, h=half: e.tensor_copy(
                        amf[0:h, :], am[h:2 * h, :]), waits=[], inc="V")
                    p.emit("vector", lambda e, h=half: e.tensor_tensor(
                        am[0:h, :], am[0:h, :], amf[0:h, :], ALU.max),
                        waits=[], inc="V")
                for half in (16, 8, 4, 2, 1):
                    msk = [(i + half) % 32 for i in range(32)]
                    p.emit("vector", lambda e, m=msk: e.stream_shuffle(
                        amf[0:32, :], am[0:32, :], m), waits=[], inc="V")
                    p.emit("vector", lambda e, h=half: e.tensor_tensor(
                        am[0:h, :], am[0:h, :], amf[0:h, :], ALU.max),
                        waits=[], inc="V")
                p.emit("vector", lambda e: e.tensor_scalar(
                    rows[0:1, 0:NTOK], am[0:1, :], 1e-5, None, ALU.max),
                    waits=[], inc="V")
                p.emit("vector", lambda e: e.reciprocal(
                    rows[0:1, NTOK:2 * NTOK], rows[0:1, 0:NTOK]),
                    waits=[], inc="V")
                ts1 = p.emit("vector", lambda e: e.tensor_scalar(
                    rows2[0:1, 0:NTOK], rows[0:1, NTOK:2 * NTOK], 127.0, None,
                    ALU.mult), waits=[], inc="V")
                tiv = p.emit("vector", lambda e: e.tensor_scalar(
                    rows2[0:1, NTOK:2 * NTOK], rows[0:1, 0:NTOK], 1.0 / 127.0,
                    None, ALU.mult), waits=[], inc="V")
                tbs1 = p.emit("tensor", lambda e: e.matmul(
                    pb[1][:, 0:NTOK], ones_r[:, :], rows2[0:1, 0:NTOK],
                    start=True, stop=True), waits=[ts1], inc="PE")
                tivd = p.emit("gpsimd", lambda e, dt=dt: e.dma_start(
                    out=iv1_s[dt, :, :], in_=rows2[0:1, NTOK:2 * NTOK]),
                    waits=[tiv], inc="dG", inc_n=16)
                prev_iv = tivd
                ti1 = None
                for cc in range(0, KC, 2):
                    u0v = pb[5][:, 0:512]
                    pgA = pb[6][:, 0:512]
                    pgB = pb[7][:, 0:512]
                    p.emit("vector", lambda e, cc=cc, u0v=u0v: e.tensor_tensor(
                        u0v[:, 0:NTOK], x2c(cc), pb[1][:, 0:NTOK], ALU.mult),
                        waits=[tbs1, prev_st1 if cc == 0 else None], inc="V")
                    p.emit("vector", lambda e, cc=cc, u0v=u0v: e.tensor_tensor(
                        u0v[:, NTOK:2 * NTOK], x2c(cc + 1), pb[1][:, 0:NTOK],
                        ALU.mult), waits=[], inc="V")
                    p.emit("vector", lambda e, u0v=u0v: e.tensor_scalar(
                        u0v, u0v, 128.0, None, ALU.add), waits=[], inc="V")
                    srcv = u0v
                    for ki, k in enumerate(range(7, -1, -1)):
                        cth = float(2 ** k) - 0.5
                        p.emit("vector", lambda e, srcv=srcv, cth=cth, k=k:
                               e.tensor_scalar(
                            bsb[:, :], srcv, cth, -float(2 ** k),
                            ALU.is_ge, ALU.mult), waits=[], inc="V")
                        dstv = ufin[:, :] if k == 0 else (pgA if ki % 2 == 0 else pgB)
                        p.emit("vector", lambda e, srcv=srcv, dstv=dstv:
                               e.tensor_tensor(
                            dstv, bsb[:, :], srcv, ALU.add), waits=[], inc="V")
                        srcv = dstv
                    p.emit("vector", lambda e, u0v=u0v: e.scalar_tensor_tensor(
                        bsb[:, :], ufin[:, :], -1.0, u0v, ALU.mult, ALU.add),
                        waits=[], inc="V")
                    p.emit("vector", lambda e, cc=cc: e.tensor_scalar(
                        i1sb[:, cc, :], bsb[:, 0:NTOK], -128.0, None, ALU.add),
                        waits=[], inc="V")
                    ti1 = p.emit("vector", lambda e, cc=cc: e.tensor_scalar(
                        i1sb[:, cc + 1, :], bsb[:, NTOK:2 * NTOK], -128.0, None,
                        ALU.add), waits=[], inc="V")
                ti1_last = ti1
                prev_st1 = p.emit("gpsimd", lambda e, dt=dt: e.dma_start(
                    out=i1_s[dt, :, :, :], in_=i1sb[:, :, :]), waits=[ti1],
                    inc="dG", inc_n=16)
                prev_store = p.emit("gpsimd", lambda e, dt=dt: e.dma_start(
                    out=x2_s[dt, :, :, :], in_=x2sb[:, dpar, :, :]),
                    waits=[fold_last], inc="dG", inc_n=16)

            ti1_last = None
            prev_iv = None
            prev_st1 = None
            prev_sqpe = None
            pb01_free = None
            prev_S = None
            prev_tlo = None
            wo_v_ck = None
            for t in range(nta):
                par = t % 2
                bar = []
                xc = lambda c: xsb[:, 0, c, :]
                tx = x_ready[t]

                if pha < 1:
                    barrier = {s: p.ck(s) for s in ("PE", "V", "S", "dG")}
                    continue
                sq_pe = []
                mm1_last = None
                for c in range(KC):
                    tsqr = p.emit("scalar", lambda e, c=c: e.activation(
                        sq[:, c % 2, :], xc(c), ACT.Square),
                        waits=[tx, (sq_pe[c - 2] if c >= 2 else prev_sqpe)],
                        inc="S")
                    mm1_last = p.emit("tensor", lambda e, c=c: e.matmul(
                        pb[0][0:1, 0:NTOK], ones_c[:, :], xc(c),
                        start=(c == 0), stop=(c == KC - 1), skip_group_check=True),
                        waits=[tx, t_consts, pb01_free],
                        inc="PE" if c == KC - 1 else None)
                    sq_pe.append(p.emit("tensor", lambda e, c=c: e.matmul(
                        pb[7][0:1, 0:NTOK], ones_c[:, :], sq[:, c % 2, :],
                        start=(c == 0), stop=(c == KC - 1), skip_group_check=True), waits=[tsqr], inc="PE"))

                def ln_rows(tok_mm, prev_dma=None):
                    tmu = p.emit("vector", lambda e: e.tensor_scalar(
                        rows[0:1, 0:NTOK], pb[0][0:1, 0:NTOK], 1.0 / D, None,
                        ALU.mult), waits=[tok_mm, prev_dma], inc="V")
                    p.emit("vector", lambda e: e.tensor_scalar(
                        rows[0:1, NTOK:2 * NTOK], pb[7][0:1, 0:NTOK],
                        1.0 / D, None, ALU.mult), waits=[], inc="V")
                    p.emit("vector", lambda e: e.tensor_tensor(
                        rows2[0:1, 0:NTOK], rows[0:1, 0:NTOK], rows[0:1, 0:NTOK],
                        ALU.mult), waits=[], inc="V")
                    tvar = p.emit("vector", lambda e: e.tensor_tensor(
                        rows2[0:1, NTOK:2 * NTOK], rows[0:1, NTOK:2 * NTOK],
                        rows2[0:1, 0:NTOK], ALU.subtract), waits=[], inc="V")
                    tsq = p.emit("scalar", lambda e: e.activation(
                        rows2[0:1, 0:NTOK], rows2[0:1, NTOK:2 * NTOK], ACT.Sqrt,
                        bias=eps1[:, :]), waits=[tvar], inc="S")
                    trr = p.emit("vector", lambda e: e.reciprocal(
                        rows[0:1, NTOK:2 * NTOK], rows2[0:1, 0:NTOK]),
                        waits=[tsq], inc="V")
                    tbc1 = p.emit("tensor", lambda e: e.matmul(
                        pb[0][:, 0:NTOK], ones_r[:, :], rows[0:1, 0:NTOK],
                        start=True, stop=True), waits=[tmu, trr], inc="PE")
                    tbc2 = p.emit("tensor", lambda e: e.matmul(
                        pb[0][:, NTOK:2 * NTOK], ones_r[:, :],
                        rows[0:1, NTOK:2 * NTOK], start=True, stop=True),
                        waits=[trr], inc="PE")
                    trbc = p.emit("vector", lambda e: e.tensor_copy(
                        rbc_sb[:, :], pb[0][:, NTOK:2 * NTOK]), waits=[tbc2],
                        inc="V")
                    return tbc1, tbc2, trbc

                tbc1, tbc2, trbc = ln_rows(sq_pe[-1], prev_iv)
                if pha < 2:
                    barrier = {s: p.ck(s) for s in ("PE", "V", "S", "dG")}
                    continue

                for c in range(KC):
                    reg = pb[2] if c % 2 == 0 else pb[3]
                    p.emit("vector", lambda e, c=c, reg=reg: e.tensor_tensor(
                        reg[:, 0:NTOK], xc(c), pb[0][:, 0:NTOK], ALU.subtract),
                        waits=[tbc2], inc="V")
                    p.emit("vector", lambda e, reg=reg: e.tensor_tensor(
                        reg[:, NTOK:2 * NTOK], reg[:, 0:NTOK], rbc_sb[:, :],
                        ALU.mult), waits=[trbc], inc="V")
                    p.emit("vector", lambda e, c=c, reg=reg: e.tensor_scalar(
                        xnhi[:, c, :], reg[:, NTOK:2 * NTOK], gam[:, c:c + 1],
                        bet[:, c:c + 1], ALU.mult, ALU.add), waits=[], inc="V")
                    p.emit("vector", lambda e, c=c, reg=reg: e.tensor_scalar(
                        reg[:, 0:NTOK], reg[:, NTOK:2 * NTOK], gam[:, c:c + 1],
                        bet[:, c:c + 1], ALU.mult, ALU.add), waits=[], inc="V")
                    p.emit("vector", lambda e, c=c, reg=reg: e.tensor_tensor(
                        xnlo[:, c, :], reg[:, 0:NTOK], xnhi[:, c, :], ALU.subtract),
                        waits=[], inc="V")
                xn_done = p.ck("V")

                if pha < 3:
                    barrier = {s: p.ck(s) for s in ("PE", "V", "S", "dG")}
                    continue
                regions = [pb[2][:, 0:NTOK], pb[3][:, 0:NTOK],
                           pb[4][:, 0:NTOK]]
                regions2 = [pb[2][:, NTOK:2 * NTOK], pb[3][:, NTOK:2 * NTOK],
                            pb[4][:, NTOK:2 * NTOK]]
                reg_free = [None, None, None]
                reg_i = [0]

                def next_region():
                    i = reg_i[0] % 3
                    reg_i[0] += 1
                    return i, regions[i]

                def proj3(reg, stat_hi, stat_lo, mov_hi, mov_lo, first_waits):
                    last = None
                    for k in range(KC):
                        fst = (k == 0)
                        lst = (k == KC - 1)
                        p.emit("tensor", lambda e, k=k, reg=reg, sh=stat_hi,
                               mh=mov_hi, fst=fst: e.matmul(
                            reg, sh(k), mh(k), start=fst, stop=False),
                            waits=first_waits if fst else [], inc=None)
                        p.emit("tensor", lambda e, k=k, reg=reg, sh=stat_hi,
                               ml=mov_lo: e.matmul(
                            reg, sh(k), ml(k), start=False, stop=False),
                            waits=[], inc=None)
                        last = p.emit("tensor", lambda e, k=k, reg=reg, sl=stat_lo,
                                      mh=mov_hi, lst=lst: e.matmul(
                            reg, sl(k), mh(k), start=False, stop=lst),
                            waits=[], inc="PE" if lst else None)
                    return last

                m_v = t * 5 + 0
                wv_par, wv_tok = m_v % 2, w_ready[m_v]
                vt_ev = []
                for e2 in range(2):
                    for hf in range(2):
                        i = (e2 * 2 + hf) % 2
                        regfull = pb[4] if i == 0 else pb[5]
                        last = proj3(
                            regfull[:, 0:512],
                            lambda k, e2=e2: xnhi[:, k, e2 * 128:(e2 + 1) * 128],
                            lambda k, e2=e2: xnlo[:, k, e2 * 128:(e2 + 1) * 128],
                            lambda k, wv_par=wv_par, hf=hf:
                                wbh[:, wv_par, k, hf * 512:(hf + 1) * 512],
                            lambda k, wv_par=wv_par, hf=hf:
                                wbl[:, wv_par, k, hf * 512:(hf + 1) * 512],
                            [wv_tok, xn_done, wo_v_ck,
                             (vt_ev[-2] if len(vt_ev) >= 2 else None)])
                        vt_ev.append(p.emit("vector",
                            lambda e, e2=e2, hf=hf, regfull=regfull:
                            e.tensor_tensor(
                                vtm[:, e2, hf * 512:(hf + 1) * 512],
                                regfull[:, 0:512], bvbc[:, hf * 512:(hf + 1) * 512],
                                ALU.add), waits=[last, t_bvbc], inc="V"))
                vtm_done = p.ck("V")
                w_free[m_v] = p.ck("PE")
                prefetch_w(m_v + 2)

                if pha < 4:
                    barrier = {s: p.ck(s) for s in ("PE", "V", "S", "dG")}
                    continue
                reg_free[0] = reg_free[1] = xn_done
                reg_free[2] = vt_ev[3]
                reg_i[0] = 0
                ev_toks = {"q": [], "k": [], "g": []}
                for j, (nm, wn) in enumerate((("q", 1), ("k", 2), ("g", 3))):
                    m = t * 5 + wn
                    wpar, wtok = m % 2, w_ready[m]
                    for c in range(KC):
                        ri, reg = next_region()
                        last = proj3(
                            reg,
                            lambda k, wpar=wpar, c=c:
                                wbh[:, wpar, k, c * 128:(c + 1) * 128],
                            lambda k, wpar=wpar, c=c:
                                wbl[:, wpar, k, c * 128:(c + 1) * 128],
                            lambda k: xnhi[:, k, :],
                            lambda k: xnlo[:, k, :],
                            [wtok, xn_done, reg_free[ri]])
                        if nm == "q":
                            for bb in range(BC):
                                tok = p.emit("vector", lambda e, c=c, reg=reg,
                                             bb=bb: e.scalar_tensor_tensor(
                                    qf[:, c, bb * 64:(bb + 1) * 64],
                                    reg[:, bb * 64:(bb + 1) * 64],
                                    bias["bq"][:, c:c + 1], aq[:, c, :],
                                    ALU.add, ALU.mult),
                                    waits=[last, tok_atiles], inc="V")
                        elif nm == "k":
                            for bb in range(BC):
                                tok = p.emit("vector", lambda e, c=c, reg=reg,
                                             bb=bb: e.scalar_tensor_tensor(
                                    kf[:, c, bb * 64:(bb + 1) * 64],
                                    reg[:, bb * 64:(bb + 1) * 64],
                                    bias["bk"][:, c:c + 1], ak[:, c, :],
                                    ALU.add, ALU.mult),
                                    waits=[last], inc="V")
                        else:
                            tok = p.emit("scalar", lambda e, c=c, reg=reg:
                                         e.activation(
                                gs[:, c, :], reg, ACT.Silu,
                                bias=bias["bg"][:, c:c + 1]),
                                waits=[last, prev_tlo], inc="S")
                        ev_toks[nm].append(tok)
                        reg_free[ri] = tok
                    w_free[m] = p.ck("PE")
                    prefetch_w(m + 2)
                    if nm == "q" and (t - 1) in defer_ctx:
                        emit_deferred(t - 1)

                if pha < 5:
                    barrier = {s: p.ck(s) for s in ("PE", "V", "S", "dG")}
                    continue
                tr_banks = [pb[0], pb[6], pb[1], pb[7]]
                ktr_ev = []
                nidx = 0
                for c in range(KC):
                    for b4 in range(BC):
                        for hq in range(2):
                            bank = tr_banks[nidx % 4]
                            ttr = p.emit("tensor",
                                lambda e, c=c, b4=b4, hq=hq, bank=bank:
                                e.transpose(
                                    bank[0:64, NTOK:NTOK + 64],
                                    kf[hq * 64:hq * 64 + 64, c, b4 * 64:(b4 + 1) * 64],
                                    ident[hq * 64:hq * 64 + 64, :],
                                    tile_position=(hq * 64, 0)),
                                waits=[ev_toks["k"][c],
                                       (ktr_ev[-4] if len(ktr_ev) >= 4 else None)],
                                inc="PE")
                            ktr_ev.append(p.emit("vector",
                                lambda e, c=c, b4=b4, hq=hq, bank=bank:
                                e.tensor_copy(
                                    ktm[(b4 % 2) * 64:(b4 % 2) * 64 + 64, b4 // 2,
                                        c * 128 + hq * 64:c * 128 + hq * 64 + 64],
                                    bank[0:64, NTOK:NTOK + 64]),
                                waits=[ttr], inc="V"))
                            nidx += 1
                ktm_done = p.ck("V")

                if pha < 6:
                    barrier = {s: p.ck(s) for s in ("PE", "V", "S", "dG")}
                    continue
                prev_scan_v = None
                og_done = None
                tlo_last = None
                for c in range(KC):
                    pmm = None
                    for b4 in range(BC):
                        for hq in range(2):
                            bank = pb[6] if hq == 0 else pb[7]
                            pmm = p.emit("tensor", lambda e, c=c, b4=b4, hq=hq,
                                         bank=bank: e.matmul(
                                bank[(b4 % 2) * 64:(b4 % 2) * 64 + 64,
                                     (b4 // 2) * 64:(b4 // 2) * 64 + 64],
                                kf[hq * 64:hq * 64 + 64, c, b4 * 64:(b4 + 1) * 64],
                                qf[hq * 64:hq * 64 + 64, c, b4 * 64:(b4 + 1) * 64],
                                start=True, stop=True,
                                tile_position=(hq * 64, (b4 % 2) * 64)),
                                waits=[ev_toks["q"][c], ev_toks["k"][c],
                                       prev_scan_v, ktm_done], inc="PE")
                    pmkv = pmk[:, :].rearrange("p (a h t) -> p a h t", a=2, h=2)
                    maskv = mask[:, 0:128].rearrange("p (a t) -> p a t", a=2)
                    tpm0 = p.emit("vector", lambda e, pmkv=pmkv, maskv=maskv:
                                  e.tensor_tensor(
                        pmkv[:, :, 0, :],
                        pb[6][:, 0:128].rearrange("p (a t) -> p a t", a=2),
                        maskv, ALU.mult), waits=[pmm], inc="V")
                    tpm = p.emit("vector", lambda e, pmkv=pmkv, maskv=maskv:
                                 e.tensor_tensor(
                        pmkv[:, :, 1, :],
                        pb[7][:, 0:128].rearrange("p (a t) -> p a t", a=2),
                        maskv, ALU.mult), waits=[], inc="V")
                    omm = None
                    for b4 in range(BC):
                        for hq in range(2):
                            p.emit("tensor", lambda e, c=c, b4=b4, hq=hq: e.matmul(
                                pb[2][(b4 % 2) * 64:(b4 % 2) * 64 + 64,
                                      (b4 // 2) * 128 + hq * 64:
                                      (b4 // 2) * 128 + hq * 64 + 64],
                                vtm[(b4 % 2) * 64:(b4 % 2) * 64 + 64, b4 // 2,
                                    c * 128 + hq * 64:c * 128 + hq * 64 + 64],
                                pmk[(b4 % 2) * 64:(b4 % 2) * 64 + 64,
                                    (b4 // 2) * 128 + hq * 64:
                                    (b4 // 2) * 128 + hq * 64 + 64],
                                start=True, stop=True,
                                tile_position=((b4 % 2) * 64, (b4 % 2) * 64)),
                                waits=[tpm, vtm_done, og_done], inc=None)
                            omm = p.emit("tensor", lambda e, c=c, b4=b4, hq=hq,
                                         par=par: e.matmul(
                                pb[3][hq * 64:hq * 64 + 64, b4 * 64:(b4 + 1) * 64],
                                hp[par][hq * 64:hq * 64 + 64, c,
                                        b4 * 64:(b4 + 1) * 64],
                                qf[hq * 64:hq * 64 + 64, c, b4 * 64:(b4 + 1) * 64],
                                start=True, stop=True,
                                tile_position=(hq * 64, hq * 64)),
                                waits=[th0 if t == 0 else None], inc="PE")
                    smm = None
                    for b4 in range(BC):
                        for hq in range(2):
                            smm = p.emit("tensor", lambda e, c=c, b4=b4, hq=hq:
                                         e.matmul(
                                pb[4][(b4 % 2) * 64:(b4 % 2) * 64 + 64,
                                      (b4 // 2) * 128 + hq * 64:
                                      (b4 // 2) * 128 + hq * 64 + 64],
                                ktm[(b4 % 2) * 64:(b4 % 2) * 64 + 64, b4 // 2,
                                    c * 128 + hq * 64:c * 128 + hq * 64 + 64],
                                vtm[(b4 % 2) * 64:(b4 % 2) * 64 + 64, b4 // 2,
                                    c * 128 + hq * 64:c * 128 + hq * 64 + 64],
                                start=True, stop=True,
                                tile_position=((b4 % 2) * 64, (b4 % 2) * 64)),
                                waits=[], inc="PE")
                    cp = None
                    for b4 in range(BC):
                        for hq in range(2):
                            cp = p.emit("vector", lambda e, b4=b4, hq=hq:
                                        e.tensor_copy(
                                o1sb[hq * 64:hq * 64 + 64, b4 * 64:(b4 + 1) * 64],
                                pb[2][(b4 % 2) * 64:(b4 % 2) * 64 + 64,
                                      (b4 // 2) * 128 + hq * 64:
                                      (b4 // 2) * 128 + hq * 64 + 64]),
                                waits=[omm], inc="V")
                    tadd = p.emit("vector", lambda e: e.tensor_tensor(
                        pb[3][:, NTOK:2 * NTOK], o1sb[:, :], pb[3][:, 0:NTOK],
                        ALU.add), waits=[], inc="V")
                    tog = p.emit("vector", lambda e, c=c: e.tensor_tensor(
                        pb[5][:, 0:NTOK], pb[3][:, NTOK:2 * NTOK], gs[:, c, :],
                        ALU.mult), waits=[ev_toks["g"][c]], inc="V")
                    p.emit("vector", lambda e, c=c: e.tensor_copy(
                        ohi[:, c, :], pb[5][:, 0:NTOK]), waits=[], inc="V")
                    tlo_last = p.emit("vector", lambda e, c=c: e.tensor_tensor(
                        olo[:, c, :], pb[5][:, 0:NTOK], ohi[:, c, :],
                        ALU.subtract), waits=[], inc="V")
                    og_done = tlo_last
                    for b4 in range(BC):
                        for hq in range(2):
                            p.emit("vector", lambda e, b4=b4, hq=hq: e.tensor_copy(
                                srm[hq * 64:hq * 64 + 64, b4 * 64:(b4 + 1) * 64],
                                pb[4][(b4 % 2) * 64:(b4 % 2) * 64 + 64,
                                      (b4 // 2) * 128 + hq * 64:
                                      (b4 // 2) * 128 + hq * 64 + 64]),
                                waits=[smm], inc="V")
                    p.emit("vector", lambda e, c=c, par=par: e.tensor_tensor(
                        pb[4][:, NTOK:2 * NTOK], hp[par][:, c, :], srm[:, :],
                        ALU.add), waits=[], inc="V")
                    prev_scan_v = p.emit("vector", lambda e, c=c, par=par:
                                         e.tensor_scalar(
                        hp[1 - par][:, c, :], pb[4][:, NTOK:2 * NTOK],
                        a64[:, c:c + 1], None, ALU.mult), waits=[], inc="V")
                scan_v_done = p.ck("V")

                m_o = t * 5 + 4
                wo_par, wo_tok = m_o % 2, w_ready[m_o]
                s_toks = []
                for mc in range(KC):
                    ri, reg = next_region()
                    reg2 = regions2[ri]
                    last = proj3(
                        reg,
                        lambda k, wo_par=wo_par, mc=mc:
                            wbh[:, wo_par, k, mc * 128:(mc + 1) * 128],
                        lambda k, wo_par=wo_par, mc=mc:
                            wbl[:, wo_par, k, mc * 128:(mc + 1) * 128],
                        lambda k: ohi[:, k, :],
                        lambda k: olo[:, k, :],
                        [wo_tok, tlo_last, reg_free[ri], scan_v_done])
                    tb = p.emit("vector", lambda e, mc=mc, reg=reg, reg2=reg2:
                                e.tensor_scalar(
                        reg2, reg, bias["bo"][:, mc:mc + 1], None, ALU.add),
                        waits=[last], inc="V")
                    stok = p.emit("vector", lambda e, mc=mc, reg2=reg2:
                                  e.scalar_tensor_tensor(
                        ssb[:, mc, :], xc(mc), 2.0, reg2, ALU.mult, ALU.add),
                        waits=[prev_S if mc == 0 else None], inc="V")
                    s_toks.append(stok)
                    reg_free[ri] = stok
                x_free[t] = s_toks[-1]
                wo_v_ck = p.ck("V")
                w_free[m_o] = p.ck("PE")
                prefetch_w(m_o + 2)
                prefetch_x(t + 1)

                if pha < 8:
                    barrier = {s: p.ck(s) for s in ("PE", "V", "S", "dG")}
                    continue
                sq_pe2 = []
                mm2_last = None
                for c in range(KC):
                    tsqr = p.emit("scalar", lambda e, c=c: e.activation(
                        sq[:, c % 2, :], ssb[:, c, :], ACT.Square),
                        waits=[s_toks[c], (sq_pe2[c - 2] if c >= 2 else None)],
                        inc="S")
                    mm2_last = p.emit("tensor", lambda e, c=c: e.matmul(
                        pb[0][0:1, 0:NTOK], ones_c[:, :], ssb[:, c, :],
                        start=(c == 0), stop=(c == KC - 1), skip_group_check=True),
                        waits=[s_toks[c]], inc="PE" if c == KC - 1 else None)
                    sq_pe2.append(p.emit("tensor", lambda e, c=c: e.matmul(
                        pb[7][0:1, 0:NTOK], ones_c[:, :], sq[:, c % 2, :],
                        start=(c == 0), stop=(c == KC - 1), skip_group_check=True), waits=[tsqr],
                        inc="PE"))
                tbc1, tbc2, trbc = ln_rows(sq_pe2[-1], prev_iv)

                x2_toks = []
                for c in range(KC):
                    reg = pb[2] if c % 2 == 0 else pb[3]
                    p.emit("vector", lambda e, c=c, reg=reg: e.tensor_tensor(
                        reg[:, 0:NTOK], ssb[:, c, :], pb[0][:, 0:NTOK],
                        ALU.subtract), waits=[tbc2], inc="V")
                    x2_toks.append(p.emit("vector", lambda e, c=c, reg=reg,
                                          par=par: e.tensor_tensor(
                        x2sb[:, par, c, :], reg[:, 0:NTOK], rbc_sb[:, :],
                        ALU.mult), waits=[trbc, prev_store], inc="V"))
                pb01_free = p.ck("V")
                defer_ctx[t] = dict(par=par, x2_toks=list(x2_toks))
                prev_sqpe = sq_pe2[-1]
                prev_S = p.ck("S")
                prev_tlo = tlo_last

                if t == 0 and dump:
                    taps = [("xnhi", xnhi, F16), ("xnlo", xnlo, BF16),
                            ("qf", qf, F32), ("kf", kf, F32), ("gs", gs, F32),
                            ("ohi", ohi, F16), ("olo", olo, BF16),
                            ("ssb", ssb, F32), ("x2sb", x2sb[:, 0], F32),
                            ("hp1", hp[1], F32)]
                    vck = p.ck("V")
                    for nm, tile, dt in taps:
                        dp = dump_out(nm, [128, KC, NTOK], dt)
                        if dp is not None:
                            p.emit("gpsimd", lambda e, dp=dp, tile=tile:
                                   e.dma_start(out=dp[:, :, :], in_=tile[:, :, :]),
                                   waits=[vck], inc="dG", inc_n=16)
                    for nm, tile in (("aq", aq), ("ak", ak)):
                        dp = dump_out(nm, [128, KC, 64])
                        if dp is not None:
                            p.emit("gpsimd", lambda e, dp=dp, tile=tile:
                                   e.dma_start(out=dp[:, :, :], in_=tile[:, :, :]),
                                   waits=[vck], inc="dG", inc_n=16)
                    for nm, tile in (("vtm", vtm), ("ktm", ktm)):
                        dp = dump_out(nm, [128, 2, DI])
                        if dp is not None:
                            p.emit("gpsimd", lambda e, dp=dp, tile=tile:
                                   e.dma_start(out=dp[:, :, :], in_=tile[:, :, :]),
                                   waits=[vck], inc="dG", inc_n=16)
                    p.emit("vector", lambda e: e.engine_nop(),
                           waits=[p.ck("dG")], inc="V")

                barrier = {s: p.ck(s) for s in ("PE", "V", "S", "dG")}

            if (nta - 1) in defer_ctx:
                emit_deferred(nta - 1)
            passA_dG = p.ck("dG")
            passA_V = p.ck("V")

        with contextlib.ExitStack() as esb:
            def sbb(name, shape, dt=F32):
                return esb.enter_context(nc.sbuf_tensor(name, shape, dt))

            tg_sb = sbb("tg_sb", [128, KC, FF], FP8)
            tu_sb = sbb("tu_sb", [128, KC, FF], FP8)
            td_sb = sbb("td_sb", [128, OC, D], FP8)
            i1b = sbb("i1b", [128, 1, KC, NTOK], BF16)
            x2b = sbb("x2b", [128, 1, KC, NTOK])
            iv1r = sbb("iv1r", [1, 1, NTOK])
            iv1bc = sbb("iv1bc", [128, NTOK])
            iv2bc = sbb("iv2bc", [128, NTOK])
            g1 = sbb("g1", [128, 2, NTOK])
            gate = sbb("gate", [128, 2, NTOK])
            hsb = sbb("hsb", [128, OC, NTOK])
            i2sb = sbb("i2sb", [128, OC, NTOK], BF16)
            s3sb = sbb("s3sb", [128, KC, NTOK])
            outsb = sbb("outsb", [128, KC, NTOK])
            accpB = sbb("accpB", [128, NTOK])
            accnB = sbb("accnB", [128, NTOK])
            amB = sbb("amB", [128, NTOK])
            amfB = sbb("amfB", [64, NTOK])
            rbcB = sbb("rbcB", [128, NTOK])
            bsbB = sbb("bsbB", [128, 512])
            ufinB = sbb("ufinB", [128, 512])
            sqB = sbb("sqB", [128, 2, NTOK])

            passA_PE = p.ck("PE")
            passA_S = p.ck("S")
            for dst2, src2 in [(tg_sb[:, :, :], tg_s[:, :, :]),
                               (tu_sb[:, :, :], tu_s[:, :, :]),
                               (td_sb[:, :, :], td_s[:, :, :])]:
                dma("sync", dst2, src2, waits=[passA_dG, passA_PE, passA_V,
                                               passA_S])
            tT = p.ck("dS")

            xb_free = {}
            xb_ready = {}

            def prefetch_b(t):
                if t >= ntb or t in xb_ready:
                    return
                p.emit("sync", lambda e, t=t: e.dma_start(
                    out=i1b[:, 0, :, :], in_=i1_s[t, :, :, :]),
                    waits=[xb_free.get(t - 1), passA_dG, passA_PE, passA_S,
                           passA_V], inc="dS", inc_n=16)
                p.emit("sync", lambda e, t=t: e.dma_start(
                    out=x2b[:, 0, :, :], in_=x2_s[t, :, :, :]),
                    waits=[], inc="dS", inc_n=16)
                xb_ready[t] = p.emit("sync", lambda e, t=t: e.dma_start(
                    out=iv1r[:, 0, :], in_=iv1_s[t, :, :]),
                    waits=[], inc="dS", inc_n=16)

            prefetch_b(0)
            barB_V = passA_V
            prev_out = None

            for t in range(ntb):
                par = t % 2
                tin = xb_ready[t]
                tb = p.emit("tensor", lambda e, par=par: e.matmul(
                    pb[1][:, 0:NTOK], ones_r[:, :], iv1r[0:1, 0, :],
                    start=True, stop=True), waits=[tin, tT, barB_V], inc="PE")
                tivc = p.emit("vector", lambda e: e.tensor_copy(
                    iv1bc[:, :], pb[1][:, 0:NTOK]), waits=[tb, passA_dG], inc="V")

                h_last = None
                h_toks = []
                for oc in range(OC):
                    rg = pb[2 + (oc % 2) * 2][:, 0:NTOK]
                    ru = pb[3 + (oc % 2) * 2]
                    lastg = None
                    lastu = None
                    for k in range(KC):
                        fst = (k == 0)
                        lst = (k == KC - 1)
                        lastg = p.emit("tensor", lambda e, k=k, oc=oc, rg=rg,
                                       par=par, fst=fst, lst=lst: e.matmul(
                            rg, tg_sb[:, k, oc * 128:(oc + 1) * 128],
                            i1b[:, 0, k, :], start=fst, stop=lst),
                            waits=([tin, (h_toks[oc - 2] if oc >= 2 else None)]
                                   if fst else []), inc="PE" if lst else None)
                    for k in range(KC):
                        fst = (k == 0)
                        lst = (k == KC - 1)
                        lastu = p.emit("tensor", lambda e, k=k, oc=oc, ru=ru,
                                       par=par, fst=fst, lst=lst: e.matmul(
                            ru[:, 0:NTOK], tu_sb[:, k, oc * 128:(oc + 1) * 128],
                            i1b[:, 0, k, :], start=fst, stop=lst),
                            waits=[], inc="PE" if lst else None)
                    tg1 = p.emit("vector", lambda e, oc=oc, rg=rg:
                                 e.scalar_tensor_tensor(
                        g1[:, oc % 2, :], rg, mghat[:, 0:1], iv1bc[:, :],
                        ALU.mult, ALU.mult), waits=[lastg, tivc], inc="V")
                    tgate = p.emit("scalar", lambda e, oc=oc: e.activation(
                        gate[:, oc % 2, :], g1[:, oc % 2, :], ACT.Silu),
                        waits=[tg1], inc="S")
                    tu1 = p.emit("vector", lambda e, oc=oc, ru=ru:
                                 e.scalar_tensor_tensor(
                        ru[:, NTOK:2 * NTOK], ru[:, 0:NTOK], mghat[:, 1:2],
                        iv1bc[:, :], ALU.mult, ALU.mult), waits=[lastu], inc="V")
                    th = p.emit("vector", lambda e, oc=oc, ru=ru: e.tensor_tensor(
                        pb[6][:, (oc % 2) * NTOK:(oc % 2 + 1) * NTOK],
                        gate[:, oc % 2, :], ru[:, NTOK:2 * NTOK], ALU.mult),
                        waits=[tgate, tu1], inc="V")
                    p.emit("vector", lambda e, oc=oc: e.tensor_copy(
                        hsb[:, oc, :],
                        pb[6][:, (oc % 2) * NTOK:(oc % 2 + 1) * NTOK]),
                        waits=[], inc="V")
                    if oc == 0:
                        p.emit("vector", lambda e: e.tensor_copy(
                            accpB[:, :], pb[6][:, 0:NTOK]), waits=[], inc="V")
                        h_last = p.emit("vector", lambda e: e.tensor_copy(
                            accnB[:, :], pb[6][:, 0:NTOK]), waits=[], inc="V")
                    else:
                        p.emit("vector", lambda e, oc=oc: e.tensor_tensor(
                            accpB[:, :], accpB[:, :],
                            pb[6][:, (oc % 2) * NTOK:(oc % 2 + 1) * NTOK],
                            ALU.max), waits=[], inc="V")
                        h_last = p.emit("vector", lambda e, oc=oc: e.tensor_tensor(
                            accnB[:, :], accnB[:, :],
                            pb[6][:, (oc % 2) * NTOK:(oc % 2 + 1) * NTOK],
                            ALU.min), waits=[], inc="V")
                    h_toks.append(h_last)

                p.emit("vector", lambda e: e.scalar_tensor_tensor(
                    amB[:, :], accnB[:, :], -1.0, accpB[:, :], ALU.mult, ALU.max),
                    waits=[h_last], inc="V")
                for half in (64, 32):
                    p.emit("vector", lambda e, h=half: e.tensor_copy(
                        amfB[0:h, :], amB[h:2 * h, :]), waits=[], inc="V")
                    p.emit("vector", lambda e, h=half: e.tensor_tensor(
                        amB[0:h, :], amB[0:h, :], amfB[0:h, :], ALU.max),
                        waits=[], inc="V")
                for half in (16, 8, 4, 2, 1):
                    msk = [(i + half) % 32 for i in range(32)]
                    p.emit("vector", lambda e, m=msk: e.stream_shuffle(
                        amfB[0:32, :], amB[0:32, :], m), waits=[], inc="V")
                    p.emit("vector", lambda e, h=half: e.tensor_tensor(
                        amB[0:h, :], amB[0:h, :], amfB[0:h, :], ALU.max),
                        waits=[], inc="V")
                p.emit("vector", lambda e: e.tensor_scalar(
                    rows[0:1, 0:NTOK], amB[0:1, :], 1e-5, None, ALU.max),
                    waits=[], inc="V")
                p.emit("vector", lambda e: e.reciprocal(
                    rows[0:1, NTOK:2 * NTOK], rows[0:1, 0:NTOK]), waits=[], inc="V")
                ts2 = p.emit("vector", lambda e: e.tensor_scalar(
                    rows2[0:1, 0:NTOK], rows[0:1, NTOK:2 * NTOK], 127.0, None,
                    ALU.mult), waits=[], inc="V")
                tiv2 = p.emit("vector", lambda e: e.tensor_scalar(
                    rows2[0:1, NTOK:2 * NTOK], rows[0:1, 0:NTOK], 1.0 / 127.0,
                    None, ALU.mult), waits=[], inc="V")
                tb1 = p.emit("tensor", lambda e: e.matmul(
                    pb[1][:, 0:NTOK], ones_r[:, :], rows2[0:1, 0:NTOK],
                    start=True, stop=True), waits=[ts2], inc="PE")
                tb2 = p.emit("tensor", lambda e: e.matmul(
                    pb[1][:, NTOK:2 * NTOK], ones_r[:, :],
                    rows2[0:1, NTOK:2 * NTOK], start=True, stop=True),
                    waits=[tiv2], inc="PE")
                tiv2c = p.emit("vector", lambda e: e.tensor_copy(
                    iv2bc[:, :], pb[1][:, NTOK:2 * NTOK]), waits=[tb2, tb1],
                    inc="V")

                ti2 = None
                for cc in range(0, OC, 2):
                    u0v = pb[2][:, 0:512]
                    pgA = pb[3][:, 0:512]
                    pgB = pb[6][:, 0:512]
                    p.emit("vector", lambda e, cc=cc: e.tensor_tensor(
                        pb[2][:, 0:NTOK], hsb[:, cc, :], pb[1][:, 0:NTOK],
                        ALU.mult), waits=[tiv2c], inc="V")
                    p.emit("vector", lambda e, cc=cc: e.tensor_tensor(
                        pb[2][:, NTOK:2 * NTOK], hsb[:, cc + 1, :], pb[1][:, 0:NTOK],
                        ALU.mult), waits=[], inc="V")
                    p.emit("vector", lambda e, u0v=u0v: e.tensor_scalar(
                        u0v, u0v, 128.0, None, ALU.add),
                        waits=[], inc="V")
                    srcv = u0v
                    for ki, k in enumerate(range(7, -1, -1)):
                        cth = float(2 ** k) - 0.5
                        p.emit("vector", lambda e, srcv=srcv, cth=cth, k=k:
                               e.tensor_scalar(
                            bsbB[:, :], srcv, cth, -float(2 ** k),
                            ALU.is_ge, ALU.mult), waits=[], inc="V")
                        dstv = ufinB[:, :] if k == 0 else (pgA if ki % 2 == 0 else pgB)
                        p.emit("vector", lambda e, srcv=srcv, dstv=dstv:
                               e.tensor_tensor(
                            dstv, bsbB[:, :], srcv, ALU.add), waits=[], inc="V")
                        srcv = dstv
                    p.emit("vector", lambda e, u0v=u0v: e.scalar_tensor_tensor(
                        bsbB[:, :], ufinB[:, :], -1.0, u0v, ALU.mult, ALU.add),
                        waits=[], inc="V")
                    p.emit("vector", lambda e, cc=cc: e.tensor_scalar(
                        i2sb[:, cc, :], bsbB[:, 0:NTOK], -128.0, None, ALU.add),
                        waits=[], inc="V")
                    ti2 = p.emit("vector", lambda e, cc=cc: e.tensor_scalar(
                        i2sb[:, cc + 1, :], bsbB[:, NTOK:2 * NTOK], -128.0, None,
                        ALU.add), waits=[], inc="V")

                s3_toks = []
                for mc in range(KC):
                    reg = pb[4][:, 0:NTOK] if mc % 2 == 0 else pb[5][:, 0:NTOK]
                    reg2 = (pb[4][:, NTOK:2 * NTOK] if mc % 2 == 0
                            else pb[5][:, NTOK:2 * NTOK])
                    last = None
                    for k in range(OC):
                        fst = (k == 0)
                        lst = (k == OC - 1)
                        last = p.emit("tensor", lambda e, k=k, mc=mc, reg=reg,
                                      fst=fst, lst=lst: e.matmul(
                            reg, td_sb[:, k, mc * 128:(mc + 1) * 128],
                            i2sb[:, k, :], start=fst, stop=lst),
                            waits=([ti2, (s3_toks[mc - 2] if mc >= 2 else None)]
                                   if fst else []), inc="PE" if lst else None)
                    p.emit("vector", lambda e, reg=reg, reg2=reg2:
                           e.scalar_tensor_tensor(
                        reg2, reg, mghat[:, 2:3], iv2bc[:, :], ALU.mult,
                        ALU.mult), waits=[last], inc="V")
                    s3_toks.append(p.emit("vector", lambda e, mc=mc, par=par,
                                          reg2=reg2: e.tensor_tensor(
                        s3sb[:, mc, :], x2b[:, 0, mc, :], reg2, ALU.add),
                        waits=[], inc="V"))
                xb_free[t] = s3_toks[-1]
                prefetch_b(t + 1)

                sq_pe3 = []
                mm3_last = None
                for c in range(KC):
                    tsqr = p.emit("scalar", lambda e, c=c: e.activation(
                        sqB[:, c % 2, :], s3sb[:, c, :], ACT.Square),
                        waits=[s3_toks[c], (sq_pe3[c - 2] if c >= 2 else None)],
                        inc="S")
                    mm3_last = p.emit("tensor", lambda e, c=c: e.matmul(
                        pb[0][0:1, 0:NTOK], ones_c[:, :], s3sb[:, c, :],
                        start=(c == 0), stop=(c == KC - 1), skip_group_check=True),
                        waits=[s3_toks[c]], inc="PE" if c == KC - 1 else None)
                    sq_pe3.append(p.emit("tensor", lambda e, c=c: e.matmul(
                        pb[1][0:1, 0:NTOK], ones_c[:, :], sqB[:, c % 2, :],
                        start=(c == 0), stop=(c == KC - 1), skip_group_check=True), waits=[tsqr],
                        inc="PE"))
                tmu = p.emit("vector", lambda e: e.tensor_scalar(
                    rows[0:1, 0:NTOK], pb[0][0:1, 0:NTOK], 1.0 / D, None,
                    ALU.mult), waits=[sq_pe3[-1], mm3_last], inc="V")
                p.emit("vector", lambda e: e.tensor_scalar(
                    rows[0:1, NTOK:2 * NTOK], pb[1][0:1, 0:NTOK], 1.0 / D,
                    None, ALU.mult), waits=[], inc="V")
                p.emit("vector", lambda e: e.tensor_tensor(
                    rows2[0:1, 0:NTOK], rows[0:1, 0:NTOK], rows[0:1, 0:NTOK],
                    ALU.mult), waits=[], inc="V")
                tvar = p.emit("vector", lambda e: e.tensor_tensor(
                    rows2[0:1, NTOK:2 * NTOK], rows[0:1, NTOK:2 * NTOK],
                    rows2[0:1, 0:NTOK], ALU.subtract), waits=[], inc="V")
                tsq = p.emit("scalar", lambda e: e.activation(
                    rows2[0:1, 0:NTOK], rows2[0:1, NTOK:2 * NTOK], ACT.Sqrt,
                    bias=eps1[:, :]), waits=[tvar], inc="S")
                trr = p.emit("vector", lambda e: e.reciprocal(
                    rows[0:1, NTOK:2 * NTOK], rows2[0:1, 0:NTOK]),
                    waits=[tsq], inc="V")
                tbc1 = p.emit("tensor", lambda e: e.matmul(
                    pb[0][:, 0:NTOK], ones_r[:, :], rows[0:1, 0:NTOK],
                    start=True, stop=True), waits=[tmu, trr], inc="PE")
                tbc2 = p.emit("tensor", lambda e: e.matmul(
                    pb[0][:, NTOK:2 * NTOK], ones_r[:, :],
                    rows[0:1, NTOK:2 * NTOK], start=True, stop=True),
                    waits=[trr], inc="PE")
                trbc = p.emit("vector", lambda e: e.tensor_copy(
                    rbcB[:, :], pb[0][:, NTOK:2 * NTOK]), waits=[tbc2], inc="V")
                for c in range(KC):
                    reg = pb[2] if c % 2 == 0 else pb[3]
                    p.emit("vector", lambda e, c=c, reg=reg: e.tensor_tensor(
                        reg[:, 0:NTOK], s3sb[:, c, :], pb[0][:, 0:NTOK],
                        ALU.subtract), waits=[tbc2, prev_out], inc="V")
                    otok = p.emit("vector", lambda e, c=c, reg=reg: e.tensor_tensor(
                        outsb[:, c, :], reg[:, 0:NTOK], rbcB[:, :], ALU.mult),
                        waits=[trbc], inc="V")
                    prev_out = p.emit("gpsimd", lambda e, t=t, c=c: e.dma_start(
                        out=out_t[:, t, :].rearrange("(k p) n -> p k n", p=128)[
                            :, c, :],
                        in_=outsb[:, c, :]), waits=[otok], inc="dG", inc_n=16)
                barB_V = p.ck("V")

            final_dg = p.ck("dG")
            p.emit("gpsimd", lambda e: e.engine_nop(), waits=[final_dg], inc="G")

        block = es.enter_context(nc.Block())
        p.build(block)

    return nc, dumps


def _host_inputs(core, L0_attn_x, L0_attn_WQ, L0_attn_bQ, L0_attn_WK, L0_attn_bK,
                 L0_attn_WV, L0_attn_bV, L0_attn_WO, L0_attn_bO, L0_attn_Wg,
                 L0_attn_bg, L0_attn_alpha_logit, L0_attn_ln_gamma,
                 L0_attn_ln_beta, L0_ffn_w_gate, L0_ffn_w_up, L0_ffn_w_down):
    xc = np.asarray(L0_attn_x)[4 * core:4 * core + 4]
    x_t = np.ascontiguousarray(
        xc.transpose(2, 1, 0).reshape(D, NT, C, BC).transpose(0, 1, 3, 2)
        .reshape(D, NT, NTOK))

    def colform(b):
        return np.ascontiguousarray(np.asarray(b, np.float32).reshape(KC, 128).T)
    a = np.asarray(L0_attn_alpha_logit, np.float32).reshape(KC, 2, 64)
    alpha_c = np.ascontiguousarray(a.transpose(1, 2, 0).reshape(128, KC))
    pp, ff = np.meshgrid(np.arange(128), np.arange(NTOK), indexing="ij")
    mask_t = ((pp % 64) <= (ff % 64)).astype(np.float32)
    return {
        "x_t": x_t,
        "wq_t": np.ascontiguousarray(np.asarray(L0_attn_WQ, np.float32).T),
        "wk_t": np.ascontiguousarray(np.asarray(L0_attn_WK, np.float32).T),
        "wv_t": np.ascontiguousarray(np.asarray(L0_attn_WV, np.float32).T),
        "wg_t": np.ascontiguousarray(np.asarray(L0_attn_Wg, np.float32).T),
        "wo_t": np.ascontiguousarray(np.asarray(L0_attn_WO, np.float32).T),
        "wg1_t": np.ascontiguousarray(np.asarray(L0_ffn_w_gate, np.float32).T),
        "wu1_t": np.ascontiguousarray(np.asarray(L0_ffn_w_up, np.float32).T),
        "wd1_t": np.ascontiguousarray(np.asarray(L0_ffn_w_down, np.float32).T),
        "bq_c": colform(L0_attn_bQ), "bk_c": colform(L0_attn_bK),
        "bv_c": colform(L0_attn_bV), "bg_c": colform(L0_attn_bg),
        "bo_c": colform(L0_attn_bO),
        "gamma_c": colform(L0_attn_ln_gamma), "beta_c": colform(L0_attn_ln_beta),
        "alpha_c": alpha_c,
        "bv_row": np.ascontiguousarray(
            np.asarray(L0_attn_bV, np.float32).reshape(1, DI)),
        "mask_t": mask_t,
        "ones_col": np.ones((128, 1), np.float32),
        "ones_row": np.ones((1, 128), np.float32),
        "ident64": np.eye(64, dtype=np.float32),
    }


def kernel(**inputs):
    nc, _ = build_program()
    in_maps = [_host_inputs(c, **inputs) for c in range(8)]
    res = run_bass_kernel_spmd(nc, in_maps, core_ids=list(range(8)))
    outs = []
    for c in range(8):
        ot = np.asarray(res.results[c]["out_t"])
        oc = (ot.reshape(D, NT, BC, C).transpose(0, 1, 3, 2)
              .reshape(D, T, BC).transpose(2, 1, 0))
        outs.append(oc)
    return np.ascontiguousarray(np.concatenate(outs, axis=0)).astype(np.float32)


# version 10
# speedup vs baseline: 1.0123x; 1.0123x over previous
import contextlib
import numpy as np

import concourse.bass as bass
import concourse.mybir as mybir
from concourse.bass_utils import run_bass_kernel_spmd

F32 = mybir.dt.float32
F16 = mybir.dt.float16
BF16 = mybir.dt.bfloat16
FP8 = mybir.dt.float8e4
ALU = mybir.AluOpType
ACT = mybir.ActivationFunctionType
AX = mybir.AxisListType

D = 1024
DI = 1024
BC = 4
T = 2048
C = 64
NT = T // C
NTOK = BC * C
KC = D // 128
FF = 4096
OC = FF // 128
LN_EPS = 1e-5

ENGINES = ("sync", "tensor", "vector", "scalar", "gpsimd")


class Plan:

    def __init__(self, nc):
        self.nc = nc
        self.ops = {e: [] for e in ENGINES}
        self.cnt = {}
        self.sem_objs = {}
        self.last_wait = {e: {} for e in ENGINES}
        self.own = {"sync": None, "tensor": "PE", "vector": "V",
                    "scalar": "S", "gpsimd": "G"}

    def add_sem(self, name, obj):
        self.sem_objs[name] = obj
        self.cnt[name] = 0

    def emit(self, eng, fn, waits=(), inc=None, inc_n=1):
        rw = []
        for w in waits:
            if w is None:
                continue
            s, v = w
            if v <= 0:
                continue
            assert v <= self.cnt[s], f"wait on future {s}:{v} > {self.cnt[s]}"
            if s == self.own[eng]:
                continue
            if self.last_wait[eng].get(s, -1) >= v:
                continue
            rw.append((s, v))
            self.last_wait[eng][s] = v
        tok = None
        if inc is not None:
            self.cnt[inc] += inc_n
            tok = (inc, self.cnt[inc])
        self.ops[eng].append((fn, rw, (inc, inc_n) if inc else None))
        return tok

    def ck(self, sem):
        return (sem, self.cnt[sem])

    def build(self, block):
        sem_objs = self.sem_objs
        for eng in ENGINES:
            ops = self.ops[eng]
            if not ops:
                continue

            def make(ops):
                def run(e):
                    for fn, waits, inc in ops:
                        for s, v in waits:
                            e.wait_ge(sem_objs[s], v)
                        r = fn(e)
                        if inc is not None:
                            s, n = inc
                            r.then_inc(sem_objs[s], n)
                return run
            getattr(block, eng)(make(ops))


def build_program(dump=(), nta=NT, ntb=NT, pha=99):
    nc = bass.Bass("TRN2", target_bir_lowering=False,
                   detect_race_conditions=False)
    dump = set(dump)

    x_t = nc.dram_tensor("x_t", [D, NT, NTOK], F32, kind="ExternalInput").ap()
    w_in = {}
    for w in ("wq", "wk", "wv", "wg", "wo"):
        w_in[w] = nc.dram_tensor(w + "_t", [D, DI], F32, kind="ExternalInput").ap()
    wg1_t = nc.dram_tensor("wg1_t", [D, FF], F32, kind="ExternalInput").ap()
    wu1_t = nc.dram_tensor("wu1_t", [D, FF], F32, kind="ExternalInput").ap()
    wd1_t = nc.dram_tensor("wd1_t", [FF, D], F32, kind="ExternalInput").ap()
    b_in = {}
    for b in ("bq", "bk", "bv", "bg", "bo"):
        b_in[b] = nc.dram_tensor(b + "_c", [128, KC], F32, kind="ExternalInput").ap()
    gamma_c = nc.dram_tensor("gamma_c", [128, KC], F32, kind="ExternalInput").ap()
    beta_c = nc.dram_tensor("beta_c", [128, KC], F32, kind="ExternalInput").ap()
    alpha_in = nc.dram_tensor("alpha_c", [128, KC], F32, kind="ExternalInput").ap()
    bv_row = nc.dram_tensor("bv_row", [1, DI], F32, kind="ExternalInput").ap()
    mask_in = nc.dram_tensor("mask_t", [128, NTOK], F32, kind="ExternalInput").ap()
    onesc_in = nc.dram_tensor("ones_col", [128, 1], F32, kind="ExternalInput").ap()
    onesr_in = nc.dram_tensor("ones_row", [1, 128], F32, kind="ExternalInput").ap()
    ident_in = nc.dram_tensor("ident64", [64, 64], F32, kind="ExternalInput").ap()
    out_t = nc.dram_tensor("out_t", [D, NT, NTOK], F32, kind="ExternalOutput").ap()

    whi_s = nc.dram_tensor("whi_s", [5, 128, KC, DI], F16, kind="Internal").ap()
    wlo_s = nc.dram_tensor("wlo_s", [5, 128, KC, DI], BF16, kind="Internal").ap()
    tg_s = nc.dram_tensor("tg_s", [128, KC, FF], FP8, kind="Internal").ap()
    tu_s = nc.dram_tensor("tu_s", [128, KC, FF], FP8, kind="Internal").ap()
    td_s = nc.dram_tensor("td_s", [128, OC, D], FP8, kind="Internal").ap()
    i1_s = nc.dram_tensor("i1_s", [NT, 128, KC, NTOK], BF16, kind="Internal").ap()
    x2_s = nc.dram_tensor("x2_s", [NT, 128, KC, NTOK], F32, kind="Internal").ap()
    iv1_s = nc.dram_tensor("iv1_s", [NT, 1, NTOK], F32, kind="Internal").ap()

    dumps = {}
    def dump_out(name, shape, dtype=F32):
        if name in dump and name not in dumps:
            dumps[name] = nc.dram_tensor("dbg_" + name, shape, dtype,
                                         kind="ExternalOutput").ap()
        return dumps.get(name)

    es = contextlib.ExitStack()
    with es:
        p = Plan(nc)
        for s in ("dS", "dG", "PE", "V", "S", "G"):
            p.add_sem(s, es.enter_context(nc.semaphore(s)))

        def sb(name, shape, dt=F32):
            return es.enter_context(nc.sbuf_tensor(name, shape, dt))

        pb = [es.enter_context(nc.psum_tensor("pb%d" % i, [128, 512], F32))
              for i in range(8)]
        bias = {b: sb(b, [128, KC]) for b in ("bq", "bk", "bv", "bg", "bo")}
        gam = sb("gam", [128, KC]); bet = sb("bet", [128, KC])
        alpha = sb("alpha", [128, KC])
        a64 = sb("a64", [128, KC])
        mghat = sb("mghat", [128, 3])
        mask = sb("mask", [128, NTOK])
        ones_c = sb("ones_c", [128, 1])
        ones_r = sb("ones_r", [1, 128])
        ident = sb("ident", [128, 64])
        eps1 = sb("eps1", [1, 1])
        rows = sb("rows", [1, 512])
        rows2 = sb("rows2", [1, 512])
        negc = sb("negc", [128, 8])
        aq = sb("aq", [128, KC, 64])
        ak = sb("ak", [128, KC, 64])
        bvbc = sb("bvbc", [128, DI])

        def dma(eng, out, in_, waits=()):
            return p.emit(eng, lambda e, o=out, i=in_: e.dma_start(out=o, in_=i),
                          waits=waits,
                          inc="dS" if eng in ("sync", "scalar") else "dG", inc_n=16)

        for dst, src in [(mask[:, :], mask_in[:, :]),
                         (ones_c[:, :], onesc_in[:, :]),
                         (ones_r[:, :], onesr_in[:, :]),
                         (ident[0:64, :], ident_in[:, :]),
                         (ident[64:128, :], ident_in[:, :]),
                         (gam[:, :], gamma_c[:, :]),
                         (bet[:, :], beta_c[:, :]),
                         (alpha[:, :], alpha_in[:, :])]:
            dma("sync", dst, src)
        for b in bias:
            dma("sync", bias[b][:, :], b_in[b][:, :])
        t_consts = p.ck("dS")

        p.emit("vector", lambda e: e.memset(eps1[:, :], LN_EPS), waits=[], inc="V")
        for k in range(8):
            cval = -(127.5 - 2.0 ** -17)
            p.emit("vector", lambda e, k=k, c=cval: e.memset(negc[:, k:k + 1], c),
                   waits=[], inc="V")
        tok_alpha = p.emit("scalar", lambda e: e.activation(
            alpha[:, :], alpha[:, :], ACT.Sigmoid), waits=[t_consts], inc="S")

        esp = contextlib.ExitStack()
        def sbp(name, shape, dt=F32):
            return esp.enter_context(nc.sbuf_tensor(name, shape, dt))
        apow = sbp("apow", [128, 64])
        apneg = sbp("apneg", [128, 64])
        acols = sbp("acols", [128, 16])
        def vdrain():
            p.emit("vector", lambda e: e.drain(), waits=[], inc=None)

        for c in range(KC):
            al = alpha[:, c:c + 1]
            def em(fn, waits=()):
                tok = p.emit("vector", fn, waits=waits, inc="V")
                vdrain()
                return tok
            w0 = [tok_alpha]
            em(lambda e, al=al: e.tensor_tensor(acols[:, 0:1], al, al, ALU.mult), w0)
            for j in range(1, 5):
                em(lambda e, j=j: e.tensor_tensor(
                    acols[:, j:j + 1], acols[:, j - 1:j], acols[:, j - 1:j], ALU.mult))
            em(lambda e, c=c: e.tensor_tensor(
                a64[:, c:c + 1], acols[:, 4:5], acols[:, 4:5], ALU.mult))
            em(lambda e, al=al: e.reciprocal(acols[:, 5:6], al), w0)
            for j in range(6, 11):
                em(lambda e, j=j: e.tensor_tensor(
                    acols[:, j:j + 1], acols[:, j - 1:j], acols[:, j - 1:j], ALU.mult))
            em(lambda e: e.memset(apow[:, 0:1], 1.0))
            em(lambda e, al=al: e.tensor_copy(apow[:, 1:2], al), w0)
            for j in range(1, 6):
                n = 2 ** j
                em(lambda e, n=n, j=j: e.tensor_scalar(
                    apow[:, n:2 * n], apow[:, 0:n], acols[:, j - 1:j], None, ALU.mult))
            em(lambda e: e.memset(apneg[:, 0:1], 1.0))
            em(lambda e: e.tensor_copy(apneg[:, 1:2], acols[:, 5:6]))
            for j in range(1, 6):
                n = 2 ** j
                em(lambda e, n=n, j=j: e.tensor_scalar(
                    apneg[:, n:2 * n], apneg[:, 0:n], acols[:, 5 + j:6 + j], None,
                    ALU.mult))
            em(lambda e, c=c: e.tensor_scalar(
                aq[:, c, :], apow[:, :], acols[:, 10:11], None, ALU.mult))
            em(lambda e, c=c: e.tensor_scalar(
                ak[:, c, :], apneg[:, :], acols[:, 4:5], None, ALU.mult))
        tok_atiles = p.ck("V")

        bvrow_sb = sbp("bvrow_sb", [1, DI])
        t_bv = dma("sync", bvrow_sb[:, :], bv_row[:, :])
        t1 = p.emit("tensor", lambda e: e.matmul(
            pb[0][:, 0:512], ones_r[:, :], bvrow_sb[:, 0:512],
            start=True, stop=True), waits=[t_bv, t_consts], inc="PE")
        t2 = p.emit("tensor", lambda e: e.matmul(
            pb[1][:, 0:512], ones_r[:, :], bvrow_sb[:, 512:1024],
            start=True, stop=True), waits=[t_bv], inc="PE")
        p.emit("vector", lambda e: e.tensor_copy(bvbc[:, 0:512], pb[0][:, 0:512]),
               waits=[t1], inc="V")
        t_bvbc = p.emit("vector", lambda e: e.tensor_copy(
            bvbc[:, 512:1024], pb[1][:, 0:512]), waits=[t2], inc="V")

        wtmp = sbp("wtmp", [128, 2, DI])
        whtmp = sbp("whtmp", [128, 2, DI], F16)
        wltmp = sbp("wltmp", [128, 2, DI], BF16)
        sfree = [None, None]
        for wi, w in enumerate(("wq", "wk", "wv", "wg", "wo")):
            for k in range(KC):
                par = (wi * KC + k) % 2
                tin = dma("sync", wtmp[:, par, :], w_in[w][k * 128:(k + 1) * 128, :],
                          waits=[sfree[par]])
                th = p.emit("vector", lambda e, par=par: e.tensor_copy(
                    whtmp[:, par, :], wtmp[:, par, :]), waits=[tin], inc="V")
                tl = p.emit("vector", lambda e, par=par: e.tensor_tensor(
                    wltmp[:, par, :], wtmp[:, par, :], whtmp[:, par, :],
                    ALU.subtract), waits=[], inc="V")
                dma("sync", whi_s[wi, :, k, :], whtmp[:, par, :], waits=[th])
                sfree[par] = dma("sync", wlo_s[wi, :, k, :], wltmp[:, par, :],
                                 waits=[tl])

        ftmp = sbp("ftmp", [128, 2, FF])
        fb1 = sbp("fb1", [128, FF])
        ftern = sbp("ftern", [128, FF], FP8)
        redc = sbp("redc", [128, 4])
        foldt = sbp("foldt", [64, NTOK])
        for wi, (src, dst, kn, cols) in enumerate(
                [(wg1_t, tg_s, KC, FF), (wu1_t, tu_s, KC, FF), (wd1_t, td_s, OC, D)]):
            tz = p.emit("vector", lambda e: e.memset(redc[:, 0:1], 0.0),
                        waits=[], inc="V")
            lin = [None, None]
            for k in range(kn):
                par = k % 2
                tin = dma("sync", ftmp[:, par, 0:cols],
                          src[k * 128:(k + 1) * 128, :], waits=[lin[par]])
                p.emit("vector", lambda e, par=par, cols=cols: e.tensor_reduce(
                    redc[:, 1:2], ftmp[:, par, 0:cols], AX.X, ALU.add,
                    apply_absolute_value=True), waits=[tin, tz], inc="V")
                vdrain()
                lin[par] = p.emit("vector", lambda e: e.tensor_tensor(
                    redc[:, 0:1], redc[:, 0:1], redc[:, 1:2], ALU.add),
                    waits=[], inc="V")
                vdrain()
            for half in (64, 32):
                p.emit("vector", lambda e, h=half: e.tensor_copy(
                    foldt[0:h, 0:1], redc[h:2 * h, 0:1]), waits=[], inc="V")
                vdrain()
                p.emit("vector", lambda e, h=half: e.tensor_tensor(
                    redc[0:h, 0:1], redc[0:h, 0:1], foldt[0:h, 0:1], ALU.add),
                    waits=[], inc="V")
                vdrain()
            for half in (16, 8, 4, 2, 1):
                msk = [(i + half) % 32 for i in range(32)]
                p.emit("vector", lambda e, m=msk: e.stream_shuffle(
                    foldt[0:32, 0:1], redc[0:32, 0:1], m), waits=[], inc="V")
                vdrain()
                p.emit("vector", lambda e, h=half: e.tensor_tensor(
                    redc[0:h, 0:1], redc[0:h, 0:1], foldt[0:h, 0:1], ALU.add),
                    waits=[], inc="V")
                vdrain()
            p.emit("vector", lambda e, n=float(kn * 128 * cols): e.tensor_scalar(
                rows[0:1, 0:1], redc[0:1, 0:1], 1.0 / n, 1e-5, ALU.mult, ALU.max),
                waits=[], inc="V")
            vdrain()
            tsw = p.emit("vector", lambda e: e.reciprocal(
                rows[0:1, 1:2], rows[0:1, 0:1]), waits=[], inc="V")
            vdrain()
            tm1 = p.emit("tensor", lambda e: e.matmul(
                pb[0][:, 0:1], ones_r[:, :], rows[0:1, 0:1], start=True, stop=True),
                waits=[tsw, t_bvbc], inc="PE")
            tm2 = p.emit("tensor", lambda e: e.matmul(
                pb[0][:, 1:2], ones_r[:, :], rows[0:1, 1:2], start=True, stop=True),
                waits=[], inc="PE")
            p.emit("vector", lambda e, wi=wi: e.tensor_copy(
                mghat[:, wi:wi + 1], pb[0][:, 0:1]), waits=[tm1, tm2], inc="V")
            tswc = p.emit("vector", lambda e: e.tensor_copy(
                redc[:, 2:3], pb[0][:, 1:2]), waits=[], inc="V")
            lin = [None, None]
            for k in range(kn):
                par = k % 2
                tin = dma("sync", ftmp[:, par, 0:cols],
                          src[k * 128:(k + 1) * 128, :], waits=[lin[par], tswc])
                p.emit("vector", lambda e, par=par, cols=cols: e.tensor_scalar(
                    fb1[:, 0:cols], ftmp[:, par, 0:cols], redc[:, 2:3], 0.5,
                    ALU.mult, ALU.is_gt), waits=[tin], inc="V")
                p.emit("vector", lambda e, par=par, cols=cols: e.tensor_scalar(
                    ftmp[:, par, 0:cols], ftmp[:, par, 0:cols], redc[:, 2:3], -0.5,
                    ALU.mult, ALU.is_lt), waits=[], inc="V")
                t3 = p.emit("vector", lambda e, par=par, cols=cols: e.tensor_tensor(
                    ftern[:, 0:cols], fb1[:, 0:cols], ftmp[:, par, 0:cols],
                    ALU.subtract), waits=[], inc="V")
                lin[par] = dma("sync", dst[:, k, :], ftern[:, 0:cols], waits=[t3])
        t_prelude_dS = p.ck("dS")
        esp.close()

        barrier = {s: p.ck(s) for s in ("PE", "V", "S", "dG")}

        with contextlib.ExitStack() as esa:
            def sba(name, shape, dt=F32):
                return esa.enter_context(nc.sbuf_tensor(name, shape, dt))

            xsb = sba("xsb", [128, 1, KC, NTOK])
            sq = sba("sq", [128, 2, NTOK])
            xnhi = sba("xnhi", [128, KC, NTOK], F16)
            xnlo = sba("xnlo", [128, KC, NTOK], BF16)
            qf = sba("qf", [128, KC, NTOK])
            kf = sba("kf", [128, KC, NTOK])
            gs = sba("gs", [128, KC, NTOK])
            vtm = sba("vtm", [128, 2, DI])
            ktm = sba("ktm", [128, 2, DI])
            pmk = sba("pmk", [128, NTOK])
            o1sb = sba("o1sb", [128, NTOK])
            srm = sba("srm", [128, NTOK])
            ohi = sba("ohi", [128, KC, NTOK], F16)
            olo = sba("olo", [128, KC, NTOK], BF16)
            ssb = sba("ssb", [128, KC, NTOK])
            x2sb = sba("x2sb", [128, 2, KC, NTOK])
            i1sb = sba("i1sb", [128, KC, NTOK], BF16)
            accp = sba("accp", [128, NTOK])
            accn = sba("accn", [128, NTOK])
            am = sba("am", [128, NTOK])
            amf = sba("amf", [64, NTOK])
            rbc_sb = sba("rbc_sb", [128, NTOK])
            bsb = sba("bsb", [128, 512])
            ufin = sba("ufin", [128, 512])
            hp = [sba("hp0", [128, KC, NTOK]), sba("hp1", [128, KC, NTOK])]
            wbh = sba("wbh", [128, 2, KC, DI], F16)
            wbl = sba("wbl", [128, 2, KC, DI], BF16)

            th0 = p.emit("vector", lambda e: e.memset(hp[0][:, :, :], 0.0),
                         waits=[t_prelude_dS], inc="V")
            p.emit("vector", lambda e: e.memset(hp[1][:, :, :], 0.0),
                   waits=[], inc="V")

            WROW = (2, 0, 1, 3, 4)
            w_free = {}
            w_ready = {}

            def prefetch_w(m):
                if m >= nta * 5 or m in w_ready:
                    return
                par = m % 2
                wi = WROW[m % 5]
                p.emit("sync", lambda e, wi=wi, par=par: e.dma_start(
                    out=wbh[:, par, :, :], in_=whi_s[wi, :, :, :]),
                    waits=[w_free.get(m - 2), t_prelude_dS], inc="dS", inc_n=16)
                w_ready[m] = p.emit("sync", lambda e, wi=wi, par=par: e.dma_start(
                    out=wbl[:, par, :, :], in_=wlo_s[wi, :, :, :]),
                    waits=[], inc="dS", inc_n=16)

            x_free = {}
            x_ready = {}

            def prefetch_x(t):
                if t >= nta or t in x_ready:
                    return
                x_ready[t] = p.emit("sync", lambda e, t=t: e.dma_start(
                    out=xsb[:, 0, :, :],
                    in_=x_t[:, t, :].rearrange("(k p) n -> p k n", p=128),
                    ), waits=[x_free.get(t - 1), t_prelude_dS],
                    inc="dS", inc_n=16)

            prefetch_w(0); prefetch_w(1)
            prefetch_x(0)

            prev_store = None
            defer_ctx = {}

            def emit_deferred(dt, piece=None):
                nonlocal prev_store, ti1_last, prev_iv, prev_st1
                ctx = defer_ctx[dt]
                dpar = ctx["par"]
                x2c = lambda c: x2sb[:, dpar, c, :]
                if piece is not None and piece > 0:
                    tbs1 = ctx["tbs1"]
                    pairs = {1: [0], 2: [2], 3: [4, 6]}[piece]
                    for cc in pairs:
                        emit_ladder_pair(dt, ctx, cc, x2c, tbs1)
                    return
                fold_last = None
                for c in range(KC):
                    if c == 0:
                        p.emit("vector", lambda e, c=c: e.tensor_copy(
                            accp[:, :], x2c(c)), waits=[ctx["x2_toks"][0]],
                            inc="V")
                        fold_last = p.emit("vector", lambda e, c=c: e.tensor_copy(
                            accn[:, :], x2c(c)), waits=[], inc="V")
                    else:
                        p.emit("vector", lambda e, c=c: e.tensor_tensor(
                            accp[:, :], accp[:, :], x2c(c), ALU.max),
                            waits=[ctx["x2_toks"][c]], inc="V")
                        fold_last = p.emit("vector", lambda e, c=c: e.tensor_tensor(
                            accn[:, :], accn[:, :], x2c(c), ALU.min),
                            waits=[], inc="V")
                p.emit("vector", lambda e: e.scalar_tensor_tensor(
                    am[:, :], accn[:, :], -1.0, accp[:, :], ALU.mult, ALU.max),
                    waits=[fold_last], inc="V")
                for half in (64, 32):
                    p.emit("vector", lambda e, h=half: e.tensor_copy(
                        amf[0:h, :], am[h:2 * h, :]), waits=[], inc="V")
                    p.emit("vector", lambda e, h=half: e.tensor_tensor(
                        am[0:h, :], am[0:h, :], amf[0:h, :], ALU.max),
                        waits=[], inc="V")
                for half in (16, 8, 4, 2, 1):
                    msk = [(i + half) % 32 for i in range(32)]
                    p.emit("vector", lambda e, m=msk: e.stream_shuffle(
                        amf[0:32, :], am[0:32, :], m), waits=[], inc="V")
                    p.emit("vector", lambda e, h=half: e.tensor_tensor(
                        am[0:h, :], am[0:h, :], amf[0:h, :], ALU.max),
                        waits=[], inc="V")
                p.emit("vector", lambda e: e.tensor_scalar(
                    rows[0:1, 0:NTOK], am[0:1, :], 1e-5, None, ALU.max),
                    waits=[], inc="V")
                p.emit("vector", lambda e: e.reciprocal(
                    rows[0:1, NTOK:2 * NTOK], rows[0:1, 0:NTOK]),
                    waits=[], inc="V")
                ts1 = p.emit("vector", lambda e: e.tensor_scalar(
                    rows2[0:1, 0:NTOK], rows[0:1, NTOK:2 * NTOK], 127.0, None,
                    ALU.mult), waits=[], inc="V")
                tiv = p.emit("vector", lambda e: e.tensor_scalar(
                    rows2[0:1, NTOK:2 * NTOK], rows[0:1, 0:NTOK], 1.0 / 127.0,
                    None, ALU.mult), waits=[], inc="V")
                tbs1 = p.emit("tensor", lambda e: e.matmul(
                    pb[1][:, 0:NTOK], ones_r[:, :], rows2[0:1, 0:NTOK],
                    start=True, stop=True), waits=[ts1], inc="PE")
                tivd = p.emit("gpsimd", lambda e, dt=dt: e.dma_start(
                    out=iv1_s[dt, :, :], in_=rows2[0:1, NTOK:2 * NTOK]),
                    waits=[tiv], inc="dG", inc_n=16)
                prev_iv = tivd
                ctx["tbs1"] = tbs1
                ctx["fold_last"] = fold_last
                if piece is None:
                    for cc in range(0, KC, 2):
                        emit_ladder_pair(dt, ctx, cc, x2c, tbs1)

            def emit_ladder_pair(dt, ctx, cc, x2c, tbs1):
                nonlocal prev_store, prev_st1, ti1_last
                u0v = pb[5][:, 0:512]
                pgA = pb[6][:, 0:512]
                pgB = pb[7][:, 0:512]
                p.emit("vector", lambda e, cc=cc, u0v=u0v: e.tensor_tensor(
                    u0v[:, 0:NTOK], x2c(cc), pb[1][:, 0:NTOK], ALU.mult),
                    waits=[tbs1, prev_st1 if cc == 0 else None], inc="V")
                p.emit("vector", lambda e, cc=cc, u0v=u0v: e.tensor_tensor(
                    u0v[:, NTOK:2 * NTOK], x2c(cc + 1), pb[1][:, 0:NTOK],
                    ALU.mult), waits=[], inc="V")
                p.emit("vector", lambda e, u0v=u0v: e.tensor_scalar(
                    u0v, u0v, 128.0, None, ALU.add), waits=[], inc="V")
                srcv = u0v
                for ki, k in enumerate(range(7, -1, -1)):
                    cth = float(2 ** k) - 0.5
                    p.emit("vector", lambda e, srcv=srcv, cth=cth, k=k:
                           e.tensor_scalar(
                        bsb[:, :], srcv, cth, -float(2 ** k),
                        ALU.is_ge, ALU.mult), waits=[], inc="V")
                    dstv = ufin[:, :] if k == 0 else (pgA if ki % 2 == 0 else pgB)
                    p.emit("vector", lambda e, srcv=srcv, dstv=dstv:
                           e.tensor_tensor(
                        dstv, bsb[:, :], srcv, ALU.add), waits=[], inc="V")
                    srcv = dstv
                p.emit("vector", lambda e, u0v=u0v: e.scalar_tensor_tensor(
                    bsb[:, :], ufin[:, :], -1.0, u0v, ALU.mult, ALU.add),
                    waits=[], inc="V")
                p.emit("vector", lambda e, cc=cc: e.tensor_scalar(
                    i1sb[:, cc, :], bsb[:, 0:NTOK], -128.0, None, ALU.add),
                    waits=[], inc="V")
                ti1 = p.emit("vector", lambda e, cc=cc: e.tensor_scalar(
                    i1sb[:, cc + 1, :], bsb[:, NTOK:2 * NTOK], -128.0, None,
                    ALU.add), waits=[], inc="V")
                ti1_last = ti1
                if cc == KC - 2:
                    defer_ctx.pop(dt)
                    prev_st1 = p.emit("gpsimd", lambda e, dt=dt: e.dma_start(
                        out=i1_s[dt, :, :, :], in_=i1sb[:, :, :]), waits=[ti1],
                        inc="dG", inc_n=16)
                    prev_store = p.emit("gpsimd", lambda e, dt=dt: e.dma_start(
                        out=x2_s[dt, :, :, :], in_=x2sb[:, ctx["par"], :, :]),
                        waits=[ctx["fold_last"]], inc="dG", inc_n=16)

            ti1_last = None
            prev_iv = None
            prev_st1 = None
            prev_sqpe = None
            pb01_free = None
            prev_S = None
            prev_tlo = None
            wo_v_ck = None
            for t in range(nta):
                par = t % 2
                bar = []
                xc = lambda c: xsb[:, 0, c, :]
                tx = x_ready[t]

                if pha < 1:
                    barrier = {s: p.ck(s) for s in ("PE", "V", "S", "dG")}
                    continue
                sq_pe = []
                mm1_last = None
                for c in range(KC):
                    tsqr = p.emit("scalar", lambda e, c=c: e.activation(
                        sq[:, c % 2, :], xc(c), ACT.Square),
                        waits=[tx, (sq_pe[c - 2] if c >= 2 else prev_sqpe)],
                        inc="S")
                    mm1_last = p.emit("tensor", lambda e, c=c: e.matmul(
                        pb[0][0:1, 0:NTOK], ones_c[:, :], xc(c),
                        start=(c == 0), stop=(c == KC - 1), skip_group_check=True),
                        waits=[tx, t_consts, pb01_free],
                        inc="PE" if c == KC - 1 else None)
                    sq_pe.append(p.emit("tensor", lambda e, c=c: e.matmul(
                        pb[7][0:1, 0:NTOK], ones_c[:, :], sq[:, c % 2, :],
                        start=(c == 0), stop=(c == KC - 1), skip_group_check=True), waits=[tsqr], inc="PE"))

                def ln_rows(tok_mm, prev_dma=None):
                    tmu = p.emit("vector", lambda e: e.tensor_scalar(
                        rows[0:1, 0:NTOK], pb[0][0:1, 0:NTOK], 1.0 / D, None,
                        ALU.mult), waits=[tok_mm, prev_dma], inc="V")
                    p.emit("vector", lambda e: e.tensor_scalar(
                        rows[0:1, NTOK:2 * NTOK], pb[7][0:1, 0:NTOK],
                        1.0 / D, None, ALU.mult), waits=[], inc="V")
                    p.emit("vector", lambda e: e.tensor_tensor(
                        rows2[0:1, 0:NTOK], rows[0:1, 0:NTOK], rows[0:1, 0:NTOK],
                        ALU.mult), waits=[], inc="V")
                    tvar = p.emit("vector", lambda e: e.tensor_tensor(
                        rows2[0:1, NTOK:2 * NTOK], rows[0:1, NTOK:2 * NTOK],
                        rows2[0:1, 0:NTOK], ALU.subtract), waits=[], inc="V")
                    tsq = p.emit("scalar", lambda e: e.activation(
                        rows2[0:1, 0:NTOK], rows2[0:1, NTOK:2 * NTOK], ACT.Sqrt,
                        bias=eps1[:, :]), waits=[tvar], inc="S")
                    trr = p.emit("vector", lambda e: e.reciprocal(
                        rows[0:1, NTOK:2 * NTOK], rows2[0:1, 0:NTOK]),
                        waits=[tsq], inc="V")
                    tbc1 = p.emit("tensor", lambda e: e.matmul(
                        pb[0][:, 0:NTOK], ones_r[:, :], rows[0:1, 0:NTOK],
                        start=True, stop=True), waits=[tmu, trr], inc="PE")
                    tbc2 = p.emit("tensor", lambda e: e.matmul(
                        pb[0][:, NTOK:2 * NTOK], ones_r[:, :],
                        rows[0:1, NTOK:2 * NTOK], start=True, stop=True),
                        waits=[trr], inc="PE")
                    trbc = p.emit("vector", lambda e: e.tensor_copy(
                        rbc_sb[:, :], pb[0][:, NTOK:2 * NTOK]), waits=[tbc2],
                        inc="V")
                    return tbc1, tbc2, trbc

                tbc1, tbc2, trbc = ln_rows(sq_pe[-1], prev_iv)
                if pha < 2:
                    barrier = {s: p.ck(s) for s in ("PE", "V", "S", "dG")}
                    continue

                for c in range(KC):
                    reg = pb[2] if c % 2 == 0 else pb[3]
                    p.emit("vector", lambda e, c=c, reg=reg: e.tensor_tensor(
                        reg[:, 0:NTOK], xc(c), pb[0][:, 0:NTOK], ALU.subtract),
                        waits=[tbc2], inc="V")
                    p.emit("vector", lambda e, reg=reg: e.tensor_tensor(
                        reg[:, NTOK:2 * NTOK], reg[:, 0:NTOK], rbc_sb[:, :],
                        ALU.mult), waits=[trbc], inc="V")
                    p.emit("vector", lambda e, c=c, reg=reg: e.tensor_scalar(
                        xnhi[:, c, :], reg[:, NTOK:2 * NTOK], gam[:, c:c + 1],
                        bet[:, c:c + 1], ALU.mult, ALU.add), waits=[], inc="V")
                    p.emit("vector", lambda e, c=c, reg=reg: e.tensor_scalar(
                        reg[:, 0:NTOK], reg[:, NTOK:2 * NTOK], gam[:, c:c + 1],
                        bet[:, c:c + 1], ALU.mult, ALU.add), waits=[], inc="V")
                    p.emit("vector", lambda e, c=c, reg=reg: e.tensor_tensor(
                        xnlo[:, c, :], reg[:, 0:NTOK], xnhi[:, c, :], ALU.subtract),
                        waits=[], inc="V")
                xn_done = p.ck("V")

                if pha < 3:
                    barrier = {s: p.ck(s) for s in ("PE", "V", "S", "dG")}
                    continue
                regions = [pb[2][:, 0:NTOK], pb[3][:, 0:NTOK],
                           pb[4][:, 0:NTOK]]
                regions2 = [pb[2][:, NTOK:2 * NTOK], pb[3][:, NTOK:2 * NTOK],
                            pb[4][:, NTOK:2 * NTOK]]
                reg_free = [None, None, None]
                reg_i = [0]

                def next_region():
                    i = reg_i[0] % 3
                    reg_i[0] += 1
                    return i, regions[i]

                def proj3(reg, stat_hi, stat_lo, mov_hi, mov_lo, first_waits):
                    last = None
                    for k in range(KC):
                        fst = (k == 0)
                        lst = (k == KC - 1)
                        p.emit("tensor", lambda e, k=k, reg=reg, sh=stat_hi,
                               mh=mov_hi, fst=fst: e.matmul(
                            reg, sh(k), mh(k), start=fst, stop=False),
                            waits=first_waits if fst else [], inc=None)
                        p.emit("tensor", lambda e, k=k, reg=reg, sh=stat_hi,
                               ml=mov_lo: e.matmul(
                            reg, sh(k), ml(k), start=False, stop=False),
                            waits=[], inc=None)
                        last = p.emit("tensor", lambda e, k=k, reg=reg, sl=stat_lo,
                                      mh=mov_hi, lst=lst: e.matmul(
                            reg, sl(k), mh(k), start=False, stop=lst),
                            waits=[], inc="PE" if lst else None)
                    return last

                m_v = t * 5 + 0
                wv_par, wv_tok = m_v % 2, w_ready[m_v]
                vt_ev = []
                for e2 in range(2):
                    for hf in range(2):
                        i = (e2 * 2 + hf) % 2
                        regfull = pb[4] if i == 0 else pb[5]
                        last = proj3(
                            regfull[:, 0:512],
                            lambda k, e2=e2: xnhi[:, k, e2 * 128:(e2 + 1) * 128],
                            lambda k, e2=e2: xnlo[:, k, e2 * 128:(e2 + 1) * 128],
                            lambda k, wv_par=wv_par, hf=hf:
                                wbh[:, wv_par, k, hf * 512:(hf + 1) * 512],
                            lambda k, wv_par=wv_par, hf=hf:
                                wbl[:, wv_par, k, hf * 512:(hf + 1) * 512],
                            [wv_tok, xn_done, wo_v_ck,
                             (vt_ev[-2] if len(vt_ev) >= 2 else None)])
                        vt_ev.append(p.emit("vector",
                            lambda e, e2=e2, hf=hf, regfull=regfull:
                            e.tensor_tensor(
                                vtm[:, e2, hf * 512:(hf + 1) * 512],
                                regfull[:, 0:512], bvbc[:, hf * 512:(hf + 1) * 512],
                                ALU.add), waits=[last, t_bvbc], inc="V"))
                vtm_done = p.ck("V")
                w_free[m_v] = p.ck("PE")
                prefetch_w(m_v + 2)

                if pha < 4:
                    barrier = {s: p.ck(s) for s in ("PE", "V", "S", "dG")}
                    continue
                reg_free[0] = reg_free[1] = xn_done
                reg_free[2] = vt_ev[3]
                reg_i[0] = 0
                ev_toks = {"q": [], "k": [], "g": []}
                for j, (nm, wn) in enumerate((("q", 1), ("k", 2), ("g", 3))):
                    m = t * 5 + wn
                    wpar, wtok = m % 2, w_ready[m]
                    for c in range(KC):
                        if nm == "q" and (t - 1) in defer_ctx:
                            if c == 0:
                                emit_deferred(t - 1, piece=0)
                            elif c in (2, 4, 6):
                                emit_deferred(t - 1, piece=c // 2)
                        ri, reg = next_region()
                        last = proj3(
                            reg,
                            lambda k, wpar=wpar, c=c:
                                wbh[:, wpar, k, c * 128:(c + 1) * 128],
                            lambda k, wpar=wpar, c=c:
                                wbl[:, wpar, k, c * 128:(c + 1) * 128],
                            lambda k: xnhi[:, k, :],
                            lambda k: xnlo[:, k, :],
                            [wtok, xn_done, reg_free[ri]])
                        if nm == "q":
                            for bb in range(BC):
                                tok = p.emit("vector", lambda e, c=c, reg=reg,
                                             bb=bb: e.scalar_tensor_tensor(
                                    qf[:, c, bb * 64:(bb + 1) * 64],
                                    reg[:, bb * 64:(bb + 1) * 64],
                                    bias["bq"][:, c:c + 1], aq[:, c, :],
                                    ALU.add, ALU.mult),
                                    waits=[last, tok_atiles], inc="V")
                        elif nm == "k":
                            for bb in range(BC):
                                tok = p.emit("vector", lambda e, c=c, reg=reg,
                                             bb=bb: e.scalar_tensor_tensor(
                                    kf[:, c, bb * 64:(bb + 1) * 64],
                                    reg[:, bb * 64:(bb + 1) * 64],
                                    bias["bk"][:, c:c + 1], ak[:, c, :],
                                    ALU.add, ALU.mult),
                                    waits=[last], inc="V")
                        else:
                            tok = p.emit("scalar", lambda e, c=c, reg=reg:
                                         e.activation(
                                gs[:, c, :], reg, ACT.Silu,
                                bias=bias["bg"][:, c:c + 1]),
                                waits=[last, prev_tlo], inc="S")
                        ev_toks[nm].append(tok)
                        reg_free[ri] = tok
                    w_free[m] = p.ck("PE")
                    prefetch_w(m + 2)
                    pass

                if pha < 5:
                    barrier = {s: p.ck(s) for s in ("PE", "V", "S", "dG")}
                    continue
                tr_banks = [pb[0], pb[6], pb[1], pb[7]]
                ktr_ev = []
                nidx = 0
                for c in range(KC):
                    for b4 in range(BC):
                        for hq in range(2):
                            bank = tr_banks[nidx % 4]
                            ttr = p.emit("tensor",
                                lambda e, c=c, b4=b4, hq=hq, bank=bank:
                                e.transpose(
                                    bank[0:64, NTOK:NTOK + 64],
                                    kf[hq * 64:hq * 64 + 64, c, b4 * 64:(b4 + 1) * 64],
                                    ident[hq * 64:hq * 64 + 64, :],
                                    tile_position=(hq * 64, 0)),
                                waits=[ev_toks["k"][c],
                                       (ktr_ev[-4] if len(ktr_ev) >= 4 else None)],
                                inc="PE")
                            ktr_ev.append(p.emit("vector",
                                lambda e, c=c, b4=b4, hq=hq, bank=bank:
                                e.tensor_copy(
                                    ktm[(b4 % 2) * 64:(b4 % 2) * 64 + 64, b4 // 2,
                                        c * 128 + hq * 64:c * 128 + hq * 64 + 64],
                                    bank[0:64, NTOK:NTOK + 64]),
                                waits=[ttr], inc="V"))
                            nidx += 1
                ktm_done = p.ck("V")

                if pha < 6:
                    barrier = {s: p.ck(s) for s in ("PE", "V", "S", "dG")}
                    continue
                prev_scan_v = None
                og_done = None
                tlo_last = None
                for c in range(KC):
                    pmm = None
                    for b4 in range(BC):
                        for hq in range(2):
                            bank = pb[6] if hq == 0 else pb[7]
                            pmm = p.emit("tensor", lambda e, c=c, b4=b4, hq=hq,
                                         bank=bank: e.matmul(
                                bank[(b4 % 2) * 64:(b4 % 2) * 64 + 64,
                                     (b4 // 2) * 64:(b4 // 2) * 64 + 64],
                                kf[hq * 64:hq * 64 + 64, c, b4 * 64:(b4 + 1) * 64],
                                qf[hq * 64:hq * 64 + 64, c, b4 * 64:(b4 + 1) * 64],
                                start=True, stop=True,
                                tile_position=(hq * 64, (b4 % 2) * 64)),
                                waits=[ev_toks["q"][c], ev_toks["k"][c],
                                       prev_scan_v, ktm_done], inc="PE")
                    pmkv = pmk[:, :].rearrange("p (a h t) -> p a h t", a=2, h=2)
                    maskv = mask[:, 0:128].rearrange("p (a t) -> p a t", a=2)
                    tpm0 = p.emit("vector", lambda e, pmkv=pmkv, maskv=maskv:
                                  e.tensor_tensor(
                        pmkv[:, :, 0, :],
                        pb[6][:, 0:128].rearrange("p (a t) -> p a t", a=2),
                        maskv, ALU.mult), waits=[pmm], inc="V")
                    tpm = p.emit("vector", lambda e, pmkv=pmkv, maskv=maskv:
                                 e.tensor_tensor(
                        pmkv[:, :, 1, :],
                        pb[7][:, 0:128].rearrange("p (a t) -> p a t", a=2),
                        maskv, ALU.mult), waits=[], inc="V")
                    omm = None
                    for b4 in range(BC):
                        for hq in range(2):
                            p.emit("tensor", lambda e, c=c, b4=b4, hq=hq: e.matmul(
                                pb[2][(b4 % 2) * 64:(b4 % 2) * 64 + 64,
                                      (b4 // 2) * 128 + hq * 64:
                                      (b4 // 2) * 128 + hq * 64 + 64],
                                vtm[(b4 % 2) * 64:(b4 % 2) * 64 + 64, b4 // 2,
                                    c * 128 + hq * 64:c * 128 + hq * 64 + 64],
                                pmk[(b4 % 2) * 64:(b4 % 2) * 64 + 64,
                                    (b4 // 2) * 128 + hq * 64:
                                    (b4 // 2) * 128 + hq * 64 + 64],
                                start=True, stop=True,
                                tile_position=((b4 % 2) * 64, (b4 % 2) * 64)),
                                waits=[tpm, vtm_done, og_done], inc=None)
                            omm = p.emit("tensor", lambda e, c=c, b4=b4, hq=hq,
                                         par=par: e.matmul(
                                pb[3][hq * 64:hq * 64 + 64, b4 * 64:(b4 + 1) * 64],
                                hp[par][hq * 64:hq * 64 + 64, c,
                                        b4 * 64:(b4 + 1) * 64],
                                qf[hq * 64:hq * 64 + 64, c, b4 * 64:(b4 + 1) * 64],
                                start=True, stop=True,
                                tile_position=(hq * 64, hq * 64)),
                                waits=[th0 if t == 0 else None], inc="PE")
                    smm = None
                    for b4 in range(BC):
                        for hq in range(2):
                            smm = p.emit("tensor", lambda e, c=c, b4=b4, hq=hq:
                                         e.matmul(
                                pb[4][(b4 % 2) * 64:(b4 % 2) * 64 + 64,
                                      (b4 // 2) * 128 + hq * 64:
                                      (b4 // 2) * 128 + hq * 64 + 64],
                                ktm[(b4 % 2) * 64:(b4 % 2) * 64 + 64, b4 // 2,
                                    c * 128 + hq * 64:c * 128 + hq * 64 + 64],
                                vtm[(b4 % 2) * 64:(b4 % 2) * 64 + 64, b4 // 2,
                                    c * 128 + hq * 64:c * 128 + hq * 64 + 64],
                                start=True, stop=True,
                                tile_position=((b4 % 2) * 64, (b4 % 2) * 64)),
                                waits=[], inc="PE")
                    cp = None
                    for b4 in range(BC):
                        for hq in range(2):
                            cp = p.emit("vector", lambda e, b4=b4, hq=hq:
                                        e.tensor_copy(
                                o1sb[hq * 64:hq * 64 + 64, b4 * 64:(b4 + 1) * 64],
                                pb[2][(b4 % 2) * 64:(b4 % 2) * 64 + 64,
                                      (b4 // 2) * 128 + hq * 64:
                                      (b4 // 2) * 128 + hq * 64 + 64]),
                                waits=[omm], inc="V")
                    tadd = p.emit("vector", lambda e: e.tensor_tensor(
                        pb[3][:, NTOK:2 * NTOK], o1sb[:, :], pb[3][:, 0:NTOK],
                        ALU.add), waits=[], inc="V")
                    tog = p.emit("vector", lambda e, c=c: e.tensor_tensor(
                        pb[5][:, 0:NTOK], pb[3][:, NTOK:2 * NTOK], gs[:, c, :],
                        ALU.mult), waits=[ev_toks["g"][c]], inc="V")
                    p.emit("vector", lambda e, c=c: e.tensor_copy(
                        ohi[:, c, :], pb[5][:, 0:NTOK]), waits=[], inc="V")
                    tlo_last = p.emit("vector", lambda e, c=c: e.tensor_tensor(
                        olo[:, c, :], pb[5][:, 0:NTOK], ohi[:, c, :],
                        ALU.subtract), waits=[], inc="V")
                    og_done = tlo_last
                    for b4 in range(BC):
                        for hq in range(2):
                            p.emit("vector", lambda e, b4=b4, hq=hq: e.tensor_copy(
                                srm[hq * 64:hq * 64 + 64, b4 * 64:(b4 + 1) * 64],
                                pb[4][(b4 % 2) * 64:(b4 % 2) * 64 + 64,
                                      (b4 // 2) * 128 + hq * 64:
                                      (b4 // 2) * 128 + hq * 64 + 64]),
                                waits=[smm], inc="V")
                    p.emit("vector", lambda e, c=c, par=par: e.tensor_tensor(
                        pb[4][:, NTOK:2 * NTOK], hp[par][:, c, :], srm[:, :],
                        ALU.add), waits=[], inc="V")
                    prev_scan_v = p.emit("vector", lambda e, c=c, par=par:
                                         e.tensor_scalar(
                        hp[1 - par][:, c, :], pb[4][:, NTOK:2 * NTOK],
                        a64[:, c:c + 1], None, ALU.mult), waits=[], inc="V")
                scan_v_done = p.ck("V")

                m_o = t * 5 + 4
                wo_par, wo_tok = m_o % 2, w_ready[m_o]
                s_toks = []
                for mc in range(KC):
                    ri, reg = next_region()
                    reg2 = regions2[ri]
                    last = proj3(
                        reg,
                        lambda k, wo_par=wo_par, mc=mc:
                            wbh[:, wo_par, k, mc * 128:(mc + 1) * 128],
                        lambda k, wo_par=wo_par, mc=mc:
                            wbl[:, wo_par, k, mc * 128:(mc + 1) * 128],
                        lambda k: ohi[:, k, :],
                        lambda k: olo[:, k, :],
                        [wo_tok, tlo_last, reg_free[ri], scan_v_done])
                    tb = p.emit("vector", lambda e, mc=mc, reg=reg, reg2=reg2:
                                e.tensor_scalar(
                        reg2, reg, bias["bo"][:, mc:mc + 1], None, ALU.add),
                        waits=[last], inc="V")
                    stok = p.emit("vector", lambda e, mc=mc, reg2=reg2:
                                  e.scalar_tensor_tensor(
                        ssb[:, mc, :], xc(mc), 2.0, reg2, ALU.mult, ALU.add),
                        waits=[prev_S if mc == 0 else None], inc="V")
                    s_toks.append(stok)
                    reg_free[ri] = stok
                x_free[t] = s_toks[-1]
                wo_v_ck = p.ck("V")
                w_free[m_o] = p.ck("PE")
                prefetch_w(m_o + 2)
                prefetch_x(t + 1)

                if pha < 8:
                    barrier = {s: p.ck(s) for s in ("PE", "V", "S", "dG")}
                    continue
                sq_pe2 = []
                mm2_last = None
                for c in range(KC):
                    tsqr = p.emit("scalar", lambda e, c=c: e.activation(
                        sq[:, c % 2, :], ssb[:, c, :], ACT.Square),
                        waits=[s_toks[c], (sq_pe2[c - 2] if c >= 2 else None)],
                        inc="S")
                    mm2_last = p.emit("tensor", lambda e, c=c: e.matmul(
                        pb[0][0:1, 0:NTOK], ones_c[:, :], ssb[:, c, :],
                        start=(c == 0), stop=(c == KC - 1), skip_group_check=True),
                        waits=[s_toks[c]], inc="PE" if c == KC - 1 else None)
                    sq_pe2.append(p.emit("tensor", lambda e, c=c: e.matmul(
                        pb[7][0:1, 0:NTOK], ones_c[:, :], sq[:, c % 2, :],
                        start=(c == 0), stop=(c == KC - 1), skip_group_check=True), waits=[tsqr],
                        inc="PE"))
                tbc1, tbc2, trbc = ln_rows(sq_pe2[-1], prev_iv)

                x2_toks = []
                for c in range(KC):
                    reg = pb[2] if c % 2 == 0 else pb[3]
                    p.emit("vector", lambda e, c=c, reg=reg: e.tensor_tensor(
                        reg[:, 0:NTOK], ssb[:, c, :], pb[0][:, 0:NTOK],
                        ALU.subtract), waits=[tbc2], inc="V")
                    x2_toks.append(p.emit("vector", lambda e, c=c, reg=reg,
                                          par=par: e.tensor_tensor(
                        x2sb[:, par, c, :], reg[:, 0:NTOK], rbc_sb[:, :],
                        ALU.mult), waits=[trbc, prev_store], inc="V"))
                pb01_free = p.ck("V")
                defer_ctx[t] = dict(par=par, x2_toks=list(x2_toks))
                prev_sqpe = sq_pe2[-1]
                prev_S = p.ck("S")
                prev_tlo = tlo_last

                if t == 0 and dump:
                    taps = [("xnhi", xnhi, F16), ("xnlo", xnlo, BF16),
                            ("qf", qf, F32), ("kf", kf, F32), ("gs", gs, F32),
                            ("ohi", ohi, F16), ("olo", olo, BF16),
                            ("ssb", ssb, F32), ("x2sb", x2sb[:, 0], F32),
                            ("hp1", hp[1], F32)]
                    vck = p.ck("V")
                    for nm, tile, dt in taps:
                        dp = dump_out(nm, [128, KC, NTOK], dt)
                        if dp is not None:
                            p.emit("gpsimd", lambda e, dp=dp, tile=tile:
                                   e.dma_start(out=dp[:, :, :], in_=tile[:, :, :]),
                                   waits=[vck], inc="dG", inc_n=16)
                    for nm, tile in (("aq", aq), ("ak", ak)):
                        dp = dump_out(nm, [128, KC, 64])
                        if dp is not None:
                            p.emit("gpsimd", lambda e, dp=dp, tile=tile:
                                   e.dma_start(out=dp[:, :, :], in_=tile[:, :, :]),
                                   waits=[vck], inc="dG", inc_n=16)
                    for nm, tile in (("vtm", vtm), ("ktm", ktm)):
                        dp = dump_out(nm, [128, 2, DI])
                        if dp is not None:
                            p.emit("gpsimd", lambda e, dp=dp, tile=tile:
                                   e.dma_start(out=dp[:, :, :], in_=tile[:, :, :]),
                                   waits=[vck], inc="dG", inc_n=16)
                    p.emit("vector", lambda e: e.engine_nop(),
                           waits=[p.ck("dG")], inc="V")

                barrier = {s: p.ck(s) for s in ("PE", "V", "S", "dG")}

            if (nta - 1) in defer_ctx:
                emit_deferred(nta - 1)
            passA_dG = p.ck("dG")
            passA_V = p.ck("V")

        with contextlib.ExitStack() as esb:
            def sbb(name, shape, dt=F32):
                return esb.enter_context(nc.sbuf_tensor(name, shape, dt))

            tg_sb = sbb("tg_sb", [128, KC, FF], FP8)
            tu_sb = sbb("tu_sb", [128, KC, FF], FP8)
            td_sb = sbb("td_sb", [128, OC, D], FP8)
            i1b = sbb("i1b", [128, 1, KC, NTOK], BF16)
            x2b = sbb("x2b", [128, 1, KC, NTOK])
            iv1r = sbb("iv1r", [1, 1, NTOK])
            iv1bc = sbb("iv1bc", [128, NTOK])
            iv2bc = sbb("iv2bc", [128, NTOK])
            g1 = sbb("g1", [128, 2, NTOK])
            gate = sbb("gate", [128, 2, NTOK])
            hsb = sbb("hsb", [128, OC, NTOK])
            i2sb = sbb("i2sb", [128, OC, NTOK], BF16)
            s3sb = sbb("s3sb", [128, KC, NTOK])
            outsb = sbb("outsb", [128, KC, NTOK])
            accpB = sbb("accpB", [128, NTOK])
            accnB = sbb("accnB", [128, NTOK])
            amB = sbb("amB", [128, NTOK])
            amfB = sbb("amfB", [64, NTOK])
            rbcB = sbb("rbcB", [128, NTOK])
            bsbB = sbb("bsbB", [128, 512])
            ufinB = sbb("ufinB", [128, 512])
            sqB = sbb("sqB", [128, 2, NTOK])

            passA_PE = p.ck("PE")
            passA_S = p.ck("S")
            for dst2, src2 in [(tg_sb[:, :, :], tg_s[:, :, :]),
                               (tu_sb[:, :, :], tu_s[:, :, :]),
                               (td_sb[:, :, :], td_s[:, :, :])]:
                dma("sync", dst2, src2, waits=[passA_dG, passA_PE, passA_V,
                                               passA_S])
            tT = p.ck("dS")

            xb_free = {}
            xb_ready = {}

            def prefetch_b(t):
                if t >= ntb or t in xb_ready:
                    return
                p.emit("sync", lambda e, t=t: e.dma_start(
                    out=i1b[:, 0, :, :], in_=i1_s[t, :, :, :]),
                    waits=[xb_free.get(t - 1), passA_dG, passA_PE, passA_S,
                           passA_V], inc="dS", inc_n=16)
                p.emit("sync", lambda e, t=t: e.dma_start(
                    out=x2b[:, 0, :, :], in_=x2_s[t, :, :, :]),
                    waits=[], inc="dS", inc_n=16)
                xb_ready[t] = p.emit("sync", lambda e, t=t: e.dma_start(
                    out=iv1r[:, 0, :], in_=iv1_s[t, :, :]),
                    waits=[], inc="dS", inc_n=16)

            prefetch_b(0)
            barB_V = passA_V
            prev_out = None

            for t in range(ntb):
                par = t % 2
                tin = xb_ready[t]
                tb = p.emit("tensor", lambda e, par=par: e.matmul(
                    pb[1][:, 0:NTOK], ones_r[:, :], iv1r[0:1, 0, :],
                    start=True, stop=True), waits=[tin, tT, barB_V], inc="PE")
                tivc = p.emit("vector", lambda e: e.tensor_copy(
                    iv1bc[:, :], pb[1][:, 0:NTOK]), waits=[tb, passA_dG], inc="V")

                h_last = None
                h_toks = []
                for oc in range(OC):
                    rg = pb[2 + (oc % 2) * 2][:, 0:NTOK]
                    ru = pb[3 + (oc % 2) * 2]
                    lastg = None
                    lastu = None
                    for k in range(KC):
                        fst = (k == 0)
                        lst = (k == KC - 1)
                        lastg = p.emit("tensor", lambda e, k=k, oc=oc, rg=rg,
                                       par=par, fst=fst, lst=lst: e.matmul(
                            rg, tg_sb[:, k, oc * 128:(oc + 1) * 128],
                            i1b[:, 0, k, :], start=fst, stop=lst),
                            waits=([tin, (h_toks[oc - 2] if oc >= 2 else None)]
                                   if fst else []), inc="PE" if lst else None)
                    for k in range(KC):
                        fst = (k == 0)
                        lst = (k == KC - 1)
                        lastu = p.emit("tensor", lambda e, k=k, oc=oc, ru=ru,
                                       par=par, fst=fst, lst=lst: e.matmul(
                            ru[:, 0:NTOK], tu_sb[:, k, oc * 128:(oc + 1) * 128],
                            i1b[:, 0, k, :], start=fst, stop=lst),
                            waits=[], inc="PE" if lst else None)
                    tg1 = p.emit("vector", lambda e, oc=oc, rg=rg:
                                 e.scalar_tensor_tensor(
                        g1[:, oc % 2, :], rg, mghat[:, 0:1], iv1bc[:, :],
                        ALU.mult, ALU.mult), waits=[lastg, tivc], inc="V")
                    tgate = p.emit("scalar", lambda e, oc=oc: e.activation(
                        gate[:, oc % 2, :], g1[:, oc % 2, :], ACT.Silu),
                        waits=[tg1], inc="S")
                    tu1 = p.emit("vector", lambda e, oc=oc, ru=ru:
                                 e.scalar_tensor_tensor(
                        ru[:, NTOK:2 * NTOK], ru[:, 0:NTOK], mghat[:, 1:2],
                        iv1bc[:, :], ALU.mult, ALU.mult), waits=[lastu], inc="V")
                    th = p.emit("vector", lambda e, oc=oc, ru=ru: e.tensor_tensor(
                        pb[6][:, (oc % 2) * NTOK:(oc % 2 + 1) * NTOK],
                        gate[:, oc % 2, :], ru[:, NTOK:2 * NTOK], ALU.mult),
                        waits=[tgate, tu1], inc="V")
                    p.emit("vector", lambda e, oc=oc: e.tensor_copy(
                        hsb[:, oc, :],
                        pb[6][:, (oc % 2) * NTOK:(oc % 2 + 1) * NTOK]),
                        waits=[], inc="V")
                    if oc == 0:
                        p.emit("vector", lambda e: e.tensor_copy(
                            accpB[:, :], pb[6][:, 0:NTOK]), waits=[], inc="V")
                        h_last = p.emit("vector", lambda e: e.tensor_copy(
                            accnB[:, :], pb[6][:, 0:NTOK]), waits=[], inc="V")
                    else:
                        p.emit("vector", lambda e, oc=oc: e.tensor_tensor(
                            accpB[:, :], accpB[:, :],
                            pb[6][:, (oc % 2) * NTOK:(oc % 2 + 1) * NTOK],
                            ALU.max), waits=[], inc="V")
                        h_last = p.emit("vector", lambda e, oc=oc: e.tensor_tensor(
                            accnB[:, :], accnB[:, :],
                            pb[6][:, (oc % 2) * NTOK:(oc % 2 + 1) * NTOK],
                            ALU.min), waits=[], inc="V")
                    h_toks.append(h_last)

                p.emit("vector", lambda e: e.scalar_tensor_tensor(
                    amB[:, :], accnB[:, :], -1.0, accpB[:, :], ALU.mult, ALU.max),
                    waits=[h_last], inc="V")
                for half in (64, 32):
                    p.emit("vector", lambda e, h=half: e.tensor_copy(
                        amfB[0:h, :], amB[h:2 * h, :]), waits=[], inc="V")
                    p.emit("vector", lambda e, h=half: e.tensor_tensor(
                        amB[0:h, :], amB[0:h, :], amfB[0:h, :], ALU.max),
                        waits=[], inc="V")
                for half in (16, 8, 4, 2, 1):
                    msk = [(i + half) % 32 for i in range(32)]
                    p.emit("vector", lambda e, m=msk: e.stream_shuffle(
                        amfB[0:32, :], amB[0:32, :], m), waits=[], inc="V")
                    p.emit("vector", lambda e, h=half: e.tensor_tensor(
                        amB[0:h, :], amB[0:h, :], amfB[0:h, :], ALU.max),
                        waits=[], inc="V")
                p.emit("vector", lambda e: e.tensor_scalar(
                    rows[0:1, 0:NTOK], amB[0:1, :], 1e-5, None, ALU.max),
                    waits=[], inc="V")
                p.emit("vector", lambda e: e.reciprocal(
                    rows[0:1, NTOK:2 * NTOK], rows[0:1, 0:NTOK]), waits=[], inc="V")
                ts2 = p.emit("vector", lambda e: e.tensor_scalar(
                    rows2[0:1, 0:NTOK], rows[0:1, NTOK:2 * NTOK], 127.0, None,
                    ALU.mult), waits=[], inc="V")
                tiv2 = p.emit("vector", lambda e: e.tensor_scalar(
                    rows2[0:1, NTOK:2 * NTOK], rows[0:1, 0:NTOK], 1.0 / 127.0,
                    None, ALU.mult), waits=[], inc="V")
                tb1 = p.emit("tensor", lambda e: e.matmul(
                    pb[1][:, 0:NTOK], ones_r[:, :], rows2[0:1, 0:NTOK],
                    start=True, stop=True), waits=[ts2], inc="PE")
                tb2 = p.emit("tensor", lambda e: e.matmul(
                    pb[1][:, NTOK:2 * NTOK], ones_r[:, :],
                    rows2[0:1, NTOK:2 * NTOK], start=True, stop=True),
                    waits=[tiv2], inc="PE")
                tiv2c = p.emit("vector", lambda e: e.tensor_copy(
                    iv2bc[:, :], pb[1][:, NTOK:2 * NTOK]), waits=[tb2, tb1],
                    inc="V")

                ti2 = None
                for cc in range(0, OC, 2):
                    u0v = pb[2][:, 0:512]
                    pgA = pb[3][:, 0:512]
                    pgB = pb[6][:, 0:512]
                    p.emit("vector", lambda e, cc=cc: e.tensor_tensor(
                        pb[2][:, 0:NTOK], hsb[:, cc, :], pb[1][:, 0:NTOK],
                        ALU.mult), waits=[tiv2c], inc="V")
                    p.emit("vector", lambda e, cc=cc: e.tensor_tensor(
                        pb[2][:, NTOK:2 * NTOK], hsb[:, cc + 1, :], pb[1][:, 0:NTOK],
                        ALU.mult), waits=[], inc="V")
                    p.emit("vector", lambda e, u0v=u0v: e.tensor_scalar(
                        u0v, u0v, 128.0, None, ALU.add),
                        waits=[], inc="V")
                    srcv = u0v
                    for ki, k in enumerate(range(7, -1, -1)):
                        cth = float(2 ** k) - 0.5
                        p.emit("vector", lambda e, srcv=srcv, cth=cth, k=k:
                               e.tensor_scalar(
                            bsbB[:, :], srcv, cth, -float(2 ** k),
                            ALU.is_ge, ALU.mult), waits=[], inc="V")
                        dstv = ufinB[:, :] if k == 0 else (pgA if ki % 2 == 0 else pgB)
                        p.emit("vector", lambda e, srcv=srcv, dstv=dstv:
                               e.tensor_tensor(
                            dstv, bsbB[:, :], srcv, ALU.add), waits=[], inc="V")
                        srcv = dstv
                    p.emit("vector", lambda e, u0v=u0v: e.scalar_tensor_tensor(
                        bsbB[:, :], ufinB[:, :], -1.0, u0v, ALU.mult, ALU.add),
                        waits=[], inc="V")
                    p.emit("vector", lambda e, cc=cc: e.tensor_scalar(
                        i2sb[:, cc, :], bsbB[:, 0:NTOK], -128.0, None, ALU.add),
                        waits=[], inc="V")
                    ti2 = p.emit("vector", lambda e, cc=cc: e.tensor_scalar(
                        i2sb[:, cc + 1, :], bsbB[:, NTOK:2 * NTOK], -128.0, None,
                        ALU.add), waits=[], inc="V")

                s3_toks = []
                for mc in range(KC):
                    reg = pb[4][:, 0:NTOK] if mc % 2 == 0 else pb[5][:, 0:NTOK]
                    reg2 = (pb[4][:, NTOK:2 * NTOK] if mc % 2 == 0
                            else pb[5][:, NTOK:2 * NTOK])
                    last = None
                    for k in range(OC):
                        fst = (k == 0)
                        lst = (k == OC - 1)
                        last = p.emit("tensor", lambda e, k=k, mc=mc, reg=reg,
                                      fst=fst, lst=lst: e.matmul(
                            reg, td_sb[:, k, mc * 128:(mc + 1) * 128],
                            i2sb[:, k, :], start=fst, stop=lst),
                            waits=([ti2, (s3_toks[mc - 2] if mc >= 2 else None)]
                                   if fst else []), inc="PE" if lst else None)
                    p.emit("vector", lambda e, reg=reg, reg2=reg2:
                           e.scalar_tensor_tensor(
                        reg2, reg, mghat[:, 2:3], iv2bc[:, :], ALU.mult,
                        ALU.mult), waits=[last], inc="V")
                    s3_toks.append(p.emit("vector", lambda e, mc=mc, par=par,
                                          reg2=reg2: e.tensor_tensor(
                        s3sb[:, mc, :], x2b[:, 0, mc, :], reg2, ALU.add),
                        waits=[], inc="V"))
                xb_free[t] = s3_toks[-1]
                prefetch_b(t + 1)

                sq_pe3 = []
                mm3_last = None
                for c in range(KC):
                    tsqr = p.emit("scalar", lambda e, c=c: e.activation(
                        sqB[:, c % 2, :], s3sb[:, c, :], ACT.Square),
                        waits=[s3_toks[c], (sq_pe3[c - 2] if c >= 2 else None)],
                        inc="S")
                    mm3_last = p.emit("tensor", lambda e, c=c: e.matmul(
                        pb[0][0:1, 0:NTOK], ones_c[:, :], s3sb[:, c, :],
                        start=(c == 0), stop=(c == KC - 1), skip_group_check=True),
                        waits=[s3_toks[c]], inc="PE" if c == KC - 1 else None)
                    sq_pe3.append(p.emit("tensor", lambda e, c=c: e.matmul(
                        pb[1][0:1, 0:NTOK], ones_c[:, :], sqB[:, c % 2, :],
                        start=(c == 0), stop=(c == KC - 1), skip_group_check=True), waits=[tsqr],
                        inc="PE"))
                tmu = p.emit("vector", lambda e: e.tensor_scalar(
                    rows[0:1, 0:NTOK], pb[0][0:1, 0:NTOK], 1.0 / D, None,
                    ALU.mult), waits=[sq_pe3[-1], mm3_last], inc="V")
                p.emit("vector", lambda e: e.tensor_scalar(
                    rows[0:1, NTOK:2 * NTOK], pb[1][0:1, 0:NTOK], 1.0 / D,
                    None, ALU.mult), waits=[], inc="V")
                p.emit("vector", lambda e: e.tensor_tensor(
                    rows2[0:1, 0:NTOK], rows[0:1, 0:NTOK], rows[0:1, 0:NTOK],
                    ALU.mult), waits=[], inc="V")
                tvar = p.emit("vector", lambda e: e.tensor_tensor(
                    rows2[0:1, NTOK:2 * NTOK], rows[0:1, NTOK:2 * NTOK],
                    rows2[0:1, 0:NTOK], ALU.subtract), waits=[], inc="V")
                tsq = p.emit("scalar", lambda e: e.activation(
                    rows2[0:1, 0:NTOK], rows2[0:1, NTOK:2 * NTOK], ACT.Sqrt,
                    bias=eps1[:, :]), waits=[tvar], inc="S")
                trr = p.emit("vector", lambda e: e.reciprocal(
                    rows[0:1, NTOK:2 * NTOK], rows2[0:1, 0:NTOK]),
                    waits=[tsq], inc="V")
                tbc1 = p.emit("tensor", lambda e: e.matmul(
                    pb[0][:, 0:NTOK], ones_r[:, :], rows[0:1, 0:NTOK],
                    start=True, stop=True), waits=[tmu, trr], inc="PE")
                tbc2 = p.emit("tensor", lambda e: e.matmul(
                    pb[0][:, NTOK:2 * NTOK], ones_r[:, :],
                    rows[0:1, NTOK:2 * NTOK], start=True, stop=True),
                    waits=[trr], inc="PE")
                trbc = p.emit("vector", lambda e: e.tensor_copy(
                    rbcB[:, :], pb[0][:, NTOK:2 * NTOK]), waits=[tbc2], inc="V")
                for c in range(KC):
                    reg = pb[2] if c % 2 == 0 else pb[3]
                    p.emit("vector", lambda e, c=c, reg=reg: e.tensor_tensor(
                        reg[:, 0:NTOK], s3sb[:, c, :], pb[0][:, 0:NTOK],
                        ALU.subtract), waits=[tbc2, prev_out], inc="V")
                    otok = p.emit("vector", lambda e, c=c, reg=reg: e.tensor_tensor(
                        outsb[:, c, :], reg[:, 0:NTOK], rbcB[:, :], ALU.mult),
                        waits=[trbc], inc="V")
                    prev_out = p.emit("gpsimd", lambda e, t=t, c=c: e.dma_start(
                        out=out_t[:, t, :].rearrange("(k p) n -> p k n", p=128)[
                            :, c, :],
                        in_=outsb[:, c, :]), waits=[otok], inc="dG", inc_n=16)
                barB_V = p.ck("V")

            final_dg = p.ck("dG")
            p.emit("gpsimd", lambda e: e.engine_nop(), waits=[final_dg], inc="G")

        block = es.enter_context(nc.Block())
        p.build(block)

    return nc, dumps


def _host_inputs(core, L0_attn_x, L0_attn_WQ, L0_attn_bQ, L0_attn_WK, L0_attn_bK,
                 L0_attn_WV, L0_attn_bV, L0_attn_WO, L0_attn_bO, L0_attn_Wg,
                 L0_attn_bg, L0_attn_alpha_logit, L0_attn_ln_gamma,
                 L0_attn_ln_beta, L0_ffn_w_gate, L0_ffn_w_up, L0_ffn_w_down):
    xc = np.asarray(L0_attn_x)[4 * core:4 * core + 4]
    x_t = np.ascontiguousarray(
        xc.transpose(2, 1, 0).reshape(D, NT, C, BC).transpose(0, 1, 3, 2)
        .reshape(D, NT, NTOK))

    def colform(b):
        return np.ascontiguousarray(np.asarray(b, np.float32).reshape(KC, 128).T)
    a = np.asarray(L0_attn_alpha_logit, np.float32).reshape(KC, 2, 64)
    alpha_c = np.ascontiguousarray(a.transpose(1, 2, 0).reshape(128, KC))
    pp, ff = np.meshgrid(np.arange(128), np.arange(NTOK), indexing="ij")
    mask_t = ((pp % 64) <= (ff % 64)).astype(np.float32)
    return {
        "x_t": x_t,
        "wq_t": np.ascontiguousarray(np.asarray(L0_attn_WQ, np.float32).T),
        "wk_t": np.ascontiguousarray(np.asarray(L0_attn_WK, np.float32).T),
        "wv_t": np.ascontiguousarray(np.asarray(L0_attn_WV, np.float32).T),
        "wg_t": np.ascontiguousarray(np.asarray(L0_attn_Wg, np.float32).T),
        "wo_t": np.ascontiguousarray(np.asarray(L0_attn_WO, np.float32).T),
        "wg1_t": np.ascontiguousarray(np.asarray(L0_ffn_w_gate, np.float32).T),
        "wu1_t": np.ascontiguousarray(np.asarray(L0_ffn_w_up, np.float32).T),
        "wd1_t": np.ascontiguousarray(np.asarray(L0_ffn_w_down, np.float32).T),
        "bq_c": colform(L0_attn_bQ), "bk_c": colform(L0_attn_bK),
        "bv_c": colform(L0_attn_bV), "bg_c": colform(L0_attn_bg),
        "bo_c": colform(L0_attn_bO),
        "gamma_c": colform(L0_attn_ln_gamma), "beta_c": colform(L0_attn_ln_beta),
        "alpha_c": alpha_c,
        "bv_row": np.ascontiguousarray(
            np.asarray(L0_attn_bV, np.float32).reshape(1, DI)),
        "mask_t": mask_t,
        "ones_col": np.ones((128, 1), np.float32),
        "ones_row": np.ones((1, 128), np.float32),
        "ident64": np.eye(64, dtype=np.float32),
    }


def kernel(**inputs):
    nc, _ = build_program()
    in_maps = [_host_inputs(c, **inputs) for c in range(8)]
    res = run_bass_kernel_spmd(nc, in_maps, core_ids=list(range(8)))
    outs = []
    for c in range(8):
        ot = np.asarray(res.results[c]["out_t"])
        oc = (ot.reshape(D, NT, BC, C).transpose(0, 1, 3, 2)
              .reshape(D, T, BC).transpose(2, 1, 0))
        outs.append(oc)
    return np.ascontiguousarray(np.concatenate(outs, axis=0)).astype(np.float32)


# version 11
# speedup vs baseline: 1.0370x; 1.0244x over previous
import contextlib
import numpy as np

import concourse.bass as bass
import concourse.mybir as mybir
from concourse.bass_utils import run_bass_kernel_spmd

F32 = mybir.dt.float32
F16 = mybir.dt.float16
BF16 = mybir.dt.bfloat16
FP8 = mybir.dt.float8e4
ALU = mybir.AluOpType
ACT = mybir.ActivationFunctionType
AX = mybir.AxisListType

D = 1024
DI = 1024
BC = 4
T = 2048
C = 64
NT = T // C
NTOK = BC * C
KC = D // 128
FF = 4096
OC = FF // 128
LN_EPS = 1e-5

ENGINES = ("sync", "tensor", "vector", "scalar", "gpsimd")


class Plan:

    def __init__(self, nc):
        self.nc = nc
        self.ops = {e: [] for e in ENGINES}
        self.cnt = {}
        self.sem_objs = {}
        self.last_wait = {e: {} for e in ENGINES}
        self.own = {"sync": None, "tensor": "PE", "vector": "V",
                    "scalar": "S", "gpsimd": "G"}

    def add_sem(self, name, obj):
        self.sem_objs[name] = obj
        self.cnt[name] = 0

    def emit(self, eng, fn, waits=(), inc=None, inc_n=1):
        rw = []
        for w in waits:
            if w is None:
                continue
            s, v = w
            if v <= 0:
                continue
            assert v <= self.cnt[s], f"wait on future {s}:{v} > {self.cnt[s]}"
            if s == self.own[eng]:
                continue
            if self.last_wait[eng].get(s, -1) >= v:
                continue
            rw.append((s, v))
            self.last_wait[eng][s] = v
        tok = None
        if inc is not None:
            self.cnt[inc] += inc_n
            tok = (inc, self.cnt[inc])
        self.ops[eng].append((fn, rw, (inc, inc_n) if inc else None))
        return tok

    def ck(self, sem):
        return (sem, self.cnt[sem])

    def build(self, block):
        sem_objs = self.sem_objs
        for eng in ENGINES:
            ops = self.ops[eng]
            if not ops:
                continue

            def make(ops):
                def run(e):
                    for fn, waits, inc in ops:
                        for s, v in waits:
                            e.wait_ge(sem_objs[s], v)
                        r = fn(e)
                        if inc is not None:
                            s, n = inc
                            r.then_inc(sem_objs[s], n)
                return run
            getattr(block, eng)(make(ops))


def build_program(dump=(), nta=NT, ntb=NT, pha=99):
    nc = bass.Bass("TRN2", target_bir_lowering=False,
                   detect_race_conditions=False)
    dump = set(dump)

    x_t = nc.dram_tensor("x_t", [D, NT, NTOK], F32, kind="ExternalInput").ap()
    w_in = {}
    for w in ("wq", "wk", "wv", "wg", "wo"):
        w_in[w] = nc.dram_tensor(w + "_t", [D, DI], F32, kind="ExternalInput").ap()
    wg1_t = nc.dram_tensor("wg1_t", [D, FF], F32, kind="ExternalInput").ap()
    wu1_t = nc.dram_tensor("wu1_t", [D, FF], F32, kind="ExternalInput").ap()
    wd1_t = nc.dram_tensor("wd1_t", [FF, D], F32, kind="ExternalInput").ap()
    b_in = {}
    for b in ("bq", "bk", "bv", "bg", "bo"):
        b_in[b] = nc.dram_tensor(b + "_c", [128, KC], F32, kind="ExternalInput").ap()
    gamma_c = nc.dram_tensor("gamma_c", [128, KC], F32, kind="ExternalInput").ap()
    beta_c = nc.dram_tensor("beta_c", [128, KC], F32, kind="ExternalInput").ap()
    alpha_in = nc.dram_tensor("alpha_c", [128, KC], F32, kind="ExternalInput").ap()
    bv_row = nc.dram_tensor("bv_row", [1, DI], F32, kind="ExternalInput").ap()
    mask_in = nc.dram_tensor("mask_t", [128, NTOK], F32, kind="ExternalInput").ap()
    onesc_in = nc.dram_tensor("ones_col", [128, 1], F32, kind="ExternalInput").ap()
    onesr_in = nc.dram_tensor("ones_row", [1, 128], F32, kind="ExternalInput").ap()
    ident_in = nc.dram_tensor("ident64", [64, 64], F32, kind="ExternalInput").ap()
    out_t = nc.dram_tensor("out_t", [D, NT, NTOK], F32, kind="ExternalOutput").ap()

    whi_s = nc.dram_tensor("whi_s", [5, 128, KC, DI], F16, kind="Internal").ap()
    wlo_s = nc.dram_tensor("wlo_s", [5, 128, KC, DI], BF16, kind="Internal").ap()
    tg_s = nc.dram_tensor("tg_s", [128, KC, FF], FP8, kind="Internal").ap()
    tu_s = nc.dram_tensor("tu_s", [128, KC, FF], FP8, kind="Internal").ap()
    td_s = nc.dram_tensor("td_s", [128, OC, D], FP8, kind="Internal").ap()
    i1_s = nc.dram_tensor("i1_s", [NT, 128, KC, NTOK], BF16, kind="Internal").ap()
    x2_s = nc.dram_tensor("x2_s", [NT, 128, KC, NTOK], F32, kind="Internal").ap()
    iv1_s = nc.dram_tensor("iv1_s", [NT, 1, NTOK], F32, kind="Internal").ap()

    dumps = {}
    def dump_out(name, shape, dtype=F32):
        if name in dump and name not in dumps:
            dumps[name] = nc.dram_tensor("dbg_" + name, shape, dtype,
                                         kind="ExternalOutput").ap()
        return dumps.get(name)

    es = contextlib.ExitStack()
    with es:
        p = Plan(nc)
        for s in ("dS", "dG", "PE", "V", "S", "G"):
            p.add_sem(s, es.enter_context(nc.semaphore(s)))

        def sb(name, shape, dt=F32):
            return es.enter_context(nc.sbuf_tensor(name, shape, dt))

        pb = [es.enter_context(nc.psum_tensor("pb%d" % i, [128, 512], F32))
              for i in range(8)]
        bias = {b: sb(b, [128, KC]) for b in ("bq", "bk", "bv", "bg", "bo")}
        gam = sb("gam", [128, KC]); bet = sb("bet", [128, KC])
        alpha = sb("alpha", [128, KC])
        a64 = sb("a64", [128, KC])
        mghat = sb("mghat", [128, 3])
        mask = sb("mask", [128, NTOK])
        ones_c = sb("ones_c", [128, 1])
        ones_r = sb("ones_r", [1, 128])
        ident = sb("ident", [128, 64])
        eps1 = sb("eps1", [1, 1])
        rows = sb("rows", [1, 512])
        rows2 = sb("rows2", [1, 512])
        negc = sb("negc", [128, 8])
        aq = sb("aq", [128, KC, 64])
        ak = sb("ak", [128, KC, 64])
        bvbc = sb("bvbc", [128, DI])

        def dma(eng, out, in_, waits=()):
            return p.emit(eng, lambda e, o=out, i=in_: e.dma_start(out=o, in_=i),
                          waits=waits,
                          inc="dS" if eng in ("sync", "scalar") else "dG", inc_n=16)

        for dst, src in [(mask[:, :], mask_in[:, :]),
                         (ones_c[:, :], onesc_in[:, :]),
                         (ones_r[:, :], onesr_in[:, :]),
                         (ident[0:64, :], ident_in[:, :]),
                         (ident[64:128, :], ident_in[:, :]),
                         (gam[:, :], gamma_c[:, :]),
                         (bet[:, :], beta_c[:, :]),
                         (alpha[:, :], alpha_in[:, :])]:
            dma("sync", dst, src)
        for b in bias:
            dma("sync", bias[b][:, :], b_in[b][:, :])
        t_consts = p.ck("dS")

        p.emit("vector", lambda e: e.memset(eps1[:, :], LN_EPS), waits=[], inc="V")
        for k in range(8):
            cval = -(127.5 - 2.0 ** -17)
            p.emit("vector", lambda e, k=k, c=cval: e.memset(negc[:, k:k + 1], c),
                   waits=[], inc="V")
        tok_alpha = p.emit("scalar", lambda e: e.activation(
            alpha[:, :], alpha[:, :], ACT.Sigmoid), waits=[t_consts], inc="S")

        esp = contextlib.ExitStack()
        def sbp(name, shape, dt=F32):
            return esp.enter_context(nc.sbuf_tensor(name, shape, dt))
        apow = sbp("apow", [128, 64])
        apneg = sbp("apneg", [128, 64])
        acols = sbp("acols", [128, 16])
        def vdrain():
            p.emit("vector", lambda e: e.drain(), waits=[], inc=None)

        for c in range(KC):
            al = alpha[:, c:c + 1]
            def em(fn, waits=()):
                tok = p.emit("vector", fn, waits=waits, inc="V")
                vdrain()
                return tok
            w0 = [tok_alpha]
            em(lambda e, al=al: e.tensor_tensor(acols[:, 0:1], al, al, ALU.mult), w0)
            for j in range(1, 5):
                em(lambda e, j=j: e.tensor_tensor(
                    acols[:, j:j + 1], acols[:, j - 1:j], acols[:, j - 1:j], ALU.mult))
            em(lambda e, c=c: e.tensor_tensor(
                a64[:, c:c + 1], acols[:, 4:5], acols[:, 4:5], ALU.mult))
            em(lambda e, al=al: e.reciprocal(acols[:, 5:6], al), w0)
            for j in range(6, 11):
                em(lambda e, j=j: e.tensor_tensor(
                    acols[:, j:j + 1], acols[:, j - 1:j], acols[:, j - 1:j], ALU.mult))
            em(lambda e: e.memset(apow[:, 0:1], 1.0))
            em(lambda e, al=al: e.tensor_copy(apow[:, 1:2], al), w0)
            for j in range(1, 6):
                n = 2 ** j
                em(lambda e, n=n, j=j: e.tensor_scalar(
                    apow[:, n:2 * n], apow[:, 0:n], acols[:, j - 1:j], None, ALU.mult))
            em(lambda e: e.memset(apneg[:, 0:1], 1.0))
            em(lambda e: e.tensor_copy(apneg[:, 1:2], acols[:, 5:6]))
            for j in range(1, 6):
                n = 2 ** j
                em(lambda e, n=n, j=j: e.tensor_scalar(
                    apneg[:, n:2 * n], apneg[:, 0:n], acols[:, 5 + j:6 + j], None,
                    ALU.mult))
            em(lambda e, c=c: e.tensor_scalar(
                aq[:, c, :], apow[:, :], acols[:, 10:11], None, ALU.mult))
            em(lambda e, c=c: e.tensor_scalar(
                ak[:, c, :], apneg[:, :], acols[:, 4:5], None, ALU.mult))
        tok_atiles = p.ck("V")

        bvrow_sb = sbp("bvrow_sb", [1, DI])
        t_bv = dma("sync", bvrow_sb[:, :], bv_row[:, :])
        t1 = p.emit("tensor", lambda e: e.matmul(
            pb[0][:, 0:512], ones_r[:, :], bvrow_sb[:, 0:512],
            start=True, stop=True), waits=[t_bv, t_consts], inc="PE")
        t2 = p.emit("tensor", lambda e: e.matmul(
            pb[1][:, 0:512], ones_r[:, :], bvrow_sb[:, 512:1024],
            start=True, stop=True), waits=[t_bv], inc="PE")
        p.emit("vector", lambda e: e.tensor_copy(bvbc[:, 0:512], pb[0][:, 0:512]),
               waits=[t1], inc="V")
        t_bvbc = p.emit("vector", lambda e: e.tensor_copy(
            bvbc[:, 512:1024], pb[1][:, 0:512]), waits=[t2], inc="V")

        wtmp = sbp("wtmp", [128, 2, DI])
        whtmp = sbp("whtmp", [128, 2, DI], F16)
        wltmp = sbp("wltmp", [128, 2, DI], BF16)
        sfree = [None, None]
        for wi, w in enumerate(("wq", "wk", "wv", "wg", "wo")):
            for k in range(KC):
                par = (wi * KC + k) % 2
                tin = dma("sync", wtmp[:, par, :], w_in[w][k * 128:(k + 1) * 128, :],
                          waits=[sfree[par]])
                th = p.emit("vector", lambda e, par=par: e.tensor_copy(
                    whtmp[:, par, :], wtmp[:, par, :]), waits=[tin], inc="V")
                tl = p.emit("vector", lambda e, par=par: e.tensor_tensor(
                    wltmp[:, par, :], wtmp[:, par, :], whtmp[:, par, :],
                    ALU.subtract), waits=[], inc="V")
                dma("sync", whi_s[wi, :, k, :], whtmp[:, par, :], waits=[th])
                sfree[par] = dma("sync", wlo_s[wi, :, k, :], wltmp[:, par, :],
                                 waits=[tl])

        ftmp = sbp("ftmp", [128, 2, FF])
        fb1 = sbp("fb1", [128, FF])
        ftern = sbp("ftern", [128, FF], FP8)
        redc = sbp("redc", [128, 4])
        foldt = sbp("foldt", [64, NTOK])
        for wi, (src, dst, kn, cols) in enumerate(
                [(wg1_t, tg_s, KC, FF), (wu1_t, tu_s, KC, FF), (wd1_t, td_s, OC, D)]):
            tz = p.emit("vector", lambda e: e.memset(redc[:, 0:1], 0.0),
                        waits=[], inc="V")
            lin = [None, None]
            for k in range(kn):
                par = k % 2
                tin = dma("sync", ftmp[:, par, 0:cols],
                          src[k * 128:(k + 1) * 128, :], waits=[lin[par]])
                p.emit("vector", lambda e, par=par, cols=cols: e.tensor_reduce(
                    redc[:, 1:2], ftmp[:, par, 0:cols], AX.X, ALU.add,
                    apply_absolute_value=True), waits=[tin, tz], inc="V")
                vdrain()
                lin[par] = p.emit("vector", lambda e: e.tensor_tensor(
                    redc[:, 0:1], redc[:, 0:1], redc[:, 1:2], ALU.add),
                    waits=[], inc="V")
                vdrain()
            for half in (64, 32):
                p.emit("vector", lambda e, h=half: e.tensor_copy(
                    foldt[0:h, 0:1], redc[h:2 * h, 0:1]), waits=[], inc="V")
                vdrain()
                p.emit("vector", lambda e, h=half: e.tensor_tensor(
                    redc[0:h, 0:1], redc[0:h, 0:1], foldt[0:h, 0:1], ALU.add),
                    waits=[], inc="V")
                vdrain()
            for half in (16, 8, 4, 2, 1):
                msk = [(i + half) % 32 for i in range(32)]
                p.emit("vector", lambda e, m=msk: e.stream_shuffle(
                    foldt[0:32, 0:1], redc[0:32, 0:1], m), waits=[], inc="V")
                vdrain()
                p.emit("vector", lambda e, h=half: e.tensor_tensor(
                    redc[0:h, 0:1], redc[0:h, 0:1], foldt[0:h, 0:1], ALU.add),
                    waits=[], inc="V")
                vdrain()
            p.emit("vector", lambda e, n=float(kn * 128 * cols): e.tensor_scalar(
                rows[0:1, 0:1], redc[0:1, 0:1], 1.0 / n, 1e-5, ALU.mult, ALU.max),
                waits=[], inc="V")
            vdrain()
            tsw = p.emit("vector", lambda e: e.reciprocal(
                rows[0:1, 1:2], rows[0:1, 0:1]), waits=[], inc="V")
            vdrain()
            tm1 = p.emit("tensor", lambda e: e.matmul(
                pb[0][:, 0:1], ones_r[:, :], rows[0:1, 0:1], start=True, stop=True),
                waits=[tsw, t_bvbc], inc="PE")
            tm2 = p.emit("tensor", lambda e: e.matmul(
                pb[0][:, 1:2], ones_r[:, :], rows[0:1, 1:2], start=True, stop=True),
                waits=[], inc="PE")
            p.emit("vector", lambda e, wi=wi: e.tensor_copy(
                mghat[:, wi:wi + 1], pb[0][:, 0:1]), waits=[tm1, tm2], inc="V")
            tswc = p.emit("vector", lambda e: e.tensor_copy(
                redc[:, 2:3], pb[0][:, 1:2]), waits=[], inc="V")
            lin = [None, None]
            for k in range(kn):
                par = k % 2
                tin = dma("sync", ftmp[:, par, 0:cols],
                          src[k * 128:(k + 1) * 128, :], waits=[lin[par], tswc])
                p.emit("vector", lambda e, par=par, cols=cols: e.tensor_scalar(
                    fb1[:, 0:cols], ftmp[:, par, 0:cols], redc[:, 2:3], 0.5,
                    ALU.mult, ALU.is_gt), waits=[tin], inc="V")
                p.emit("vector", lambda e, par=par, cols=cols: e.tensor_scalar(
                    ftmp[:, par, 0:cols], ftmp[:, par, 0:cols], redc[:, 2:3], -0.5,
                    ALU.mult, ALU.is_lt), waits=[], inc="V")
                t3 = p.emit("vector", lambda e, par=par, cols=cols: e.tensor_tensor(
                    ftern[:, 0:cols], fb1[:, 0:cols], ftmp[:, par, 0:cols],
                    ALU.subtract), waits=[], inc="V")
                lin[par] = dma("sync", dst[:, k, :], ftern[:, 0:cols], waits=[t3])
        t_prelude_dS = p.ck("dS")
        esp.close()

        barrier = {s: p.ck(s) for s in ("PE", "V", "S", "dG")}

        with contextlib.ExitStack() as esa:
            def sba(name, shape, dt=F32):
                return esa.enter_context(nc.sbuf_tensor(name, shape, dt))

            xsb = sba("xsb", [128, 1, KC, NTOK])
            sq = sba("sq", [128, 2, NTOK])
            xnhi = sba("xnhi", [128, KC, NTOK], F16)
            xnlo = sba("xnlo", [128, KC, NTOK], BF16)
            qf = sba("qf", [128, KC, NTOK])
            kf = sba("kf", [128, KC, NTOK])
            gs = sba("gs", [128, KC, NTOK])
            vtm = sba("vtm", [128, 2, DI])
            ktm = sba("ktm", [128, 2, DI])
            pmk = sba("pmk", [128, NTOK])
            o1sb = sba("o1sb", [128, NTOK])
            srm = sba("srm", [128, NTOK])
            ohi = sba("ohi", [128, KC, NTOK], F16)
            olo = sba("olo", [128, KC, NTOK], BF16)
            ssb = sba("ssb", [128, KC, NTOK])
            x2sb = sba("x2sb", [128, 2, KC, NTOK])
            i1sb = sba("i1sb", [128, KC, NTOK], BF16)
            accp = sba("accp", [128, NTOK])
            accn = sba("accn", [128, NTOK])
            am = sba("am", [128, NTOK])
            amf = sba("amf", [64, NTOK])
            rbc_sb = sba("rbc_sb", [128, NTOK])
            bsb = sba("bsb", [128, 512])
            ufin = sba("ufin", [128, 512])
            hp = [sba("hp0", [128, KC, NTOK]), sba("hp1", [128, KC, NTOK])]
            wbh = sba("wbh", [128, 2, KC, DI], F16)
            wbl = sba("wbl", [128, 2, KC, DI], BF16)

            th0 = p.emit("vector", lambda e: e.memset(hp[0][:, :, :], 0.0),
                         waits=[t_prelude_dS], inc="V")
            p.emit("vector", lambda e: e.memset(hp[1][:, :, :], 0.0),
                   waits=[], inc="V")

            WROW = (2, 0, 1, 3, 4)
            w_free = {}
            w_ready = {}

            def prefetch_w(m):
                if m >= nta * 5 or m in w_ready:
                    return
                par = m % 2
                wi = WROW[m % 5]
                p.emit("sync", lambda e, wi=wi, par=par: e.dma_start(
                    out=wbh[:, par, :, :], in_=whi_s[wi, :, :, :]),
                    waits=[w_free.get(m - 2), t_prelude_dS], inc="dS", inc_n=16)
                w_ready[m] = p.emit("sync", lambda e, wi=wi, par=par: e.dma_start(
                    out=wbl[:, par, :, :], in_=wlo_s[wi, :, :, :]),
                    waits=[], inc="dS", inc_n=16)

            x_free = {}
            x_ready = {}

            def prefetch_x(t):
                if t >= nta or t in x_ready:
                    return
                x_ready[t] = p.emit("sync", lambda e, t=t: e.dma_start(
                    out=xsb[:, 0, :, :],
                    in_=x_t[:, t, :].rearrange("(k p) n -> p k n", p=128),
                    ), waits=[x_free.get(t - 1), t_prelude_dS],
                    inc="dS", inc_n=16)

            prefetch_w(0); prefetch_w(1)
            prefetch_x(0)

            prev_store = None
            defer_ctx = {}

            def emit_deferred(dt, piece=None):
                nonlocal prev_store, ti1_last, prev_iv, prev_st1
                ctx = defer_ctx[dt]
                dpar = ctx["par"]
                x2c = lambda c: x2sb[:, dpar, c, :]
                if piece is not None and piece > 0:
                    tbs1 = ctx["tbs1"]
                    pairs = {1: [0], 2: [2], 3: [4, 6]}[piece]
                    for cc in pairs:
                        emit_ladder_pair(dt, ctx, cc, x2c, tbs1)
                    return
                fold_last = None
                for c in range(KC):
                    if c == 0:
                        p.emit("vector", lambda e, c=c: e.tensor_copy(
                            accp[:, :], x2c(c)), waits=[ctx["x2_toks"][0]],
                            inc="V")
                        fold_last = p.emit("vector", lambda e, c=c: e.tensor_copy(
                            accn[:, :], x2c(c)), waits=[], inc="V")
                    else:
                        p.emit("vector", lambda e, c=c: e.tensor_tensor(
                            accp[:, :], accp[:, :], x2c(c), ALU.max),
                            waits=[ctx["x2_toks"][c]], inc="V")
                        fold_last = p.emit("vector", lambda e, c=c: e.tensor_tensor(
                            accn[:, :], accn[:, :], x2c(c), ALU.min),
                            waits=[], inc="V")
                p.emit("vector", lambda e: e.scalar_tensor_tensor(
                    am[:, :], accn[:, :], -1.0, accp[:, :], ALU.mult, ALU.max),
                    waits=[fold_last], inc="V")
                for half in (64, 32):
                    p.emit("vector", lambda e, h=half: e.tensor_copy(
                        amf[0:h, :], am[h:2 * h, :]), waits=[], inc="V")
                    p.emit("vector", lambda e, h=half: e.tensor_tensor(
                        am[0:h, :], am[0:h, :], amf[0:h, :], ALU.max),
                        waits=[], inc="V")
                for half in (16, 8, 4, 2, 1):
                    msk = [(i + half) % 32 for i in range(32)]
                    p.emit("vector", lambda e, m=msk: e.stream_shuffle(
                        amf[0:32, :], am[0:32, :], m), waits=[], inc="V")
                    p.emit("vector", lambda e, h=half: e.tensor_tensor(
                        am[0:h, :], am[0:h, :], amf[0:h, :], ALU.max),
                        waits=[], inc="V")
                p.emit("vector", lambda e: e.tensor_scalar(
                    rows[0:1, 0:NTOK], am[0:1, :], 1e-5, None, ALU.max),
                    waits=[], inc="V")
                p.emit("vector", lambda e: e.reciprocal(
                    rows[0:1, NTOK:2 * NTOK], rows[0:1, 0:NTOK]),
                    waits=[], inc="V")
                ts1 = p.emit("vector", lambda e: e.tensor_scalar(
                    rows2[0:1, 0:NTOK], rows[0:1, NTOK:2 * NTOK], 127.0, None,
                    ALU.mult), waits=[], inc="V")
                tiv = p.emit("vector", lambda e: e.tensor_scalar(
                    rows2[0:1, NTOK:2 * NTOK], rows[0:1, 0:NTOK], 1.0 / 127.0,
                    None, ALU.mult), waits=[], inc="V")
                tbs1 = p.emit("tensor", lambda e: e.matmul(
                    pb[1][:, 0:NTOK], ones_r[:, :], rows2[0:1, 0:NTOK],
                    start=True, stop=True), waits=[ts1], inc="PE")
                tivd = p.emit("gpsimd", lambda e, dt=dt: e.dma_start(
                    out=iv1_s[dt, :, :], in_=rows2[0:1, NTOK:2 * NTOK]),
                    waits=[tiv], inc="dG", inc_n=16)
                prev_iv = tivd
                ctx["tbs1"] = tbs1
                ctx["fold_last"] = fold_last
                if piece is None:
                    for cc in range(0, KC, 2):
                        emit_ladder_pair(dt, ctx, cc, x2c, tbs1)

            def emit_ladder_pair(dt, ctx, cc, x2c, tbs1):
                nonlocal prev_store, prev_st1, ti1_last
                u0v = pb[5][:, 0:512]
                pgA = pb[6][:, 0:512]
                pgB = pb[7][:, 0:512]
                p.emit("vector", lambda e, cc=cc, u0v=u0v: e.tensor_tensor(
                    u0v[:, 0:NTOK], x2c(cc), pb[1][:, 0:NTOK], ALU.mult),
                    waits=[tbs1, prev_st1 if cc == 0 else None], inc="V")
                p.emit("vector", lambda e, cc=cc, u0v=u0v: e.tensor_tensor(
                    u0v[:, NTOK:2 * NTOK], x2c(cc + 1), pb[1][:, 0:NTOK],
                    ALU.mult), waits=[], inc="V")
                p.emit("vector", lambda e, u0v=u0v: e.tensor_scalar(
                    u0v, u0v, 128.0, None, ALU.add), waits=[], inc="V")
                srcv = u0v
                for ki, k in enumerate(range(7, -1, -1)):
                    cth = float(2 ** k) - 0.5
                    p.emit("vector", lambda e, srcv=srcv, cth=cth, k=k:
                           e.tensor_scalar(
                        bsb[:, :], srcv, cth, -float(2 ** k),
                        ALU.is_ge, ALU.mult), waits=[], inc="V")
                    dstv = ufin[:, :] if k == 0 else (pgA if ki % 2 == 0 else pgB)
                    p.emit("vector", lambda e, srcv=srcv, dstv=dstv:
                           e.tensor_tensor(
                        dstv, bsb[:, :], srcv, ALU.add), waits=[], inc="V")
                    srcv = dstv
                p.emit("vector", lambda e, u0v=u0v: e.scalar_tensor_tensor(
                    bsb[:, :], ufin[:, :], -1.0, u0v, ALU.mult, ALU.add),
                    waits=[], inc="V")
                p.emit("vector", lambda e, cc=cc: e.tensor_scalar(
                    i1sb[:, cc, :], bsb[:, 0:NTOK], -128.0, None, ALU.add),
                    waits=[], inc="V")
                ti1 = p.emit("vector", lambda e, cc=cc: e.tensor_scalar(
                    i1sb[:, cc + 1, :], bsb[:, NTOK:2 * NTOK], -128.0, None,
                    ALU.add), waits=[], inc="V")
                ti1_last = ti1
                if cc == KC - 2:
                    defer_ctx.pop(dt)
                    prev_st1 = p.emit("gpsimd", lambda e, dt=dt: e.dma_start(
                        out=i1_s[dt, :, :, :], in_=i1sb[:, :, :]), waits=[ti1],
                        inc="dG", inc_n=16)
                    prev_store = p.emit("gpsimd", lambda e, dt=dt: e.dma_start(
                        out=x2_s[dt, :, :, :], in_=x2sb[:, ctx["par"], :, :]),
                        waits=[ctx["fold_last"]], inc="dG", inc_n=16)

            ti1_last = None
            prev_iv = None
            prev_st1 = None
            prev_sqpe = None
            pb01_free = None
            prev_S = None
            prev_tlo = None
            wo_v_ck = None
            for t in range(nta):
                par = t % 2
                bar = []
                xc = lambda c: xsb[:, 0, c, :]
                tx = x_ready[t]

                if pha < 1:
                    barrier = {s: p.ck(s) for s in ("PE", "V", "S", "dG")}
                    continue
                sq_pe = []
                mm1_last = None
                for c in range(KC):
                    tsqr = p.emit("scalar", lambda e, c=c: e.activation(
                        sq[:, c % 2, :], xc(c), ACT.Square),
                        waits=[tx, (sq_pe[c - 2] if c >= 2 else prev_sqpe)],
                        inc="S")
                    mm1_last = p.emit("tensor", lambda e, c=c: e.matmul(
                        pb[0][0:1, 0:NTOK], ones_c[:, :], xc(c),
                        start=(c == 0), stop=(c == KC - 1), skip_group_check=True),
                        waits=[tx, t_consts, pb01_free],
                        inc="PE" if c == KC - 1 else None)
                    sq_pe.append(p.emit("tensor", lambda e, c=c: e.matmul(
                        pb[7][0:1, 0:NTOK], ones_c[:, :], sq[:, c % 2, :],
                        start=(c == 0), stop=(c == KC - 1), skip_group_check=True), waits=[tsqr], inc="PE"))

                def ln_rows(tok_mm, prev_dma=None):
                    tmu = p.emit("vector", lambda e: e.tensor_scalar(
                        rows[0:1, 0:NTOK], pb[0][0:1, 0:NTOK], 1.0 / D, None,
                        ALU.mult), waits=[tok_mm, prev_dma], inc="V")
                    p.emit("vector", lambda e: e.tensor_scalar(
                        rows[0:1, NTOK:2 * NTOK], pb[7][0:1, 0:NTOK],
                        1.0 / D, None, ALU.mult), waits=[], inc="V")
                    p.emit("vector", lambda e: e.tensor_tensor(
                        rows2[0:1, 0:NTOK], rows[0:1, 0:NTOK], rows[0:1, 0:NTOK],
                        ALU.mult), waits=[], inc="V")
                    tvar = p.emit("vector", lambda e: e.tensor_tensor(
                        rows2[0:1, NTOK:2 * NTOK], rows[0:1, NTOK:2 * NTOK],
                        rows2[0:1, 0:NTOK], ALU.subtract), waits=[], inc="V")
                    tsq = p.emit("scalar", lambda e: e.activation(
                        rows2[0:1, 0:NTOK], rows2[0:1, NTOK:2 * NTOK], ACT.Sqrt,
                        bias=eps1[:, :]), waits=[tvar], inc="S")
                    trr = p.emit("vector", lambda e: e.reciprocal(
                        rows[0:1, NTOK:2 * NTOK], rows2[0:1, 0:NTOK]),
                        waits=[tsq], inc="V")
                    tbc1 = p.emit("tensor", lambda e: e.matmul(
                        pb[0][:, 0:NTOK], ones_r[:, :], rows[0:1, 0:NTOK],
                        start=True, stop=True), waits=[tmu, trr], inc="PE")
                    tbc2 = p.emit("tensor", lambda e: e.matmul(
                        pb[0][:, NTOK:2 * NTOK], ones_r[:, :],
                        rows[0:1, NTOK:2 * NTOK], start=True, stop=True),
                        waits=[trr], inc="PE")
                    trbc = p.emit("vector", lambda e: e.tensor_copy(
                        rbc_sb[:, :], pb[0][:, NTOK:2 * NTOK]), waits=[tbc2],
                        inc="V")
                    return tbc1, tbc2, trbc

                tbc1, tbc2, trbc = ln_rows(sq_pe[-1], prev_iv)
                if pha < 2:
                    barrier = {s: p.ck(s) for s in ("PE", "V", "S", "dG")}
                    continue

                for c in range(KC):
                    reg = pb[2] if c % 2 == 0 else pb[3]
                    p.emit("vector", lambda e, c=c, reg=reg: e.tensor_tensor(
                        reg[:, 0:NTOK], xc(c), pb[0][:, 0:NTOK], ALU.subtract),
                        waits=[tbc2], inc="V")
                    p.emit("vector", lambda e, reg=reg: e.tensor_tensor(
                        reg[:, NTOK:2 * NTOK], reg[:, 0:NTOK], rbc_sb[:, :],
                        ALU.mult), waits=[trbc], inc="V")
                    p.emit("vector", lambda e, c=c, reg=reg: e.tensor_scalar(
                        xnhi[:, c, :], reg[:, NTOK:2 * NTOK], gam[:, c:c + 1],
                        bet[:, c:c + 1], ALU.mult, ALU.add), waits=[], inc="V")
                    p.emit("vector", lambda e, c=c, reg=reg: e.tensor_scalar(
                        reg[:, 0:NTOK], reg[:, NTOK:2 * NTOK], gam[:, c:c + 1],
                        bet[:, c:c + 1], ALU.mult, ALU.add), waits=[], inc="V")
                    p.emit("vector", lambda e, c=c, reg=reg: e.tensor_tensor(
                        xnlo[:, c, :], reg[:, 0:NTOK], xnhi[:, c, :], ALU.subtract),
                        waits=[], inc="V")
                xn_done = p.ck("V")

                if pha < 3:
                    barrier = {s: p.ck(s) for s in ("PE", "V", "S", "dG")}
                    continue
                regions = [pb[2][:, 0:NTOK], pb[3][:, 0:NTOK],
                           pb[4][:, 0:NTOK]]
                regions2 = [pb[2][:, NTOK:2 * NTOK], pb[3][:, NTOK:2 * NTOK],
                            pb[4][:, NTOK:2 * NTOK]]
                reg_free = [None, None, None]
                reg_i = [0]

                def next_region():
                    i = reg_i[0] % 3
                    reg_i[0] += 1
                    return i, regions[i]

                def proj3(reg, stat_hi, stat_lo, mov_hi, mov_lo, first_waits):
                    last = None
                    for k in range(KC):
                        fst = (k == 0)
                        lst = (k == KC - 1)
                        p.emit("tensor", lambda e, k=k, reg=reg, sh=stat_hi,
                               mh=mov_hi, fst=fst: e.matmul(
                            reg, sh(k), mh(k), start=fst, stop=False),
                            waits=first_waits if fst else [], inc=None)
                        p.emit("tensor", lambda e, k=k, reg=reg, sh=stat_hi,
                               ml=mov_lo: e.matmul(
                            reg, sh(k), ml(k), start=False, stop=False),
                            waits=[], inc=None)
                        last = p.emit("tensor", lambda e, k=k, reg=reg, sl=stat_lo,
                                      mh=mov_hi, lst=lst: e.matmul(
                            reg, sl(k), mh(k), start=False, stop=lst),
                            waits=[], inc="PE" if lst else None)
                    return last

                m_v = t * 5 + 0
                wv_par, wv_tok = m_v % 2, w_ready[m_v]
                vt_ev = []
                for e2 in range(2):
                    for hf in range(2):
                        i = (e2 * 2 + hf) % 2
                        regfull = pb[4] if i == 0 else pb[5]
                        last = proj3(
                            regfull[:, 0:512],
                            lambda k, e2=e2: xnhi[:, k, e2 * 128:(e2 + 1) * 128],
                            lambda k, e2=e2: xnlo[:, k, e2 * 128:(e2 + 1) * 128],
                            lambda k, wv_par=wv_par, hf=hf:
                                wbh[:, wv_par, k, hf * 512:(hf + 1) * 512],
                            lambda k, wv_par=wv_par, hf=hf:
                                wbl[:, wv_par, k, hf * 512:(hf + 1) * 512],
                            [wv_tok, xn_done, wo_v_ck,
                             (vt_ev[-2] if len(vt_ev) >= 2 else None)])
                        vt_ev.append(p.emit("vector",
                            lambda e, e2=e2, hf=hf, regfull=regfull:
                            e.tensor_tensor(
                                vtm[:, e2, hf * 512:(hf + 1) * 512],
                                regfull[:, 0:512], bvbc[:, hf * 512:(hf + 1) * 512],
                                ALU.add), waits=[last, t_bvbc], inc="V"))
                vtm_done = p.ck("V")
                w_free[m_v] = p.ck("PE")
                prefetch_w(m_v + 2)

                if pha < 4:
                    barrier = {s: p.ck(s) for s in ("PE", "V", "S", "dG")}
                    continue
                reg_free[0] = reg_free[1] = xn_done
                reg_free[2] = vt_ev[3]
                reg_i[0] = 0
                ev_toks = {"q": [], "k": [], "g": []}
                for j, (nm, wn) in enumerate((("q", 1), ("k", 2), ("g", 3))):
                    m = t * 5 + wn
                    wpar, wtok = m % 2, w_ready[m]
                    for c in range(KC):
                        if nm == "q" and (t - 1) in defer_ctx:
                            if c == 0:
                                emit_deferred(t - 1, piece=0)
                            elif c in (2, 4, 6):
                                emit_deferred(t - 1, piece=c // 2)
                        ri, reg = next_region()
                        last = proj3(
                            reg,
                            lambda k, wpar=wpar, c=c:
                                wbh[:, wpar, k, c * 128:(c + 1) * 128],
                            lambda k, wpar=wpar, c=c:
                                wbl[:, wpar, k, c * 128:(c + 1) * 128],
                            lambda k: xnhi[:, k, :],
                            lambda k: xnlo[:, k, :],
                            [wtok, xn_done, reg_free[ri]])
                        if nm == "q":
                            for bb in range(BC):
                                tok = p.emit("vector", lambda e, c=c, reg=reg,
                                             bb=bb: e.scalar_tensor_tensor(
                                    qf[:, c, bb * 64:(bb + 1) * 64],
                                    reg[:, bb * 64:(bb + 1) * 64],
                                    bias["bq"][:, c:c + 1], aq[:, c, :],
                                    ALU.add, ALU.mult),
                                    waits=[last, tok_atiles], inc="V")
                        elif nm == "k":
                            for bb in range(BC):
                                tok = p.emit("vector", lambda e, c=c, reg=reg,
                                             bb=bb: e.scalar_tensor_tensor(
                                    kf[:, c, bb * 64:(bb + 1) * 64],
                                    reg[:, bb * 64:(bb + 1) * 64],
                                    bias["bk"][:, c:c + 1], ak[:, c, :],
                                    ALU.add, ALU.mult),
                                    waits=[last], inc="V")
                        else:
                            tok = p.emit("scalar", lambda e, c=c, reg=reg:
                                         e.activation(
                                gs[:, c, :], reg, ACT.Silu,
                                bias=bias["bg"][:, c:c + 1]),
                                waits=[last, prev_tlo], inc="S")
                        ev_toks[nm].append(tok)
                        reg_free[ri] = tok
                    w_free[m] = p.ck("PE")
                    prefetch_w(m + 2)
                    pass

                if pha < 5:
                    barrier = {s: p.ck(s) for s in ("PE", "V", "S", "dG")}
                    continue
                tr_banks = [pb[0], pb[6], pb[1], pb[7]]
                ktr_ev = []
                nidx = 0
                for c in range(KC):
                    for b4 in range(BC):
                        for hq in range(2):
                            bank = tr_banks[nidx % 4]
                            ttr = p.emit("tensor",
                                lambda e, c=c, b4=b4, hq=hq, bank=bank:
                                e.transpose(
                                    bank[0:64, NTOK:NTOK + 64],
                                    kf[hq * 64:hq * 64 + 64, c, b4 * 64:(b4 + 1) * 64],
                                    ident[hq * 64:hq * 64 + 64, :],
                                    tile_position=(hq * 64, 0)),
                                waits=[ev_toks["k"][c],
                                       (ktr_ev[-4] if len(ktr_ev) >= 4 else None)],
                                inc="PE")
                            ktr_ev.append(p.emit("vector",
                                lambda e, c=c, b4=b4, hq=hq, bank=bank:
                                e.tensor_copy(
                                    ktm[(b4 % 2) * 64:(b4 % 2) * 64 + 64, b4 // 2,
                                        c * 128 + hq * 64:c * 128 + hq * 64 + 64],
                                    bank[0:64, NTOK:NTOK + 64]),
                                waits=[ttr], inc="V"))
                            nidx += 1
                ktm_done = p.ck("V")

                if pha < 6:
                    barrier = {s: p.ck(s) for s in ("PE", "V", "S", "dG")}
                    continue
                prev_scan_v = None
                og_done = None
                tlo_last = None
                for c in range(KC):
                    pmm = None
                    for b4 in range(BC):
                        for hq in range(2):
                            bank = pb[6] if hq == 0 else pb[7]
                            pmm = p.emit("tensor", lambda e, c=c, b4=b4, hq=hq,
                                         bank=bank: e.matmul(
                                bank[(b4 % 2) * 64:(b4 % 2) * 64 + 64,
                                     (b4 // 2) * 64:(b4 // 2) * 64 + 64],
                                kf[hq * 64:hq * 64 + 64, c, b4 * 64:(b4 + 1) * 64],
                                qf[hq * 64:hq * 64 + 64, c, b4 * 64:(b4 + 1) * 64],
                                start=True, stop=True,
                                tile_position=(hq * 64, (b4 % 2) * 64)),
                                waits=[ev_toks["q"][c], ev_toks["k"][c],
                                       prev_scan_v, ktm_done], inc="PE")
                    pmkv = pmk[:, :].rearrange("p (a h t) -> p a h t", a=2, h=2)
                    maskv = mask[:, 0:128].rearrange("p (a t) -> p a t", a=2)
                    tpm0 = p.emit("vector", lambda e, pmkv=pmkv, maskv=maskv:
                                  e.tensor_tensor(
                        pmkv[:, :, 0, :],
                        pb[6][:, 0:128].rearrange("p (a t) -> p a t", a=2),
                        maskv, ALU.mult), waits=[pmm], inc="V")
                    tpm = p.emit("vector", lambda e, pmkv=pmkv, maskv=maskv:
                                 e.tensor_tensor(
                        pmkv[:, :, 1, :],
                        pb[7][:, 0:128].rearrange("p (a t) -> p a t", a=2),
                        maskv, ALU.mult), waits=[], inc="V")
                    omm = None
                    for b4 in range(BC):
                        for hq in range(2):
                            p.emit("tensor", lambda e, c=c, b4=b4, hq=hq: e.matmul(
                                pb[2][(b4 % 2) * 64:(b4 % 2) * 64 + 64,
                                      (b4 // 2) * 128 + hq * 64:
                                      (b4 // 2) * 128 + hq * 64 + 64],
                                vtm[(b4 % 2) * 64:(b4 % 2) * 64 + 64, b4 // 2,
                                    c * 128 + hq * 64:c * 128 + hq * 64 + 64],
                                pmk[(b4 % 2) * 64:(b4 % 2) * 64 + 64,
                                    (b4 // 2) * 128 + hq * 64:
                                    (b4 // 2) * 128 + hq * 64 + 64],
                                start=True, stop=True,
                                tile_position=((b4 % 2) * 64, (b4 % 2) * 64)),
                                waits=[tpm, vtm_done, og_done], inc=None)
                            omm = p.emit("tensor", lambda e, c=c, b4=b4, hq=hq,
                                         par=par: e.matmul(
                                pb[3][hq * 64:hq * 64 + 64, b4 * 64:(b4 + 1) * 64],
                                hp[par][hq * 64:hq * 64 + 64, c,
                                        b4 * 64:(b4 + 1) * 64],
                                qf[hq * 64:hq * 64 + 64, c, b4 * 64:(b4 + 1) * 64],
                                start=True, stop=True,
                                tile_position=(hq * 64, hq * 64)),
                                waits=[th0 if t == 0 else None], inc="PE")
                    smm = None
                    for b4 in range(BC):
                        for hq in range(2):
                            smm = p.emit("tensor", lambda e, c=c, b4=b4, hq=hq:
                                         e.matmul(
                                pb[4][(b4 % 2) * 64:(b4 % 2) * 64 + 64,
                                      (b4 // 2) * 128 + hq * 64:
                                      (b4 // 2) * 128 + hq * 64 + 64],
                                ktm[(b4 % 2) * 64:(b4 % 2) * 64 + 64, b4 // 2,
                                    c * 128 + hq * 64:c * 128 + hq * 64 + 64],
                                vtm[(b4 % 2) * 64:(b4 % 2) * 64 + 64, b4 // 2,
                                    c * 128 + hq * 64:c * 128 + hq * 64 + 64],
                                start=True, stop=True,
                                tile_position=((b4 % 2) * 64, (b4 % 2) * 64)),
                                waits=[], inc="PE")
                    cp = None
                    for b4 in range(BC):
                        for hq in range(2):
                            cp = p.emit("vector", lambda e, b4=b4, hq=hq:
                                        e.tensor_copy(
                                o1sb[hq * 64:hq * 64 + 64, b4 * 64:(b4 + 1) * 64],
                                pb[2][(b4 % 2) * 64:(b4 % 2) * 64 + 64,
                                      (b4 // 2) * 128 + hq * 64:
                                      (b4 // 2) * 128 + hq * 64 + 64]),
                                waits=[omm], inc="V")
                    tadd = p.emit("vector", lambda e: e.tensor_tensor(
                        pb[3][:, NTOK:2 * NTOK], o1sb[:, :], pb[3][:, 0:NTOK],
                        ALU.add), waits=[], inc="V")
                    tog = p.emit("vector", lambda e, c=c: e.tensor_tensor(
                        pb[5][:, 0:NTOK], pb[3][:, NTOK:2 * NTOK], gs[:, c, :],
                        ALU.mult), waits=[ev_toks["g"][c]], inc="V")
                    p.emit("vector", lambda e, c=c: e.tensor_copy(
                        ohi[:, c, :], pb[5][:, 0:NTOK]), waits=[], inc="V")
                    tlo_last = p.emit("vector", lambda e, c=c: e.tensor_tensor(
                        olo[:, c, :], pb[5][:, 0:NTOK], ohi[:, c, :],
                        ALU.subtract), waits=[], inc="V")
                    og_done = tlo_last
                    for b4 in range(BC):
                        for hq in range(2):
                            p.emit("vector", lambda e, b4=b4, hq=hq: e.tensor_copy(
                                srm[hq * 64:hq * 64 + 64, b4 * 64:(b4 + 1) * 64],
                                pb[4][(b4 % 2) * 64:(b4 % 2) * 64 + 64,
                                      (b4 // 2) * 128 + hq * 64:
                                      (b4 // 2) * 128 + hq * 64 + 64]),
                                waits=[smm], inc="V")
                    p.emit("vector", lambda e, c=c, par=par: e.tensor_tensor(
                        pb[4][:, NTOK:2 * NTOK], hp[par][:, c, :], srm[:, :],
                        ALU.add), waits=[], inc="V")
                    prev_scan_v = p.emit("vector", lambda e, c=c, par=par:
                                         e.tensor_scalar(
                        hp[1 - par][:, c, :], pb[4][:, NTOK:2 * NTOK],
                        a64[:, c:c + 1], None, ALU.mult), waits=[], inc="V")
                scan_v_done = p.ck("V")

                m_o = t * 5 + 4
                wo_par, wo_tok = m_o % 2, w_ready[m_o]
                s_toks = []
                for mc in range(KC):
                    ri, reg = next_region()
                    reg2 = regions2[ri]
                    last = proj3(
                        reg,
                        lambda k, wo_par=wo_par, mc=mc:
                            wbh[:, wo_par, k, mc * 128:(mc + 1) * 128],
                        lambda k, wo_par=wo_par, mc=mc:
                            wbl[:, wo_par, k, mc * 128:(mc + 1) * 128],
                        lambda k: ohi[:, k, :],
                        lambda k: olo[:, k, :],
                        [wo_tok, tlo_last, reg_free[ri], scan_v_done])
                    tb = p.emit("vector", lambda e, mc=mc, reg=reg, reg2=reg2:
                                e.tensor_scalar(
                        reg2, reg, bias["bo"][:, mc:mc + 1], None, ALU.add),
                        waits=[last], inc="V")
                    stok = p.emit("vector", lambda e, mc=mc, reg2=reg2:
                                  e.scalar_tensor_tensor(
                        ssb[:, mc, :], xc(mc), 2.0, reg2, ALU.mult, ALU.add),
                        waits=[prev_S if mc == 0 else None], inc="V")
                    s_toks.append(stok)
                    reg_free[ri] = stok
                x_free[t] = s_toks[-1]
                wo_v_ck = p.ck("V")
                w_free[m_o] = p.ck("PE")
                prefetch_w(m_o + 2)
                prefetch_x(t + 1)

                if pha < 8:
                    barrier = {s: p.ck(s) for s in ("PE", "V", "S", "dG")}
                    continue
                sq_pe2 = []
                mm2_last = None
                for c in range(KC):
                    tsqr = p.emit("scalar", lambda e, c=c: e.activation(
                        sq[:, c % 2, :], ssb[:, c, :], ACT.Square),
                        waits=[s_toks[c], (sq_pe2[c - 2] if c >= 2 else None)],
                        inc="S")
                    mm2_last = p.emit("tensor", lambda e, c=c: e.matmul(
                        pb[0][0:1, 0:NTOK], ones_c[:, :], ssb[:, c, :],
                        start=(c == 0), stop=(c == KC - 1), skip_group_check=True),
                        waits=[s_toks[c]], inc="PE" if c == KC - 1 else None)
                    sq_pe2.append(p.emit("tensor", lambda e, c=c: e.matmul(
                        pb[7][0:1, 0:NTOK], ones_c[:, :], sq[:, c % 2, :],
                        start=(c == 0), stop=(c == KC - 1), skip_group_check=True), waits=[tsqr],
                        inc="PE"))
                tbc1, tbc2, trbc = ln_rows(sq_pe2[-1], prev_iv)

                x2_toks = []
                for c in range(KC):
                    reg = pb[2] if c % 2 == 0 else pb[3]
                    p.emit("vector", lambda e, c=c, reg=reg: e.tensor_tensor(
                        reg[:, 0:NTOK], ssb[:, c, :], pb[0][:, 0:NTOK],
                        ALU.subtract), waits=[tbc2], inc="V")
                    x2_toks.append(p.emit("vector", lambda e, c=c, reg=reg,
                                          par=par: e.tensor_tensor(
                        x2sb[:, par, c, :], reg[:, 0:NTOK], rbc_sb[:, :],
                        ALU.mult), waits=[trbc, prev_store], inc="V"))
                pb01_free = p.ck("V")
                defer_ctx[t] = dict(par=par, x2_toks=list(x2_toks))
                prev_sqpe = sq_pe2[-1]
                prev_S = p.ck("S")
                prev_tlo = tlo_last

                if t == 0 and dump:
                    taps = [("xnhi", xnhi, F16), ("xnlo", xnlo, BF16),
                            ("qf", qf, F32), ("kf", kf, F32), ("gs", gs, F32),
                            ("ohi", ohi, F16), ("olo", olo, BF16),
                            ("ssb", ssb, F32), ("x2sb", x2sb[:, 0], F32),
                            ("hp1", hp[1], F32)]
                    vck = p.ck("V")
                    for nm, tile, dt in taps:
                        dp = dump_out(nm, [128, KC, NTOK], dt)
                        if dp is not None:
                            p.emit("gpsimd", lambda e, dp=dp, tile=tile:
                                   e.dma_start(out=dp[:, :, :], in_=tile[:, :, :]),
                                   waits=[vck], inc="dG", inc_n=16)
                    for nm, tile in (("aq", aq), ("ak", ak)):
                        dp = dump_out(nm, [128, KC, 64])
                        if dp is not None:
                            p.emit("gpsimd", lambda e, dp=dp, tile=tile:
                                   e.dma_start(out=dp[:, :, :], in_=tile[:, :, :]),
                                   waits=[vck], inc="dG", inc_n=16)
                    for nm, tile in (("vtm", vtm), ("ktm", ktm)):
                        dp = dump_out(nm, [128, 2, DI])
                        if dp is not None:
                            p.emit("gpsimd", lambda e, dp=dp, tile=tile:
                                   e.dma_start(out=dp[:, :, :], in_=tile[:, :, :]),
                                   waits=[vck], inc="dG", inc_n=16)
                    p.emit("vector", lambda e: e.engine_nop(),
                           waits=[p.ck("dG")], inc="V")

                barrier = {s: p.ck(s) for s in ("PE", "V", "S", "dG")}

            if (nta - 1) in defer_ctx:
                emit_deferred(nta - 1)
            passA_dG = p.ck("dG")
            passA_V = p.ck("V")

        with contextlib.ExitStack() as esb:
            def sbb(name, shape, dt=F32):
                return esb.enter_context(nc.sbuf_tensor(name, shape, dt))

            tg_sb = sbb("tg_sb", [128, KC, FF], FP8)
            tu_sb = sbb("tu_sb", [128, KC, FF], FP8)
            td_sb = sbb("td_sb", [128, OC, D], FP8)
            i1b = sbb("i1b", [128, 1, KC, NTOK], BF16)
            x2b = sbb("x2b", [128, 1, KC, NTOK])
            iv1r = sbb("iv1r", [1, 1, NTOK])
            iv1bc2 = sbb("iv1bc2", [128, 512])
            iv2bc = sbb("iv2bc", [128, NTOK])
            g1 = sbb("g1", [128, 2, 512])
            gate = sbb("gate", [128, 512])
            hsb = sbb("hsb", [128, OC * NTOK])
            i2sb = sbb("i2sb", [128, OC, NTOK], BF16)
            s3sb = sbb("s3sb", [128, KC, NTOK])
            accpB = sbb("accpB", [128, NTOK])
            accpB2 = sbb("accpB2", [128, 512])
            accnB2 = sbb("accnB2", [128, 512])
            accnB = sbb("accnB", [128, NTOK])
            amB = sbb("amB", [128, NTOK])
            amfB = sbb("amfB", [64, NTOK])
            rbcB = sbb("rbcB", [128, NTOK])
            bsbB = sbb("bsbB", [128, 512])
            ufinB = sbb("ufinB", [128, 512])
            sqB = sbb("sqB", [128, 2, NTOK])

            passA_PE = p.ck("PE")
            passA_S = p.ck("S")
            for dst2, src2 in [(tg_sb[:, :, :], tg_s[:, :, :]),
                               (tu_sb[:, :, :], tu_s[:, :, :]),
                               (td_sb[:, :, :], td_s[:, :, :])]:
                dma("sync", dst2, src2, waits=[passA_dG, passA_PE, passA_V,
                                               passA_S])
            tT = p.ck("dS")

            xb_free = {}
            xb_ready = {}

            def prefetch_b(t):
                if t >= ntb or t in xb_ready:
                    return
                p.emit("sync", lambda e, t=t: e.dma_start(
                    out=i1b[:, 0, :, :], in_=i1_s[t, :, :, :]),
                    waits=[xb_free.get(t - 1), passA_dG, passA_PE, passA_S,
                           passA_V], inc="dS", inc_n=16)
                p.emit("sync", lambda e, t=t: e.dma_start(
                    out=x2b[:, 0, :, :], in_=x2_s[t, :, :, :]),
                    waits=[], inc="dS", inc_n=16)
                xb_ready[t] = p.emit("sync", lambda e, t=t: e.dma_start(
                    out=iv1r[:, 0, :], in_=iv1_s[t, :, :]),
                    waits=[], inc="dS", inc_n=16)

            prefetch_b(0)
            barB_V = passA_V
            prev_out = None

            for t in range(ntb):
                par = t % 2
                tin = xb_ready[t]
                tb = p.emit("tensor", lambda e, par=par: e.matmul(
                    pb[1][:, 0:NTOK], ones_r[:, :], iv1r[0:1, 0, :],
                    start=True, stop=True), waits=[tin, tT, barB_V], inc="PE")
                p.emit("vector", lambda e: e.tensor_copy(
                    iv1bc2[:, 0:NTOK], pb[1][:, 0:NTOK]),
                    waits=[tb, passA_dG], inc="V")
                tivc = p.emit("vector", lambda e: e.tensor_copy(
                    iv1bc2[:, NTOK:2 * NTOK], pb[1][:, 0:NTOK]), waits=[], inc="V")

                h_last = None
                h_toks = []
                for ocp in range(0, OC, 2):
                    pi = (ocp // 2) % 2
                    rg = pb[2 + pi * 2]
                    ru = pb[3 + pi * 2]
                    lastg = None
                    lastu = None
                    for half in range(2):
                        oc = ocp + half
                        for k in range(KC):
                            fst = (k == 0)
                            lst = (k == KC - 1)
                            lastg = p.emit("tensor", lambda e, k=k, oc=oc, rg=rg,
                                           half=half, par=par, fst=fst, lst=lst:
                                           e.matmul(
                                rg[:, half * NTOK:(half + 1) * NTOK],
                                tg_sb[:, k, oc * 128:(oc + 1) * 128],
                                i1b[:, 0, k, :], start=fst, stop=lst,
                                skip_group_check=True),
                                waits=([tin, (h_toks[-2] if len(h_toks) >= 2
                                              else None)] if fst else []),
                                inc="PE" if lst else None)
                    for half in range(2):
                        oc = ocp + half
                        for k in range(KC):
                            fst = (k == 0)
                            lst = (k == KC - 1)
                            lastu = p.emit("tensor", lambda e, k=k, oc=oc, ru=ru,
                                           half=half, par=par, fst=fst, lst=lst:
                                           e.matmul(
                                ru[:, half * NTOK:(half + 1) * NTOK],
                                tu_sb[:, k, oc * 128:(oc + 1) * 128],
                                i1b[:, 0, k, :], start=fst, stop=lst,
                                skip_group_check=True),
                                waits=[], inc="PE" if lst else None)
                    tg1 = p.emit("vector", lambda e, rg=rg:
                                 e.scalar_tensor_tensor(
                        g1[:, 0, :], rg[:, 0:512], mghat[:, 0:1], iv1bc2[:, :],
                        ALU.mult, ALU.mult), waits=[lastg, tivc], inc="V")
                    tgate = p.emit("scalar", lambda e: e.activation(
                        gate[:, :], g1[:, 0, :], ACT.Silu),
                        waits=[tg1], inc="S")
                    tu1 = p.emit("vector", lambda e, ru=ru:
                                 e.scalar_tensor_tensor(
                        g1[:, 1, :], ru[:, 0:512], mghat[:, 1:2], iv1bc2[:, :],
                        ALU.mult, ALU.mult), waits=[lastu], inc="V")
                    th = p.emit("vector", lambda e, ocp=ocp: e.tensor_tensor(
                        hsb[:, ocp * NTOK:(ocp + 2) * NTOK],
                        gate[:, :], g1[:, 1, :], ALU.mult),
                        waits=[tgate, tu1], inc="V")
                    if ocp == 0:
                        p.emit("vector", lambda e: e.tensor_copy(
                            accpB2[:, :], hsb[:, 0:2 * NTOK]),
                            waits=[], inc="V")
                        h_last = p.emit("vector", lambda e: e.tensor_copy(
                            accnB2[:, :], hsb[:, 0:2 * NTOK]),
                            waits=[], inc="V")
                    else:
                        p.emit("vector", lambda e, ocp=ocp: e.tensor_tensor(
                            accpB2[:, :], accpB2[:, :],
                            hsb[:, ocp * NTOK:(ocp + 2) * NTOK], ALU.max),
                            waits=[], inc="V")
                        h_last = p.emit("vector", lambda e, ocp=ocp: e.tensor_tensor(
                            accnB2[:, :], accnB2[:, :],
                            hsb[:, ocp * NTOK:(ocp + 2) * NTOK], ALU.min),
                            waits=[], inc="V")
                    h_toks.append(h_last)
                p.emit("vector", lambda e: e.tensor_tensor(
                    accpB[:, :], accpB2[:, 0:NTOK], accpB2[:, NTOK:2 * NTOK],
                    ALU.max), waits=[h_last], inc="V")
                h_last = p.emit("vector", lambda e: e.tensor_tensor(
                    accnB[:, :], accnB2[:, 0:NTOK], accnB2[:, NTOK:2 * NTOK],
                    ALU.min), waits=[], inc="V")

                p.emit("vector", lambda e: e.scalar_tensor_tensor(
                    amB[:, :], accnB[:, :], -1.0, accpB[:, :], ALU.mult, ALU.max),
                    waits=[h_last], inc="V")
                for half in (64, 32):
                    p.emit("vector", lambda e, h=half: e.tensor_copy(
                        amfB[0:h, :], amB[h:2 * h, :]), waits=[], inc="V")
                    p.emit("vector", lambda e, h=half: e.tensor_tensor(
                        amB[0:h, :], amB[0:h, :], amfB[0:h, :], ALU.max),
                        waits=[], inc="V")
                for half in (16, 8, 4, 2, 1):
                    msk = [(i + half) % 32 for i in range(32)]
                    p.emit("vector", lambda e, m=msk: e.stream_shuffle(
                        amfB[0:32, :], amB[0:32, :], m), waits=[], inc="V")
                    p.emit("vector", lambda e, h=half: e.tensor_tensor(
                        amB[0:h, :], amB[0:h, :], amfB[0:h, :], ALU.max),
                        waits=[], inc="V")
                p.emit("vector", lambda e: e.tensor_scalar(
                    rows[0:1, 0:NTOK], amB[0:1, :], 1e-5, None, ALU.max),
                    waits=[], inc="V")
                p.emit("vector", lambda e: e.reciprocal(
                    rows[0:1, NTOK:2 * NTOK], rows[0:1, 0:NTOK]), waits=[], inc="V")
                ts2 = p.emit("vector", lambda e: e.tensor_scalar(
                    rows2[0:1, 0:NTOK], rows[0:1, NTOK:2 * NTOK], 127.0, None,
                    ALU.mult), waits=[], inc="V")
                tiv2 = p.emit("vector", lambda e: e.tensor_scalar(
                    rows2[0:1, NTOK:2 * NTOK], rows[0:1, 0:NTOK], 1.0 / 127.0,
                    None, ALU.mult), waits=[], inc="V")
                tb1 = p.emit("tensor", lambda e: e.matmul(
                    pb[1][:, 0:NTOK], ones_r[:, :], rows2[0:1, 0:NTOK],
                    start=True, stop=True), waits=[ts2], inc="PE")
                tb1b = p.emit("tensor", lambda e: e.matmul(
                    pb[1][:, NTOK:2 * NTOK], ones_r[:, :], rows2[0:1, 0:NTOK],
                    start=True, stop=True), waits=[], inc="PE")
                tb2 = p.emit("tensor", lambda e: e.matmul(
                    pb[0][:, 0:NTOK], ones_r[:, :],
                    rows2[0:1, NTOK:2 * NTOK], start=True, stop=True),
                    waits=[tiv2], inc="PE")
                tiv2c = p.emit("vector", lambda e: e.tensor_copy(
                    iv2bc[:, :], pb[0][:, 0:NTOK]), waits=[tb2, tb1b],
                    inc="V")

                ti2 = None
                for cc in range(0, OC, 2):
                    u0v = pb[2][:, 0:512]
                    pgA = pb[3][:, 0:512]
                    pgB = pb[6][:, 0:512]
                    p.emit("vector", lambda e, cc=cc: e.tensor_tensor(
                        pb[2][:, 0:512], hsb[:, cc * NTOK:(cc + 2) * NTOK],
                        pb[1][:, 0:512], ALU.mult), waits=[tiv2c], inc="V")
                    p.emit("vector", lambda e, u0v=u0v: e.tensor_scalar(
                        u0v, u0v, 128.0, None, ALU.add),
                        waits=[], inc="V")
                    srcv = u0v
                    for ki, k in enumerate(range(7, -1, -1)):
                        cth = float(2 ** k) - 0.5
                        p.emit("vector", lambda e, srcv=srcv, cth=cth, k=k:
                               e.tensor_scalar(
                            bsbB[:, :], srcv, cth, -float(2 ** k),
                            ALU.is_ge, ALU.mult), waits=[], inc="V")
                        dstv = ufinB[:, :] if k == 0 else (pgA if ki % 2 == 0 else pgB)
                        p.emit("vector", lambda e, srcv=srcv, dstv=dstv:
                               e.tensor_tensor(
                            dstv, bsbB[:, :], srcv, ALU.add), waits=[], inc="V")
                        srcv = dstv
                    p.emit("vector", lambda e, u0v=u0v: e.scalar_tensor_tensor(
                        bsbB[:, :], ufinB[:, :], -1.0, u0v, ALU.mult, ALU.add),
                        waits=[], inc="V")
                    p.emit("vector", lambda e, cc=cc: e.tensor_scalar(
                        i2sb[:, cc, :], bsbB[:, 0:NTOK], -128.0, None, ALU.add),
                        waits=[], inc="V")
                    ti2 = p.emit("vector", lambda e, cc=cc: e.tensor_scalar(
                        i2sb[:, cc + 1, :], bsbB[:, NTOK:2 * NTOK], -128.0, None,
                        ALU.add), waits=[], inc="V")

                s3_toks = []
                for mc in range(KC):
                    reg = pb[4][:, 0:NTOK] if mc % 2 == 0 else pb[5][:, 0:NTOK]
                    reg2 = (pb[4][:, NTOK:2 * NTOK] if mc % 2 == 0
                            else pb[5][:, NTOK:2 * NTOK])
                    last = None
                    for k in range(OC):
                        fst = (k == 0)
                        lst = (k == OC - 1)
                        last = p.emit("tensor", lambda e, k=k, mc=mc, reg=reg,
                                      fst=fst, lst=lst: e.matmul(
                            reg, td_sb[:, k, mc * 128:(mc + 1) * 128],
                            i2sb[:, k, :], start=fst, stop=lst),
                            waits=([ti2, (s3_toks[mc - 2] if mc >= 2 else None)]
                                   if fst else []), inc="PE" if lst else None)
                    p.emit("vector", lambda e, reg=reg, reg2=reg2:
                           e.scalar_tensor_tensor(
                        reg2, reg, mghat[:, 2:3], iv2bc[:, :], ALU.mult,
                        ALU.mult), waits=[last], inc="V")
                    s3_toks.append(p.emit("vector", lambda e, mc=mc, par=par,
                                          reg2=reg2: e.tensor_tensor(
                        s3sb[:, mc, :], x2b[:, 0, mc, :], reg2, ALU.add),
                        waits=[], inc="V"))
                xb_free[t] = s3_toks[-1]
                prefetch_b(t + 1)

                sq_pe3 = []
                mm3_last = None
                for c in range(KC):
                    tsqr = p.emit("scalar", lambda e, c=c: e.activation(
                        sqB[:, c % 2, :], s3sb[:, c, :], ACT.Square),
                        waits=[s3_toks[c], (sq_pe3[c - 2] if c >= 2 else None)],
                        inc="S")
                    mm3_last = p.emit("tensor", lambda e, c=c: e.matmul(
                        pb[0][0:1, 0:NTOK], ones_c[:, :], s3sb[:, c, :],
                        start=(c == 0), stop=(c == KC - 1), skip_group_check=True),
                        waits=[s3_toks[c]], inc="PE" if c == KC - 1 else None)
                    sq_pe3.append(p.emit("tensor", lambda e, c=c: e.matmul(
                        pb[1][0:1, 0:NTOK], ones_c[:, :], sqB[:, c % 2, :],
                        start=(c == 0), stop=(c == KC - 1), skip_group_check=True), waits=[tsqr],
                        inc="PE"))
                tmu = p.emit("vector", lambda e: e.tensor_scalar(
                    rows[0:1, 0:NTOK], pb[0][0:1, 0:NTOK], 1.0 / D, None,
                    ALU.mult), waits=[sq_pe3[-1], mm3_last], inc="V")
                p.emit("vector", lambda e: e.tensor_scalar(
                    rows[0:1, NTOK:2 * NTOK], pb[1][0:1, 0:NTOK], 1.0 / D,
                    None, ALU.mult), waits=[], inc="V")
                p.emit("vector", lambda e: e.tensor_tensor(
                    rows2[0:1, 0:NTOK], rows[0:1, 0:NTOK], rows[0:1, 0:NTOK],
                    ALU.mult), waits=[], inc="V")
                tvar = p.emit("vector", lambda e: e.tensor_tensor(
                    rows2[0:1, NTOK:2 * NTOK], rows[0:1, NTOK:2 * NTOK],
                    rows2[0:1, 0:NTOK], ALU.subtract), waits=[], inc="V")
                tsq = p.emit("scalar", lambda e: e.activation(
                    rows2[0:1, 0:NTOK], rows2[0:1, NTOK:2 * NTOK], ACT.Sqrt,
                    bias=eps1[:, :]), waits=[tvar], inc="S")
                trr = p.emit("vector", lambda e: e.reciprocal(
                    rows[0:1, NTOK:2 * NTOK], rows2[0:1, 0:NTOK]),
                    waits=[tsq], inc="V")
                tbc1 = p.emit("tensor", lambda e: e.matmul(
                    pb[0][:, 0:NTOK], ones_r[:, :], rows[0:1, 0:NTOK],
                    start=True, stop=True), waits=[tmu, trr], inc="PE")
                tbc2 = p.emit("tensor", lambda e: e.matmul(
                    pb[0][:, NTOK:2 * NTOK], ones_r[:, :],
                    rows[0:1, NTOK:2 * NTOK], start=True, stop=True),
                    waits=[trr], inc="PE")
                trbc = p.emit("vector", lambda e: e.tensor_copy(
                    rbcB[:, :], pb[0][:, NTOK:2 * NTOK]), waits=[tbc2], inc="V")
                for c in range(KC):
                    reg = pb[2] if c % 2 == 0 else pb[3]
                    p.emit("vector", lambda e, c=c, reg=reg: e.tensor_tensor(
                        reg[:, 0:NTOK], s3sb[:, c, :], pb[0][:, 0:NTOK],
                        ALU.subtract), waits=[tbc2, prev_out], inc="V")
                    otok = p.emit("vector", lambda e, c=c, reg=reg: e.tensor_tensor(
                        s3sb[:, c, :], reg[:, 0:NTOK], rbcB[:, :], ALU.mult),
                        waits=[trbc], inc="V")
                    prev_out = p.emit("gpsimd", lambda e, t=t, c=c: e.dma_start(
                        out=out_t[:, t, :].rearrange("(k p) n -> p k n", p=128)[
                            :, c, :],
                        in_=s3sb[:, c, :]), waits=[otok], inc="dG", inc_n=16)
                barB_V = p.ck("V")

            final_dg = p.ck("dG")
            p.emit("gpsimd", lambda e: e.engine_nop(), waits=[final_dg], inc="G")

        block = es.enter_context(nc.Block())
        p.build(block)

    return nc, dumps


def _host_inputs(core, L0_attn_x, L0_attn_WQ, L0_attn_bQ, L0_attn_WK, L0_attn_bK,
                 L0_attn_WV, L0_attn_bV, L0_attn_WO, L0_attn_bO, L0_attn_Wg,
                 L0_attn_bg, L0_attn_alpha_logit, L0_attn_ln_gamma,
                 L0_attn_ln_beta, L0_ffn_w_gate, L0_ffn_w_up, L0_ffn_w_down):
    xc = np.asarray(L0_attn_x)[4 * core:4 * core + 4]
    x_t = np.ascontiguousarray(
        xc.transpose(2, 1, 0).reshape(D, NT, C, BC).transpose(0, 1, 3, 2)
        .reshape(D, NT, NTOK))

    def colform(b):
        return np.ascontiguousarray(np.asarray(b, np.float32).reshape(KC, 128).T)
    a = np.asarray(L0_attn_alpha_logit, np.float32).reshape(KC, 2, 64)
    alpha_c = np.ascontiguousarray(a.transpose(1, 2, 0).reshape(128, KC))
    pp, ff = np.meshgrid(np.arange(128), np.arange(NTOK), indexing="ij")
    mask_t = ((pp % 64) <= (ff % 64)).astype(np.float32)
    return {
        "x_t": x_t,
        "wq_t": np.ascontiguousarray(np.asarray(L0_attn_WQ, np.float32).T),
        "wk_t": np.ascontiguousarray(np.asarray(L0_attn_WK, np.float32).T),
        "wv_t": np.ascontiguousarray(np.asarray(L0_attn_WV, np.float32).T),
        "wg_t": np.ascontiguousarray(np.asarray(L0_attn_Wg, np.float32).T),
        "wo_t": np.ascontiguousarray(np.asarray(L0_attn_WO, np.float32).T),
        "wg1_t": np.ascontiguousarray(np.asarray(L0_ffn_w_gate, np.float32).T),
        "wu1_t": np.ascontiguousarray(np.asarray(L0_ffn_w_up, np.float32).T),
        "wd1_t": np.ascontiguousarray(np.asarray(L0_ffn_w_down, np.float32).T),
        "bq_c": colform(L0_attn_bQ), "bk_c": colform(L0_attn_bK),
        "bv_c": colform(L0_attn_bV), "bg_c": colform(L0_attn_bg),
        "bo_c": colform(L0_attn_bO),
        "gamma_c": colform(L0_attn_ln_gamma), "beta_c": colform(L0_attn_ln_beta),
        "alpha_c": alpha_c,
        "bv_row": np.ascontiguousarray(
            np.asarray(L0_attn_bV, np.float32).reshape(1, DI)),
        "mask_t": mask_t,
        "ones_col": np.ones((128, 1), np.float32),
        "ones_row": np.ones((1, 128), np.float32),
        "ident64": np.eye(64, dtype=np.float32),
    }


def kernel(**inputs):
    nc, _ = build_program()
    in_maps = [_host_inputs(c, **inputs) for c in range(8)]
    res = run_bass_kernel_spmd(nc, in_maps, core_ids=list(range(8)))
    outs = []
    for c in range(8):
        ot = np.asarray(res.results[c]["out_t"])
        oc = (ot.reshape(D, NT, BC, C).transpose(0, 1, 3, 2)
              .reshape(D, T, BC).transpose(2, 1, 0))
        outs.append(oc)
    return np.ascontiguousarray(np.concatenate(outs, axis=0)).astype(np.float32)


# version 12
# speedup vs baseline: 1.0550x; 1.0174x over previous
import contextlib
import numpy as np

import concourse.bass as bass
import concourse.mybir as mybir
from concourse.bass_utils import run_bass_kernel_spmd

F32 = mybir.dt.float32
F16 = mybir.dt.float16
BF16 = mybir.dt.bfloat16
FP8 = mybir.dt.float8e4
ALU = mybir.AluOpType
ACT = mybir.ActivationFunctionType
AX = mybir.AxisListType

D = 1024
DI = 1024
BC = 4
T = 2048
C = 64
NT = T // C
NTOK = BC * C
KC = D // 128
FF = 4096
OC = FF // 128
LN_EPS = 1e-5

ENGINES = ("sync", "tensor", "vector", "scalar", "gpsimd")


class Plan:

    def __init__(self, nc):
        self.nc = nc
        self.ops = {e: [] for e in ENGINES}
        self.cnt = {}
        self.sem_objs = {}
        self.last_wait = {e: {} for e in ENGINES}
        self.own = {"sync": None, "tensor": "PE", "vector": "V",
                    "scalar": "S", "gpsimd": "G"}

    def add_sem(self, name, obj):
        self.sem_objs[name] = obj
        self.cnt[name] = 0

    def emit(self, eng, fn, waits=(), inc=None, inc_n=1):
        rw = []
        for w in waits:
            if w is None:
                continue
            s, v = w
            if v <= 0:
                continue
            assert v <= self.cnt[s], f"wait on future {s}:{v} > {self.cnt[s]}"
            if s == self.own[eng]:
                continue
            if self.last_wait[eng].get(s, -1) >= v:
                continue
            rw.append((s, v))
            self.last_wait[eng][s] = v
        tok = None
        if inc is not None:
            self.cnt[inc] += inc_n
            tok = (inc, self.cnt[inc])
        self.ops[eng].append((fn, rw, (inc, inc_n) if inc else None))
        return tok

    def ck(self, sem):
        return (sem, self.cnt[sem])

    def build(self, block):
        sem_objs = self.sem_objs
        for eng in ENGINES:
            ops = self.ops[eng]
            if not ops:
                continue

            def make(ops):
                def run(e):
                    for fn, waits, inc in ops:
                        for s, v in waits:
                            e.wait_ge(sem_objs[s], v)
                        r = fn(e)
                        if inc is not None:
                            s, n = inc
                            r.then_inc(sem_objs[s], n)
                return run
            getattr(block, eng)(make(ops))


def build_program(dump=(), nta=NT, ntb=NT, pha=99):
    nc = bass.Bass("TRN2", target_bir_lowering=False,
                   detect_race_conditions=False)
    dump = set(dump)

    x_t = nc.dram_tensor("x_t", [D, NT, NTOK], F32, kind="ExternalInput").ap()
    w_in = {}
    for w in ("wq", "wk", "wv", "wg", "wo"):
        w_in[w] = nc.dram_tensor(w + "_t", [D, DI], F32, kind="ExternalInput").ap()
    wg1_t = nc.dram_tensor("wg1_t", [D, FF], F32, kind="ExternalInput").ap()
    wu1_t = nc.dram_tensor("wu1_t", [D, FF], F32, kind="ExternalInput").ap()
    wd1_t = nc.dram_tensor("wd1_t", [FF, D], F32, kind="ExternalInput").ap()
    b_in = {}
    for b in ("bq", "bk", "bv", "bg", "bo"):
        b_in[b] = nc.dram_tensor(b + "_c", [128, KC], F32, kind="ExternalInput").ap()
    gamma_c = nc.dram_tensor("gamma_c", [128, KC], F32, kind="ExternalInput").ap()
    beta_c = nc.dram_tensor("beta_c", [128, KC], F32, kind="ExternalInput").ap()
    alpha_in = nc.dram_tensor("alpha_c", [128, KC], F32, kind="ExternalInput").ap()
    bv_row = nc.dram_tensor("bv_row", [1, DI], F32, kind="ExternalInput").ap()
    mask_in = nc.dram_tensor("mask_t", [128, NTOK], F32, kind="ExternalInput").ap()
    onesc_in = nc.dram_tensor("ones_col", [128, 1], F32, kind="ExternalInput").ap()
    onesr_in = nc.dram_tensor("ones_row", [1, 128], F32, kind="ExternalInput").ap()
    ident_in = nc.dram_tensor("ident64", [64, 64], F32, kind="ExternalInput").ap()
    out_t = nc.dram_tensor("out_t", [D, NT, NTOK], F32, kind="ExternalOutput").ap()

    whi_s = nc.dram_tensor("whi_s", [5, 128, KC, DI], F16, kind="Internal").ap()
    wlo_s = nc.dram_tensor("wlo_s", [5, 128, KC, DI], BF16, kind="Internal").ap()
    tg_s = nc.dram_tensor("tg_s", [128, KC, FF], FP8, kind="Internal").ap()
    tu_s = nc.dram_tensor("tu_s", [128, KC, FF], FP8, kind="Internal").ap()
    td_s = nc.dram_tensor("td_s", [128, OC, D], FP8, kind="Internal").ap()
    i1_s = nc.dram_tensor("i1_s", [NT, 128, KC, NTOK], BF16, kind="Internal").ap()
    x2_s = nc.dram_tensor("x2_s", [NT, 128, KC, NTOK], F32, kind="Internal").ap()
    iv1_s = nc.dram_tensor("iv1_s", [NT, 1, NTOK], F32, kind="Internal").ap()

    dumps = {}
    def dump_out(name, shape, dtype=F32):
        if name in dump and name not in dumps:
            dumps[name] = nc.dram_tensor("dbg_" + name, shape, dtype,
                                         kind="ExternalOutput").ap()
        return dumps.get(name)

    es = contextlib.ExitStack()
    with es:
        p = Plan(nc)
        for s in ("dS", "dG", "PE", "V", "S", "G"):
            p.add_sem(s, es.enter_context(nc.semaphore(s)))

        def sb(name, shape, dt=F32):
            return es.enter_context(nc.sbuf_tensor(name, shape, dt))

        pbW = [es.enter_context(nc.psum_tensor("pbw%d" % i, [128, 1024], F32))
               for i in range(4)]
        pb = [pbW[i // 2][:, (i % 2) * 512:(i % 2 + 1) * 512] for i in range(8)]
        bias = {b: sb(b, [128, KC]) for b in ("bq", "bk", "bv", "bg", "bo")}
        gam = sb("gam", [128, KC]); bet = sb("bet", [128, KC])
        alpha = sb("alpha", [128, KC])
        a64 = sb("a64", [128, KC])
        mghat = sb("mghat", [128, 3])
        mask = sb("mask", [128, NTOK])
        ones_c = sb("ones_c", [128, 1])
        ones_r = sb("ones_r", [1, 128])
        ident = sb("ident", [128, 64])
        eps1 = sb("eps1", [1, 1])
        rows = sb("rows", [1, 512])
        rows2 = sb("rows2", [1, 512])
        negc = sb("negc", [128, 8])
        aq = sb("aq", [128, KC, 64])
        ak = sb("ak", [128, KC, 64])
        bvbc = sb("bvbc", [128, DI])

        def dma(eng, out, in_, waits=()):
            return p.emit(eng, lambda e, o=out, i=in_: e.dma_start(out=o, in_=i),
                          waits=waits,
                          inc="dS" if eng in ("sync", "scalar") else "dG", inc_n=16)

        for dst, src in [(mask[:, :], mask_in[:, :]),
                         (ones_c[:, :], onesc_in[:, :]),
                         (ones_r[:, :], onesr_in[:, :]),
                         (ident[0:64, :], ident_in[:, :]),
                         (ident[64:128, :], ident_in[:, :]),
                         (gam[:, :], gamma_c[:, :]),
                         (bet[:, :], beta_c[:, :]),
                         (alpha[:, :], alpha_in[:, :])]:
            dma("sync", dst, src)
        for b in bias:
            dma("sync", bias[b][:, :], b_in[b][:, :])
        t_consts = p.ck("dS")

        p.emit("vector", lambda e: e.memset(eps1[:, :], LN_EPS), waits=[], inc="V")
        for k in range(8):
            cval = -(127.5 - 2.0 ** -17)
            p.emit("vector", lambda e, k=k, c=cval: e.memset(negc[:, k:k + 1], c),
                   waits=[], inc="V")
        tok_alpha = p.emit("scalar", lambda e: e.activation(
            alpha[:, :], alpha[:, :], ACT.Sigmoid), waits=[t_consts], inc="S")

        esp = contextlib.ExitStack()
        def sbp(name, shape, dt=F32):
            return esp.enter_context(nc.sbuf_tensor(name, shape, dt))
        apow = sbp("apow", [128, 64])
        apneg = sbp("apneg", [128, 64])
        acols = sbp("acols", [128, 16])
        def vdrain():
            p.emit("vector", lambda e: e.drain(), waits=[], inc=None)

        for c in range(KC):
            al = alpha[:, c:c + 1]
            def em(fn, waits=()):
                tok = p.emit("vector", fn, waits=waits, inc="V")
                vdrain()
                return tok
            w0 = [tok_alpha]
            em(lambda e, al=al: e.tensor_tensor(acols[:, 0:1], al, al, ALU.mult), w0)
            for j in range(1, 5):
                em(lambda e, j=j: e.tensor_tensor(
                    acols[:, j:j + 1], acols[:, j - 1:j], acols[:, j - 1:j], ALU.mult))
            em(lambda e, c=c: e.tensor_tensor(
                a64[:, c:c + 1], acols[:, 4:5], acols[:, 4:5], ALU.mult))
            em(lambda e, al=al: e.reciprocal(acols[:, 5:6], al), w0)
            for j in range(6, 11):
                em(lambda e, j=j: e.tensor_tensor(
                    acols[:, j:j + 1], acols[:, j - 1:j], acols[:, j - 1:j], ALU.mult))
            em(lambda e: e.memset(apow[:, 0:1], 1.0))
            em(lambda e, al=al: e.tensor_copy(apow[:, 1:2], al), w0)
            for j in range(1, 6):
                n = 2 ** j
                em(lambda e, n=n, j=j: e.tensor_scalar(
                    apow[:, n:2 * n], apow[:, 0:n], acols[:, j - 1:j], None, ALU.mult))
            em(lambda e: e.memset(apneg[:, 0:1], 1.0))
            em(lambda e: e.tensor_copy(apneg[:, 1:2], acols[:, 5:6]))
            for j in range(1, 6):
                n = 2 ** j
                em(lambda e, n=n, j=j: e.tensor_scalar(
                    apneg[:, n:2 * n], apneg[:, 0:n], acols[:, 5 + j:6 + j], None,
                    ALU.mult))
            em(lambda e, c=c: e.tensor_scalar(
                aq[:, c, :], apow[:, :], acols[:, 10:11], None, ALU.mult))
            em(lambda e, c=c: e.tensor_scalar(
                ak[:, c, :], apneg[:, :], acols[:, 4:5], None, ALU.mult))
        tok_atiles = p.ck("V")

        bvrow_sb = sbp("bvrow_sb", [1, DI])
        t_bv = dma("sync", bvrow_sb[:, :], bv_row[:, :])
        t1 = p.emit("tensor", lambda e: e.matmul(
            pb[0][:, 0:512], ones_r[:, :], bvrow_sb[:, 0:512],
            start=True, stop=True), waits=[t_bv, t_consts], inc="PE")
        t2 = p.emit("tensor", lambda e: e.matmul(
            pb[1][:, 0:512], ones_r[:, :], bvrow_sb[:, 512:1024],
            start=True, stop=True), waits=[t_bv], inc="PE")
        p.emit("vector", lambda e: e.tensor_copy(bvbc[:, 0:512], pb[0][:, 0:512]),
               waits=[t1], inc="V")
        t_bvbc = p.emit("vector", lambda e: e.tensor_copy(
            bvbc[:, 512:1024], pb[1][:, 0:512]), waits=[t2], inc="V")

        wtmp = sbp("wtmp", [128, 2, DI])
        whtmp = sbp("whtmp", [128, 2, DI], F16)
        wltmp = sbp("wltmp", [128, 2, DI], BF16)
        sfree = [None, None]
        for wi, w in enumerate(("wq", "wk", "wv", "wg", "wo")):
            for k in range(KC):
                par = (wi * KC + k) % 2
                tin = dma("sync", wtmp[:, par, :], w_in[w][k * 128:(k + 1) * 128, :],
                          waits=[sfree[par]])
                th = p.emit("vector", lambda e, par=par: e.tensor_copy(
                    whtmp[:, par, :], wtmp[:, par, :]), waits=[tin], inc="V")
                tl = p.emit("vector", lambda e, par=par: e.tensor_tensor(
                    wltmp[:, par, :], wtmp[:, par, :], whtmp[:, par, :],
                    ALU.subtract), waits=[], inc="V")
                dma("sync", whi_s[wi, :, k, :], whtmp[:, par, :], waits=[th])
                sfree[par] = dma("sync", wlo_s[wi, :, k, :], wltmp[:, par, :],
                                 waits=[tl])

        ftmp = sbp("ftmp", [128, 2, FF])
        fb1 = sbp("fb1", [128, FF])
        ftern = sbp("ftern", [128, FF], FP8)
        redc = sbp("redc", [128, 4])
        foldt = sbp("foldt", [64, NTOK])
        for wi, (src, dst, kn, cols) in enumerate(
                [(wg1_t, tg_s, KC, FF), (wu1_t, tu_s, KC, FF), (wd1_t, td_s, OC, D)]):
            tz = p.emit("vector", lambda e: e.memset(redc[:, 0:1], 0.0),
                        waits=[], inc="V")
            lin = [None, None]
            for k in range(kn):
                par = k % 2
                tin = dma("sync", ftmp[:, par, 0:cols],
                          src[k * 128:(k + 1) * 128, :], waits=[lin[par]])
                p.emit("vector", lambda e, par=par, cols=cols: e.tensor_reduce(
                    redc[:, 1:2], ftmp[:, par, 0:cols], AX.X, ALU.add,
                    apply_absolute_value=True), waits=[tin, tz], inc="V")
                vdrain()
                lin[par] = p.emit("vector", lambda e: e.tensor_tensor(
                    redc[:, 0:1], redc[:, 0:1], redc[:, 1:2], ALU.add),
                    waits=[], inc="V")
                vdrain()
            for half in (64, 32):
                p.emit("vector", lambda e, h=half: e.tensor_copy(
                    foldt[0:h, 0:1], redc[h:2 * h, 0:1]), waits=[], inc="V")
                vdrain()
                p.emit("vector", lambda e, h=half: e.tensor_tensor(
                    redc[0:h, 0:1], redc[0:h, 0:1], foldt[0:h, 0:1], ALU.add),
                    waits=[], inc="V")
                vdrain()
            for half in (16, 8, 4, 2, 1):
                msk = [(i + half) % 32 for i in range(32)]
                p.emit("vector", lambda e, m=msk: e.stream_shuffle(
                    foldt[0:32, 0:1], redc[0:32, 0:1], m), waits=[], inc="V")
                vdrain()
                p.emit("vector", lambda e, h=half: e.tensor_tensor(
                    redc[0:h, 0:1], redc[0:h, 0:1], foldt[0:h, 0:1], ALU.add),
                    waits=[], inc="V")
                vdrain()
            p.emit("vector", lambda e, n=float(kn * 128 * cols): e.tensor_scalar(
                rows[0:1, 0:1], redc[0:1, 0:1], 1.0 / n, 1e-5, ALU.mult, ALU.max),
                waits=[], inc="V")
            vdrain()
            tsw = p.emit("vector", lambda e: e.reciprocal(
                rows[0:1, 1:2], rows[0:1, 0:1]), waits=[], inc="V")
            vdrain()
            tm1 = p.emit("tensor", lambda e: e.matmul(
                pb[0][:, 0:1], ones_r[:, :], rows[0:1, 0:1], start=True, stop=True),
                waits=[tsw, t_bvbc], inc="PE")
            tm2 = p.emit("tensor", lambda e: e.matmul(
                pb[0][:, 1:2], ones_r[:, :], rows[0:1, 1:2], start=True, stop=True),
                waits=[], inc="PE")
            p.emit("vector", lambda e, wi=wi: e.tensor_copy(
                mghat[:, wi:wi + 1], pb[0][:, 0:1]), waits=[tm1, tm2], inc="V")
            tswc = p.emit("vector", lambda e: e.tensor_copy(
                redc[:, 2:3], pb[0][:, 1:2]), waits=[], inc="V")
            lin = [None, None]
            for k in range(kn):
                par = k % 2
                tin = dma("sync", ftmp[:, par, 0:cols],
                          src[k * 128:(k + 1) * 128, :], waits=[lin[par], tswc])
                p.emit("vector", lambda e, par=par, cols=cols: e.tensor_scalar(
                    fb1[:, 0:cols], ftmp[:, par, 0:cols], redc[:, 2:3], 0.5,
                    ALU.mult, ALU.is_gt), waits=[tin], inc="V")
                p.emit("vector", lambda e, par=par, cols=cols: e.tensor_scalar(
                    ftmp[:, par, 0:cols], ftmp[:, par, 0:cols], redc[:, 2:3], -0.5,
                    ALU.mult, ALU.is_lt), waits=[], inc="V")
                t3 = p.emit("vector", lambda e, par=par, cols=cols: e.tensor_tensor(
                    ftern[:, 0:cols], fb1[:, 0:cols], ftmp[:, par, 0:cols],
                    ALU.subtract), waits=[], inc="V")
                lin[par] = dma("sync", dst[:, k, :], ftern[:, 0:cols], waits=[t3])
        t_prelude_dS = p.ck("dS")
        esp.close()

        barrier = {s: p.ck(s) for s in ("PE", "V", "S", "dG")}

        with contextlib.ExitStack() as esa:
            def sba(name, shape, dt=F32):
                return esa.enter_context(nc.sbuf_tensor(name, shape, dt))

            xsb = sba("xsb", [128, 1, KC, NTOK])
            sq = sba("sq", [128, 2, NTOK])
            xnhi = sba("xnhi", [128, KC, NTOK], F16)
            xnlo = sba("xnlo", [128, KC, NTOK], BF16)
            qf = sba("qf", [128, KC, NTOK])
            kf = sba("kf", [128, KC, NTOK])
            gs = sba("gs", [128, KC, NTOK])
            vtm = sba("vtm", [128, 2, DI])
            ktm = sba("ktm", [128, 2, DI])
            pmk = sba("pmk", [128, NTOK])
            o1sb = sba("o1sb", [128, NTOK])
            srm = sba("srm", [128, NTOK])
            ohi = sba("ohi", [128, KC, NTOK], F16)
            olo = sba("olo", [128, KC, NTOK], BF16)
            ssb = sba("ssb", [128, KC, NTOK])
            x2sb = sba("x2sb", [128, 2, KC, NTOK])
            i1sb = sba("i1sb", [128, KC, NTOK], BF16)
            accp = sba("accp", [128, NTOK])
            accn = sba("accn", [128, NTOK])
            am = sba("am", [128, NTOK])
            amf = sba("amf", [64, NTOK])
            rbc_sb = sba("rbc_sb", [128, NTOK])
            bsb = sba("bsb", [128, 512])
            ufin = sba("ufin", [128, 512])
            hp = [sba("hp0", [128, KC, NTOK]), sba("hp1", [128, KC, NTOK])]
            wbh = sba("wbh", [128, 2, KC, DI], F16)
            wbl = sba("wbl", [128, 2, KC, DI], BF16)

            th0 = p.emit("vector", lambda e: e.memset(hp[0][:, :, :], 0.0),
                         waits=[t_prelude_dS], inc="V")
            p.emit("vector", lambda e: e.memset(hp[1][:, :, :], 0.0),
                   waits=[], inc="V")

            WROW = (2, 0, 1, 3, 4)
            w_free = {}
            w_ready = {}

            def prefetch_w(m):
                if m >= nta * 5 or m in w_ready:
                    return
                par = m % 2
                wi = WROW[m % 5]
                p.emit("sync", lambda e, wi=wi, par=par: e.dma_start(
                    out=wbh[:, par, :, :], in_=whi_s[wi, :, :, :]),
                    waits=[w_free.get(m - 2), t_prelude_dS], inc="dS", inc_n=16)
                w_ready[m] = p.emit("sync", lambda e, wi=wi, par=par: e.dma_start(
                    out=wbl[:, par, :, :], in_=wlo_s[wi, :, :, :]),
                    waits=[], inc="dS", inc_n=16)

            x_free = {}
            x_ready = {}

            def prefetch_x(t):
                if t >= nta or t in x_ready:
                    return
                x_ready[t] = p.emit("sync", lambda e, t=t: e.dma_start(
                    out=xsb[:, 0, :, :],
                    in_=x_t[:, t, :].rearrange("(k p) n -> p k n", p=128),
                    ), waits=[x_free.get(t - 1), t_prelude_dS],
                    inc="dS", inc_n=16)

            prefetch_w(0); prefetch_w(1)
            prefetch_x(0)

            prev_store = None
            defer_ctx = {}

            def emit_deferred(dt, piece=None):
                nonlocal prev_store, ti1_last, prev_iv, prev_st1
                ctx = defer_ctx[dt]
                dpar = ctx["par"]
                x2c = lambda c: x2sb[:, dpar, c, :]
                if piece is not None and piece > 0:
                    tbs1 = ctx["tbs1"]
                    pairs = {1: [0], 2: [2], 3: [4, 6]}[piece]
                    for cc in pairs:
                        emit_ladder_pair(dt, ctx, cc, x2c, tbs1)
                    return
                fold_last = None
                for c in range(KC):
                    if c == 0:
                        p.emit("vector", lambda e, c=c: e.tensor_copy(
                            accp[:, :], x2c(c)), waits=[ctx["x2_toks"][0]],
                            inc="V")
                        fold_last = p.emit("vector", lambda e, c=c: e.tensor_copy(
                            accn[:, :], x2c(c)), waits=[], inc="V")
                    else:
                        p.emit("vector", lambda e, c=c: e.tensor_tensor(
                            accp[:, :], accp[:, :], x2c(c), ALU.max),
                            waits=[ctx["x2_toks"][c]], inc="V")
                        fold_last = p.emit("vector", lambda e, c=c: e.tensor_tensor(
                            accn[:, :], accn[:, :], x2c(c), ALU.min),
                            waits=[], inc="V")
                p.emit("vector", lambda e: e.scalar_tensor_tensor(
                    am[:, :], accn[:, :], -1.0, accp[:, :], ALU.mult, ALU.max),
                    waits=[fold_last], inc="V")
                for half in (64, 32):
                    p.emit("vector", lambda e, h=half: e.tensor_copy(
                        amf[0:h, :], am[h:2 * h, :]), waits=[], inc="V")
                    p.emit("vector", lambda e, h=half: e.tensor_tensor(
                        am[0:h, :], am[0:h, :], amf[0:h, :], ALU.max),
                        waits=[], inc="V")
                for half in (16, 8, 4, 2, 1):
                    msk = [(i + half) % 32 for i in range(32)]
                    p.emit("vector", lambda e, m=msk: e.stream_shuffle(
                        amf[0:32, :], am[0:32, :], m), waits=[], inc="V")
                    p.emit("vector", lambda e, h=half: e.tensor_tensor(
                        am[0:h, :], am[0:h, :], amf[0:h, :], ALU.max),
                        waits=[], inc="V")
                p.emit("vector", lambda e: e.tensor_scalar(
                    rows[0:1, 0:NTOK], am[0:1, :], 1e-5, None, ALU.max),
                    waits=[], inc="V")
                p.emit("vector", lambda e: e.reciprocal(
                    rows[0:1, NTOK:2 * NTOK], rows[0:1, 0:NTOK]),
                    waits=[], inc="V")
                ts1 = p.emit("vector", lambda e: e.tensor_scalar(
                    rows2[0:1, 0:NTOK], rows[0:1, NTOK:2 * NTOK], 127.0, None,
                    ALU.mult), waits=[], inc="V")
                tiv = p.emit("vector", lambda e: e.tensor_scalar(
                    rows2[0:1, NTOK:2 * NTOK], rows[0:1, 0:NTOK], 1.0 / 127.0,
                    None, ALU.mult), waits=[], inc="V")
                tbs1 = p.emit("tensor", lambda e: e.matmul(
                    pb[1][:, 0:NTOK], ones_r[:, :], rows2[0:1, 0:NTOK],
                    start=True, stop=True), waits=[ts1], inc="PE")
                tivd = p.emit("gpsimd", lambda e, dt=dt: e.dma_start(
                    out=iv1_s[dt, :, :], in_=rows2[0:1, NTOK:2 * NTOK]),
                    waits=[tiv], inc="dG", inc_n=16)
                prev_iv = tivd
                ctx["tbs1"] = tbs1
                ctx["fold_last"] = fold_last
                if piece is None:
                    for cc in range(0, KC, 2):
                        emit_ladder_pair(dt, ctx, cc, x2c, tbs1)

            def emit_ladder_pair(dt, ctx, cc, x2c, tbs1):
                nonlocal prev_store, prev_st1, ti1_last
                u0v = pb[5][:, 0:512]
                pgA = pb[6][:, 0:512]
                pgB = pb[7][:, 0:512]
                p.emit("vector", lambda e, cc=cc, u0v=u0v: e.tensor_tensor(
                    u0v[:, 0:NTOK], x2c(cc), pb[1][:, 0:NTOK], ALU.mult),
                    waits=[tbs1, prev_st1 if cc == 0 else None], inc="V")
                p.emit("vector", lambda e, cc=cc, u0v=u0v: e.tensor_tensor(
                    u0v[:, NTOK:2 * NTOK], x2c(cc + 1), pb[1][:, 0:NTOK],
                    ALU.mult), waits=[], inc="V")
                p.emit("vector", lambda e, u0v=u0v: e.tensor_scalar(
                    u0v, u0v, 128.0, None, ALU.add), waits=[], inc="V")
                srcv = u0v
                for ki, k in enumerate(range(7, -1, -1)):
                    cth = float(2 ** k) - 0.5
                    p.emit("vector", lambda e, srcv=srcv, cth=cth, k=k:
                           e.tensor_scalar(
                        bsb[:, :], srcv, cth, -float(2 ** k),
                        ALU.is_ge, ALU.mult), waits=[], inc="V")
                    dstv = ufin[:, :] if k == 0 else (pgA if ki % 2 == 0 else pgB)
                    p.emit("vector", lambda e, srcv=srcv, dstv=dstv:
                           e.tensor_tensor(
                        dstv, bsb[:, :], srcv, ALU.add), waits=[], inc="V")
                    srcv = dstv
                p.emit("vector", lambda e, u0v=u0v: e.scalar_tensor_tensor(
                    bsb[:, :], ufin[:, :], -1.0, u0v, ALU.mult, ALU.add),
                    waits=[], inc="V")
                p.emit("vector", lambda e, cc=cc: e.tensor_scalar(
                    i1sb[:, cc, :], bsb[:, 0:NTOK], -128.0, None, ALU.add),
                    waits=[], inc="V")
                ti1 = p.emit("vector", lambda e, cc=cc: e.tensor_scalar(
                    i1sb[:, cc + 1, :], bsb[:, NTOK:2 * NTOK], -128.0, None,
                    ALU.add), waits=[], inc="V")
                ti1_last = ti1
                if cc == KC - 2:
                    defer_ctx.pop(dt)
                    prev_st1 = p.emit("gpsimd", lambda e, dt=dt: e.dma_start(
                        out=i1_s[dt, :, :, :], in_=i1sb[:, :, :]), waits=[ti1],
                        inc="dG", inc_n=16)
                    prev_store = p.emit("gpsimd", lambda e, dt=dt: e.dma_start(
                        out=x2_s[dt, :, :, :], in_=x2sb[:, ctx["par"], :, :]),
                        waits=[ctx["fold_last"]], inc="dG", inc_n=16)

            ti1_last = None
            prev_iv = None
            prev_st1 = None
            prev_sqpe = None
            pb01_free = None
            prev_S = None
            prev_tlo = None
            wo_v_ck = None
            for t in range(nta):
                par = t % 2
                bar = []
                xc = lambda c: xsb[:, 0, c, :]
                tx = x_ready[t]

                if pha < 1:
                    barrier = {s: p.ck(s) for s in ("PE", "V", "S", "dG")}
                    continue
                sq_pe = []
                mm1_last = None
                for c in range(KC):
                    tsqr = p.emit("scalar", lambda e, c=c: e.activation(
                        sq[:, c % 2, :], xc(c), ACT.Square),
                        waits=[tx, (sq_pe[c - 2] if c >= 2 else prev_sqpe)],
                        inc="S")
                    mm1_last = p.emit("tensor", lambda e, c=c: e.matmul(
                        pb[0][0:1, 0:NTOK], ones_c[:, :], xc(c),
                        start=(c == 0), stop=(c == KC - 1), skip_group_check=True),
                        waits=[tx, t_consts, pb01_free],
                        inc="PE" if c == KC - 1 else None)
                    sq_pe.append(p.emit("tensor", lambda e, c=c: e.matmul(
                        pb[7][0:1, 0:NTOK], ones_c[:, :], sq[:, c % 2, :],
                        start=(c == 0), stop=(c == KC - 1), skip_group_check=True), waits=[tsqr], inc="PE"))

                def ln_rows(tok_mm, prev_dma=None):
                    tmu = p.emit("vector", lambda e: e.tensor_scalar(
                        rows[0:1, 0:NTOK], pb[0][0:1, 0:NTOK], 1.0 / D, None,
                        ALU.mult), waits=[tok_mm, prev_dma], inc="V")
                    p.emit("vector", lambda e: e.tensor_scalar(
                        rows[0:1, NTOK:2 * NTOK], pb[7][0:1, 0:NTOK],
                        1.0 / D, None, ALU.mult), waits=[], inc="V")
                    p.emit("vector", lambda e: e.tensor_tensor(
                        rows2[0:1, 0:NTOK], rows[0:1, 0:NTOK], rows[0:1, 0:NTOK],
                        ALU.mult), waits=[], inc="V")
                    tvar = p.emit("vector", lambda e: e.tensor_tensor(
                        rows2[0:1, NTOK:2 * NTOK], rows[0:1, NTOK:2 * NTOK],
                        rows2[0:1, 0:NTOK], ALU.subtract), waits=[], inc="V")
                    tsq = p.emit("scalar", lambda e: e.activation(
                        rows2[0:1, 0:NTOK], rows2[0:1, NTOK:2 * NTOK], ACT.Sqrt,
                        bias=eps1[:, :]), waits=[tvar], inc="S")
                    trr = p.emit("vector", lambda e: e.reciprocal(
                        rows[0:1, NTOK:2 * NTOK], rows2[0:1, 0:NTOK]),
                        waits=[tsq], inc="V")
                    tbc1 = p.emit("tensor", lambda e: e.matmul(
                        pb[0][:, 0:NTOK], ones_r[:, :], rows[0:1, 0:NTOK],
                        start=True, stop=True), waits=[tmu, trr], inc="PE")
                    tbc2 = p.emit("tensor", lambda e: e.matmul(
                        pb[0][:, NTOK:2 * NTOK], ones_r[:, :],
                        rows[0:1, NTOK:2 * NTOK], start=True, stop=True),
                        waits=[trr], inc="PE")
                    trbc = p.emit("vector", lambda e: e.tensor_copy(
                        rbc_sb[:, :], pb[0][:, NTOK:2 * NTOK]), waits=[tbc2],
                        inc="V")
                    return tbc1, tbc2, trbc

                tbc1, tbc2, trbc = ln_rows(sq_pe[-1], prev_iv)
                if pha < 2:
                    barrier = {s: p.ck(s) for s in ("PE", "V", "S", "dG")}
                    continue

                for c in range(KC):
                    reg = pb[2] if c % 2 == 0 else pb[3]
                    p.emit("vector", lambda e, c=c, reg=reg: e.tensor_tensor(
                        reg[:, 0:NTOK], xc(c), pb[0][:, 0:NTOK], ALU.subtract),
                        waits=[tbc2], inc="V")
                    p.emit("vector", lambda e, reg=reg: e.tensor_tensor(
                        reg[:, NTOK:2 * NTOK], reg[:, 0:NTOK], rbc_sb[:, :],
                        ALU.mult), waits=[trbc], inc="V")
                    p.emit("vector", lambda e, c=c, reg=reg: e.tensor_scalar(
                        xnhi[:, c, :], reg[:, NTOK:2 * NTOK], gam[:, c:c + 1],
                        bet[:, c:c + 1], ALU.mult, ALU.add), waits=[], inc="V")
                    p.emit("vector", lambda e, c=c, reg=reg: e.tensor_scalar(
                        reg[:, 0:NTOK], reg[:, NTOK:2 * NTOK], gam[:, c:c + 1],
                        bet[:, c:c + 1], ALU.mult, ALU.add), waits=[], inc="V")
                    p.emit("vector", lambda e, c=c, reg=reg: e.tensor_tensor(
                        xnlo[:, c, :], reg[:, 0:NTOK], xnhi[:, c, :], ALU.subtract),
                        waits=[], inc="V")
                xn_done = p.ck("V")

                if pha < 3:
                    barrier = {s: p.ck(s) for s in ("PE", "V", "S", "dG")}
                    continue
                regions = [pb[2][:, 0:NTOK], pb[3][:, 0:NTOK],
                           pb[4][:, 0:NTOK]]
                regions2 = [pb[2][:, NTOK:2 * NTOK], pb[3][:, NTOK:2 * NTOK],
                            pb[4][:, NTOK:2 * NTOK]]
                reg_free = [None, None, None]
                reg_i = [0]

                def next_region():
                    i = reg_i[0] % 3
                    reg_i[0] += 1
                    return i, regions[i]

                def proj3(reg, stat_hi, stat_lo, mov_hi, mov_lo, first_waits):
                    last = None
                    for k in range(KC):
                        fst = (k == 0)
                        lst = (k == KC - 1)
                        p.emit("tensor", lambda e, k=k, reg=reg, sh=stat_hi,
                               mh=mov_hi, fst=fst: e.matmul(
                            reg, sh(k), mh(k), start=fst, stop=False),
                            waits=first_waits if fst else [], inc=None)
                        p.emit("tensor", lambda e, k=k, reg=reg, sh=stat_hi,
                               ml=mov_lo: e.matmul(
                            reg, sh(k), ml(k), start=False, stop=False),
                            waits=[], inc=None)
                        last = p.emit("tensor", lambda e, k=k, reg=reg, sl=stat_lo,
                                      mh=mov_hi, lst=lst: e.matmul(
                            reg, sl(k), mh(k), start=False, stop=lst),
                            waits=[], inc="PE" if lst else None)
                    return last

                m_v = t * 5 + 0
                wv_par, wv_tok = m_v % 2, w_ready[m_v]
                vt_ev = []
                for e2 in range(2):
                    for hf in range(2):
                        i = (e2 * 2 + hf) % 2
                        regfull = pb[4] if i == 0 else pb[5]
                        last = proj3(
                            regfull[:, 0:512],
                            lambda k, e2=e2: xnhi[:, k, e2 * 128:(e2 + 1) * 128],
                            lambda k, e2=e2: xnlo[:, k, e2 * 128:(e2 + 1) * 128],
                            lambda k, wv_par=wv_par, hf=hf:
                                wbh[:, wv_par, k, hf * 512:(hf + 1) * 512],
                            lambda k, wv_par=wv_par, hf=hf:
                                wbl[:, wv_par, k, hf * 512:(hf + 1) * 512],
                            [wv_tok, xn_done, wo_v_ck,
                             (vt_ev[-2] if len(vt_ev) >= 2 else None)])
                        vt_ev.append(p.emit("vector",
                            lambda e, e2=e2, hf=hf, regfull=regfull:
                            e.tensor_tensor(
                                vtm[:, e2, hf * 512:(hf + 1) * 512],
                                regfull[:, 0:512], bvbc[:, hf * 512:(hf + 1) * 512],
                                ALU.add), waits=[last, t_bvbc], inc="V"))
                vtm_done = p.ck("V")
                w_free[m_v] = p.ck("PE")
                prefetch_w(m_v + 2)

                if pha < 4:
                    barrier = {s: p.ck(s) for s in ("PE", "V", "S", "dG")}
                    continue
                reg_free[0] = reg_free[1] = xn_done
                reg_free[2] = vt_ev[3]
                reg_i[0] = 0
                ev_toks = {"q": [], "k": [], "g": []}
                for j, (nm, wn) in enumerate((("q", 1), ("k", 2), ("g", 3))):
                    m = t * 5 + wn
                    wpar, wtok = m % 2, w_ready[m]
                    for c in range(KC):
                        if nm == "q" and (t - 1) in defer_ctx:
                            if c == 0:
                                emit_deferred(t - 1, piece=0)
                            elif c in (2, 4, 6):
                                emit_deferred(t - 1, piece=c // 2)
                        ri, reg = next_region()
                        last = proj3(
                            reg,
                            lambda k, wpar=wpar, c=c:
                                wbh[:, wpar, k, c * 128:(c + 1) * 128],
                            lambda k, wpar=wpar, c=c:
                                wbl[:, wpar, k, c * 128:(c + 1) * 128],
                            lambda k: xnhi[:, k, :],
                            lambda k: xnlo[:, k, :],
                            [wtok, xn_done, reg_free[ri]])
                        if nm == "q":
                            for bb in range(BC):
                                tok = p.emit("vector", lambda e, c=c, reg=reg,
                                             bb=bb: e.scalar_tensor_tensor(
                                    qf[:, c, bb * 64:(bb + 1) * 64],
                                    reg[:, bb * 64:(bb + 1) * 64],
                                    bias["bq"][:, c:c + 1], aq[:, c, :],
                                    ALU.add, ALU.mult),
                                    waits=[last, tok_atiles], inc="V")
                        elif nm == "k":
                            for bb in range(BC):
                                tok = p.emit("vector", lambda e, c=c, reg=reg,
                                             bb=bb: e.scalar_tensor_tensor(
                                    kf[:, c, bb * 64:(bb + 1) * 64],
                                    reg[:, bb * 64:(bb + 1) * 64],
                                    bias["bk"][:, c:c + 1], ak[:, c, :],
                                    ALU.add, ALU.mult),
                                    waits=[last], inc="V")
                        else:
                            tok = p.emit("scalar", lambda e, c=c, reg=reg:
                                         e.activation(
                                gs[:, c, :], reg, ACT.Silu,
                                bias=bias["bg"][:, c:c + 1]),
                                waits=[last, prev_tlo], inc="S")
                        ev_toks[nm].append(tok)
                        reg_free[ri] = tok
                    w_free[m] = p.ck("PE")
                    prefetch_w(m + 2)
                    pass

                if pha < 5:
                    barrier = {s: p.ck(s) for s in ("PE", "V", "S", "dG")}
                    continue
                tr_banks = [pb[0], pb[6], pb[1], pb[7]]
                ktr_ev = []
                nidx = 0
                for c in range(KC):
                    for b4 in range(BC):
                        for hq in range(2):
                            bank = tr_banks[nidx % 4]
                            ttr = p.emit("tensor",
                                lambda e, c=c, b4=b4, hq=hq, bank=bank:
                                e.transpose(
                                    bank[0:64, NTOK:NTOK + 64],
                                    kf[hq * 64:hq * 64 + 64, c, b4 * 64:(b4 + 1) * 64],
                                    ident[hq * 64:hq * 64 + 64, :],
                                    tile_position=(hq * 64, 0)),
                                waits=[ev_toks["k"][c],
                                       (ktr_ev[-4] if len(ktr_ev) >= 4 else None)],
                                inc="PE")
                            ktr_ev.append(p.emit("vector",
                                lambda e, c=c, b4=b4, hq=hq, bank=bank:
                                e.tensor_copy(
                                    ktm[(b4 % 2) * 64:(b4 % 2) * 64 + 64, b4 // 2,
                                        c * 128 + hq * 64:c * 128 + hq * 64 + 64],
                                    bank[0:64, NTOK:NTOK + 64]),
                                waits=[ttr], inc="V"))
                            nidx += 1
                ktm_done = p.ck("V")

                if pha < 6:
                    barrier = {s: p.ck(s) for s in ("PE", "V", "S", "dG")}
                    continue
                prev_scan_v = None
                og_done = None
                tlo_last = None
                for c in range(KC):
                    pmm = None
                    for b4 in range(BC):
                        for hq in range(2):
                            bank = pb[6] if hq == 0 else pb[7]
                            pmm = p.emit("tensor", lambda e, c=c, b4=b4, hq=hq,
                                         bank=bank: e.matmul(
                                bank[(b4 % 2) * 64:(b4 % 2) * 64 + 64,
                                     (b4 // 2) * 64:(b4 // 2) * 64 + 64],
                                kf[hq * 64:hq * 64 + 64, c, b4 * 64:(b4 + 1) * 64],
                                qf[hq * 64:hq * 64 + 64, c, b4 * 64:(b4 + 1) * 64],
                                start=True, stop=True,
                                tile_position=(hq * 64, (b4 % 2) * 64)),
                                waits=[ev_toks["q"][c], ev_toks["k"][c],
                                       prev_scan_v, ktm_done], inc="PE")
                    pmkv = pmk[:, :].rearrange("p (a h t) -> p a h t", a=2, h=2)
                    maskv = mask[:, 0:128].rearrange("p (a t) -> p a t", a=2)
                    tpm0 = p.emit("vector", lambda e, pmkv=pmkv, maskv=maskv:
                                  e.tensor_tensor(
                        pmkv[:, :, 0, :],
                        pb[6][:, 0:128].rearrange("p (a t) -> p a t", a=2),
                        maskv, ALU.mult), waits=[pmm], inc="V")
                    tpm = p.emit("vector", lambda e, pmkv=pmkv, maskv=maskv:
                                 e.tensor_tensor(
                        pmkv[:, :, 1, :],
                        pb[7][:, 0:128].rearrange("p (a t) -> p a t", a=2),
                        maskv, ALU.mult), waits=[], inc="V")
                    omm = None
                    for b4 in range(BC):
                        for hq in range(2):
                            p.emit("tensor", lambda e, c=c, b4=b4, hq=hq: e.matmul(
                                pb[2][(b4 % 2) * 64:(b4 % 2) * 64 + 64,
                                      (b4 // 2) * 128 + hq * 64:
                                      (b4 // 2) * 128 + hq * 64 + 64],
                                vtm[(b4 % 2) * 64:(b4 % 2) * 64 + 64, b4 // 2,
                                    c * 128 + hq * 64:c * 128 + hq * 64 + 64],
                                pmk[(b4 % 2) * 64:(b4 % 2) * 64 + 64,
                                    (b4 // 2) * 128 + hq * 64:
                                    (b4 // 2) * 128 + hq * 64 + 64],
                                start=True, stop=True,
                                tile_position=((b4 % 2) * 64, (b4 % 2) * 64)),
                                waits=[tpm, vtm_done, og_done], inc=None)
                            omm = p.emit("tensor", lambda e, c=c, b4=b4, hq=hq,
                                         par=par: e.matmul(
                                pb[3][hq * 64:hq * 64 + 64, b4 * 64:(b4 + 1) * 64],
                                hp[par][hq * 64:hq * 64 + 64, c,
                                        b4 * 64:(b4 + 1) * 64],
                                qf[hq * 64:hq * 64 + 64, c, b4 * 64:(b4 + 1) * 64],
                                start=True, stop=True,
                                tile_position=(hq * 64, hq * 64)),
                                waits=[th0 if t == 0 else None], inc="PE")
                    smm = None
                    for b4 in range(BC):
                        for hq in range(2):
                            smm = p.emit("tensor", lambda e, c=c, b4=b4, hq=hq:
                                         e.matmul(
                                pb[4][(b4 % 2) * 64:(b4 % 2) * 64 + 64,
                                      (b4 // 2) * 128 + hq * 64:
                                      (b4 // 2) * 128 + hq * 64 + 64],
                                ktm[(b4 % 2) * 64:(b4 % 2) * 64 + 64, b4 // 2,
                                    c * 128 + hq * 64:c * 128 + hq * 64 + 64],
                                vtm[(b4 % 2) * 64:(b4 % 2) * 64 + 64, b4 // 2,
                                    c * 128 + hq * 64:c * 128 + hq * 64 + 64],
                                start=True, stop=True,
                                tile_position=((b4 % 2) * 64, (b4 % 2) * 64)),
                                waits=[], inc="PE")
                    cp = None
                    for b4 in range(BC):
                        for hq in range(2):
                            cp = p.emit("vector", lambda e, b4=b4, hq=hq:
                                        e.tensor_copy(
                                o1sb[hq * 64:hq * 64 + 64, b4 * 64:(b4 + 1) * 64],
                                pb[2][(b4 % 2) * 64:(b4 % 2) * 64 + 64,
                                      (b4 // 2) * 128 + hq * 64:
                                      (b4 // 2) * 128 + hq * 64 + 64]),
                                waits=[omm], inc="V")
                    tadd = p.emit("vector", lambda e: e.tensor_tensor(
                        pb[3][:, NTOK:2 * NTOK], o1sb[:, :], pb[3][:, 0:NTOK],
                        ALU.add), waits=[], inc="V")
                    tog = p.emit("vector", lambda e, c=c: e.tensor_tensor(
                        pb[5][:, 0:NTOK], pb[3][:, NTOK:2 * NTOK], gs[:, c, :],
                        ALU.mult), waits=[ev_toks["g"][c]], inc="V")
                    p.emit("vector", lambda e, c=c: e.tensor_copy(
                        ohi[:, c, :], pb[5][:, 0:NTOK]), waits=[], inc="V")
                    tlo_last = p.emit("vector", lambda e, c=c: e.tensor_tensor(
                        olo[:, c, :], pb[5][:, 0:NTOK], ohi[:, c, :],
                        ALU.subtract), waits=[], inc="V")
                    og_done = tlo_last
                    for b4 in range(BC):
                        for hq in range(2):
                            p.emit("vector", lambda e, b4=b4, hq=hq: e.tensor_copy(
                                srm[hq * 64:hq * 64 + 64, b4 * 64:(b4 + 1) * 64],
                                pb[4][(b4 % 2) * 64:(b4 % 2) * 64 + 64,
                                      (b4 // 2) * 128 + hq * 64:
                                      (b4 // 2) * 128 + hq * 64 + 64]),
                                waits=[smm], inc="V")
                    p.emit("vector", lambda e, c=c, par=par: e.tensor_tensor(
                        pb[4][:, NTOK:2 * NTOK], hp[par][:, c, :], srm[:, :],
                        ALU.add), waits=[], inc="V")
                    prev_scan_v = p.emit("vector", lambda e, c=c, par=par:
                                         e.tensor_scalar(
                        hp[1 - par][:, c, :], pb[4][:, NTOK:2 * NTOK],
                        a64[:, c:c + 1], None, ALU.mult), waits=[], inc="V")
                scan_v_done = p.ck("V")

                m_o = t * 5 + 4
                wo_par, wo_tok = m_o % 2, w_ready[m_o]
                s_toks = []
                for mc in range(KC):
                    ri, reg = next_region()
                    reg2 = regions2[ri]
                    last = proj3(
                        reg,
                        lambda k, wo_par=wo_par, mc=mc:
                            wbh[:, wo_par, k, mc * 128:(mc + 1) * 128],
                        lambda k, wo_par=wo_par, mc=mc:
                            wbl[:, wo_par, k, mc * 128:(mc + 1) * 128],
                        lambda k: ohi[:, k, :],
                        lambda k: olo[:, k, :],
                        [wo_tok, tlo_last, reg_free[ri], scan_v_done])
                    tb = p.emit("vector", lambda e, mc=mc, reg=reg, reg2=reg2:
                                e.tensor_scalar(
                        reg2, reg, bias["bo"][:, mc:mc + 1], None, ALU.add),
                        waits=[last], inc="V")
                    stok = p.emit("vector", lambda e, mc=mc, reg2=reg2:
                                  e.scalar_tensor_tensor(
                        ssb[:, mc, :], xc(mc), 2.0, reg2, ALU.mult, ALU.add),
                        waits=[prev_S if mc == 0 else None], inc="V")
                    s_toks.append(stok)
                    reg_free[ri] = stok
                x_free[t] = s_toks[-1]
                wo_v_ck = p.ck("V")
                w_free[m_o] = p.ck("PE")
                prefetch_w(m_o + 2)
                prefetch_x(t + 1)

                if pha < 8:
                    barrier = {s: p.ck(s) for s in ("PE", "V", "S", "dG")}
                    continue
                sq_pe2 = []
                mm2_last = None
                for c in range(KC):
                    tsqr = p.emit("scalar", lambda e, c=c: e.activation(
                        sq[:, c % 2, :], ssb[:, c, :], ACT.Square),
                        waits=[s_toks[c], (sq_pe2[c - 2] if c >= 2 else None)],
                        inc="S")
                    mm2_last = p.emit("tensor", lambda e, c=c: e.matmul(
                        pb[0][0:1, 0:NTOK], ones_c[:, :], ssb[:, c, :],
                        start=(c == 0), stop=(c == KC - 1), skip_group_check=True),
                        waits=[s_toks[c]], inc="PE" if c == KC - 1 else None)
                    sq_pe2.append(p.emit("tensor", lambda e, c=c: e.matmul(
                        pb[7][0:1, 0:NTOK], ones_c[:, :], sq[:, c % 2, :],
                        start=(c == 0), stop=(c == KC - 1), skip_group_check=True), waits=[tsqr],
                        inc="PE"))
                tbc1, tbc2, trbc = ln_rows(sq_pe2[-1], prev_iv)

                x2_toks = []
                for c in range(KC):
                    reg = pb[2] if c % 2 == 0 else pb[3]
                    p.emit("vector", lambda e, c=c, reg=reg: e.tensor_tensor(
                        reg[:, 0:NTOK], ssb[:, c, :], pb[0][:, 0:NTOK],
                        ALU.subtract), waits=[tbc2], inc="V")
                    x2_toks.append(p.emit("vector", lambda e, c=c, reg=reg,
                                          par=par: e.tensor_tensor(
                        x2sb[:, par, c, :], reg[:, 0:NTOK], rbc_sb[:, :],
                        ALU.mult), waits=[trbc, prev_store], inc="V"))
                pb01_free = p.ck("V")
                defer_ctx[t] = dict(par=par, x2_toks=list(x2_toks))
                prev_sqpe = sq_pe2[-1]
                prev_S = p.ck("S")
                prev_tlo = tlo_last

                if t == 0 and dump:
                    taps = [("xnhi", xnhi, F16), ("xnlo", xnlo, BF16),
                            ("qf", qf, F32), ("kf", kf, F32), ("gs", gs, F32),
                            ("ohi", ohi, F16), ("olo", olo, BF16),
                            ("ssb", ssb, F32), ("x2sb", x2sb[:, 0], F32),
                            ("hp1", hp[1], F32)]
                    vck = p.ck("V")
                    for nm, tile, dt in taps:
                        dp = dump_out(nm, [128, KC, NTOK], dt)
                        if dp is not None:
                            p.emit("gpsimd", lambda e, dp=dp, tile=tile:
                                   e.dma_start(out=dp[:, :, :], in_=tile[:, :, :]),
                                   waits=[vck], inc="dG", inc_n=16)
                    for nm, tile in (("aq", aq), ("ak", ak)):
                        dp = dump_out(nm, [128, KC, 64])
                        if dp is not None:
                            p.emit("gpsimd", lambda e, dp=dp, tile=tile:
                                   e.dma_start(out=dp[:, :, :], in_=tile[:, :, :]),
                                   waits=[vck], inc="dG", inc_n=16)
                    for nm, tile in (("vtm", vtm), ("ktm", ktm)):
                        dp = dump_out(nm, [128, 2, DI])
                        if dp is not None:
                            p.emit("gpsimd", lambda e, dp=dp, tile=tile:
                                   e.dma_start(out=dp[:, :, :], in_=tile[:, :, :]),
                                   waits=[vck], inc="dG", inc_n=16)
                    p.emit("vector", lambda e: e.engine_nop(),
                           waits=[p.ck("dG")], inc="V")

                barrier = {s: p.ck(s) for s in ("PE", "V", "S", "dG")}

            if (nta - 1) in defer_ctx:
                emit_deferred(nta - 1)
            passA_dG = p.ck("dG")
            passA_V = p.ck("V")

        with contextlib.ExitStack() as esb:
            def sbb(name, shape, dt=F32):
                return esb.enter_context(nc.sbuf_tensor(name, shape, dt))

            tg_sb = sbb("tg_sb", [128, KC, FF], FP8)
            tu_sb = sbb("tu_sb", [128, KC, FF], FP8)
            td_sb = sbb("td_sb", [128, OC, D], FP8)
            i1b = sbb("i1b", [128, 1, KC, NTOK], BF16)
            x2b = sbb("x2b", [128, 1, KC, NTOK])
            iv1r = sbb("iv1r", [1, 1, NTOK])
            iv1bc2 = sbb("iv1bc2", [128, 512])
            iv2bc = sbb("iv2bc", [128, NTOK])
            g1 = sbb("g1", [128, 2, 512])
            gate = sbb("gate", [128, 512])
            hsb = sbb("hsb", [128, OC * NTOK])
            i2sb = sbb("i2sb", [128, OC * NTOK], BF16)
            s3sb = sbb("s3sb", [128, KC, NTOK])
            accpB = sbb("accpB", [128, NTOK])
            accpB2 = sbb("accpB2", [128, 512])
            accnB2 = sbb("accnB2", [128, 512])
            accnB = sbb("accnB", [128, NTOK])
            amB = sbb("amB", [128, NTOK])
            amfB = sbb("amfB", [64, NTOK])
            rbcB = sbb("rbcB", [128, NTOK])
            bsbB = sbb("bsbB", [128, 1024])
            ufinB = sbb("ufinB", [128, 1024])
            sqB = sbb("sqB", [128, 2, NTOK])

            passA_PE = p.ck("PE")
            passA_S = p.ck("S")
            for dst2, src2 in [(tg_sb[:, :, :], tg_s[:, :, :]),
                               (tu_sb[:, :, :], tu_s[:, :, :]),
                               (td_sb[:, :, :], td_s[:, :, :])]:
                dma("sync", dst2, src2, waits=[passA_dG, passA_PE, passA_V,
                                               passA_S])
            tT = p.ck("dS")

            xb_free = {}
            xb_ready = {}

            def prefetch_b(t):
                if t >= ntb or t in xb_ready:
                    return
                p.emit("sync", lambda e, t=t: e.dma_start(
                    out=i1b[:, 0, :, :], in_=i1_s[t, :, :, :]),
                    waits=[xb_free.get(t - 1), passA_dG, passA_PE, passA_S,
                           passA_V], inc="dS", inc_n=16)
                p.emit("sync", lambda e, t=t: e.dma_start(
                    out=x2b[:, 0, :, :], in_=x2_s[t, :, :, :]),
                    waits=[], inc="dS", inc_n=16)
                xb_ready[t] = p.emit("sync", lambda e, t=t: e.dma_start(
                    out=iv1r[:, 0, :], in_=iv1_s[t, :, :]),
                    waits=[], inc="dS", inc_n=16)

            prefetch_b(0)
            barB_V = passA_V
            prev_out = None

            for t in range(ntb):
                par = t % 2
                tin = xb_ready[t]
                tb = p.emit("tensor", lambda e, par=par: e.matmul(
                    pb[1][:, 0:NTOK], ones_r[:, :], iv1r[0:1, 0, :],
                    start=True, stop=True), waits=[tin, tT, barB_V], inc="PE")
                p.emit("vector", lambda e: e.tensor_copy(
                    iv1bc2[:, 0:NTOK], pb[1][:, 0:NTOK]),
                    waits=[tb, passA_dG], inc="V")
                tivc = p.emit("vector", lambda e: e.tensor_copy(
                    iv1bc2[:, NTOK:2 * NTOK], pb[1][:, 0:NTOK]), waits=[], inc="V")

                h_last = None
                h_toks = []
                for ocp in range(0, OC, 2):
                    pi = (ocp // 2) % 2
                    rg = pb[2 + pi * 2]
                    ru = pb[3 + pi * 2]
                    lastg = None
                    lastu = None
                    for half in range(2):
                        oc = ocp + half
                        for k in range(KC):
                            fst = (k == 0)
                            lst = (k == KC - 1)
                            lastg = p.emit("tensor", lambda e, k=k, oc=oc, rg=rg,
                                           half=half, par=par, fst=fst, lst=lst:
                                           e.matmul(
                                rg[:, half * NTOK:(half + 1) * NTOK],
                                tg_sb[:, k, oc * 128:(oc + 1) * 128],
                                i1b[:, 0, k, :], start=fst, stop=lst,
                                skip_group_check=True),
                                waits=([tin, (h_toks[-2] if len(h_toks) >= 2
                                              else None)] if fst else []),
                                inc="PE" if lst else None)
                    for half in range(2):
                        oc = ocp + half
                        for k in range(KC):
                            fst = (k == 0)
                            lst = (k == KC - 1)
                            lastu = p.emit("tensor", lambda e, k=k, oc=oc, ru=ru,
                                           half=half, par=par, fst=fst, lst=lst:
                                           e.matmul(
                                ru[:, half * NTOK:(half + 1) * NTOK],
                                tu_sb[:, k, oc * 128:(oc + 1) * 128],
                                i1b[:, 0, k, :], start=fst, stop=lst,
                                skip_group_check=True),
                                waits=[], inc="PE" if lst else None)
                    tg1 = p.emit("vector", lambda e, rg=rg:
                                 e.scalar_tensor_tensor(
                        g1[:, 0, :], rg[:, 0:512], mghat[:, 0:1], iv1bc2[:, :],
                        ALU.mult, ALU.mult), waits=[lastg, tivc], inc="V")
                    tgate = p.emit("scalar", lambda e: e.activation(
                        gate[:, :], g1[:, 0, :], ACT.Silu),
                        waits=[tg1], inc="S")
                    tu1 = p.emit("vector", lambda e, ru=ru:
                                 e.scalar_tensor_tensor(
                        g1[:, 1, :], ru[:, 0:512], mghat[:, 1:2], iv1bc2[:, :],
                        ALU.mult, ALU.mult), waits=[lastu], inc="V")
                    th = p.emit("vector", lambda e, ocp=ocp: e.tensor_tensor(
                        hsb[:, ocp * NTOK:(ocp + 2) * NTOK],
                        gate[:, :], g1[:, 1, :], ALU.mult),
                        waits=[tgate, tu1], inc="V")
                    if ocp == 0:
                        p.emit("vector", lambda e: e.tensor_copy(
                            accpB2[:, :], hsb[:, 0:2 * NTOK]),
                            waits=[], inc="V")
                        h_last = p.emit("vector", lambda e: e.tensor_copy(
                            accnB2[:, :], hsb[:, 0:2 * NTOK]),
                            waits=[], inc="V")
                    else:
                        p.emit("vector", lambda e, ocp=ocp: e.tensor_tensor(
                            accpB2[:, :], accpB2[:, :],
                            hsb[:, ocp * NTOK:(ocp + 2) * NTOK], ALU.max),
                            waits=[], inc="V")
                        h_last = p.emit("vector", lambda e, ocp=ocp: e.tensor_tensor(
                            accnB2[:, :], accnB2[:, :],
                            hsb[:, ocp * NTOK:(ocp + 2) * NTOK], ALU.min),
                            waits=[], inc="V")
                    h_toks.append(h_last)
                p.emit("vector", lambda e: e.tensor_tensor(
                    accpB[:, :], accpB2[:, 0:NTOK], accpB2[:, NTOK:2 * NTOK],
                    ALU.max), waits=[h_last], inc="V")
                h_last = p.emit("vector", lambda e: e.tensor_tensor(
                    accnB[:, :], accnB2[:, 0:NTOK], accnB2[:, NTOK:2 * NTOK],
                    ALU.min), waits=[], inc="V")

                p.emit("vector", lambda e: e.scalar_tensor_tensor(
                    amB[:, :], accnB[:, :], -1.0, accpB[:, :], ALU.mult, ALU.max),
                    waits=[h_last], inc="V")
                for half in (64, 32):
                    p.emit("vector", lambda e, h=half: e.tensor_copy(
                        amfB[0:h, :], amB[h:2 * h, :]), waits=[], inc="V")
                    p.emit("vector", lambda e, h=half: e.tensor_tensor(
                        amB[0:h, :], amB[0:h, :], amfB[0:h, :], ALU.max),
                        waits=[], inc="V")
                for half in (16, 8, 4, 2, 1):
                    msk = [(i + half) % 32 for i in range(32)]
                    p.emit("vector", lambda e, m=msk: e.stream_shuffle(
                        amfB[0:32, :], amB[0:32, :], m), waits=[], inc="V")
                    p.emit("vector", lambda e, h=half: e.tensor_tensor(
                        amB[0:h, :], amB[0:h, :], amfB[0:h, :], ALU.max),
                        waits=[], inc="V")
                p.emit("vector", lambda e: e.tensor_scalar(
                    rows[0:1, 0:NTOK], amB[0:1, :], 1e-5, None, ALU.max),
                    waits=[], inc="V")
                p.emit("vector", lambda e: e.reciprocal(
                    rows[0:1, NTOK:2 * NTOK], rows[0:1, 0:NTOK]), waits=[], inc="V")
                ts2 = p.emit("vector", lambda e: e.tensor_scalar(
                    rows2[0:1, 0:NTOK], rows[0:1, NTOK:2 * NTOK], 127.0, None,
                    ALU.mult), waits=[], inc="V")
                tiv2 = p.emit("vector", lambda e: e.tensor_scalar(
                    rows2[0:1, NTOK:2 * NTOK], rows[0:1, 0:NTOK], 1.0 / 127.0,
                    None, ALU.mult), waits=[], inc="V")
                tb1 = p.emit("tensor", lambda e: e.matmul(
                    pb[1][:, 0:NTOK], ones_r[:, :], rows2[0:1, 0:NTOK],
                    start=True, stop=True), waits=[ts2], inc="PE")
                tb1b = p.emit("tensor", lambda e: e.matmul(
                    pb[1][:, NTOK:2 * NTOK], ones_r[:, :], rows2[0:1, 0:NTOK],
                    start=True, stop=True), waits=[], inc="PE")
                tb2 = p.emit("tensor", lambda e: e.matmul(
                    pb[0][:, 0:NTOK], ones_r[:, :],
                    rows2[0:1, NTOK:2 * NTOK], start=True, stop=True),
                    waits=[tiv2], inc="PE")
                tiv2c = p.emit("vector", lambda e: e.tensor_copy(
                    iv2bc[:, :], pb[0][:, 0:NTOK]), waits=[tb2, tb1b],
                    inc="V")

                ti2 = None
                ts2c = tiv2c
                for cc in range(0, OC, 4):
                    u0v = pbW[1][:, 0:1024]
                    pgA = pbW[2][:, 0:1024]
                    pgB = pbW[3][:, 0:1024]
                    p.emit("vector", lambda e, cc=cc: e.tensor_tensor(
                        pbW[1][:, 0:512], hsb[:, cc * NTOK:(cc + 2) * NTOK],
                        pbW[0][:, 512:1024], ALU.mult), waits=[tiv2c, ts2c],
                        inc="V")
                    p.emit("vector", lambda e, cc=cc: e.tensor_tensor(
                        pbW[1][:, 512:1024],
                        hsb[:, (cc + 2) * NTOK:(cc + 4) * NTOK],
                        pbW[0][:, 512:1024], ALU.mult), waits=[], inc="V")
                    p.emit("vector", lambda e, u0v=u0v: e.tensor_scalar(
                        u0v, u0v, 128.0, None, ALU.add), waits=[], inc="V")
                    srcv = u0v
                    for ki, k in enumerate(range(7, -1, -1)):
                        cth = float(2 ** k) - 0.5
                        p.emit("vector", lambda e, srcv=srcv, cth=cth, k=k:
                               e.tensor_scalar(
                            bsbB[:, :], srcv, cth, -float(2 ** k),
                            ALU.is_ge, ALU.mult), waits=[], inc="V")
                        dstv = ufinB[:, :] if k == 0 else (pgA if ki % 2 == 0 else pgB)
                        p.emit("vector", lambda e, srcv=srcv, dstv=dstv:
                               e.tensor_tensor(
                            dstv, bsbB[:, :], srcv, ALU.add), waits=[], inc="V")
                        srcv = dstv
                    p.emit("vector", lambda e, u0v=u0v: e.scalar_tensor_tensor(
                        bsbB[:, :], ufinB[:, :], -1.0, u0v, ALU.mult, ALU.add),
                        waits=[], inc="V")
                    ti2 = p.emit("vector", lambda e, cc=cc: e.tensor_scalar(
                        i2sb[:, cc * NTOK:(cc + 4) * NTOK], bsbB[:, :], -128.0,
                        None, ALU.add), waits=[], inc="V")

                s3_toks = []
                for mc in range(KC):
                    reg = pb[4][:, 0:NTOK] if mc % 2 == 0 else pb[5][:, 0:NTOK]
                    reg2 = (pb[4][:, NTOK:2 * NTOK] if mc % 2 == 0
                            else pb[5][:, NTOK:2 * NTOK])
                    last = None
                    for k in range(OC):
                        fst = (k == 0)
                        lst = (k == OC - 1)
                        last = p.emit("tensor", lambda e, k=k, mc=mc, reg=reg,
                                      fst=fst, lst=lst: e.matmul(
                            reg, td_sb[:, k, mc * 128:(mc + 1) * 128],
                            i2sb[:, k * NTOK:(k + 1) * NTOK], start=fst, stop=lst),
                            waits=([ti2, (s3_toks[mc - 2] if mc >= 2 else None)]
                                   if fst else []), inc="PE" if lst else None)
                    p.emit("vector", lambda e, reg=reg, reg2=reg2:
                           e.scalar_tensor_tensor(
                        reg2, reg, mghat[:, 2:3], iv2bc[:, :], ALU.mult,
                        ALU.mult), waits=[last], inc="V")
                    s3_toks.append(p.emit("vector", lambda e, mc=mc, par=par,
                                          reg2=reg2: e.tensor_tensor(
                        s3sb[:, mc, :], x2b[:, 0, mc, :], reg2, ALU.add),
                        waits=[], inc="V"))
                xb_free[t] = s3_toks[-1]
                prefetch_b(t + 1)

                sq_pe3 = []
                mm3_last = None
                for c in range(KC):
                    tsqr = p.emit("scalar", lambda e, c=c: e.activation(
                        sqB[:, c % 2, :], s3sb[:, c, :], ACT.Square),
                        waits=[s3_toks[c], (sq_pe3[c - 2] if c >= 2 else None)],
                        inc="S")
                    mm3_last = p.emit("tensor", lambda e, c=c: e.matmul(
                        pb[0][0:1, 0:NTOK], ones_c[:, :], s3sb[:, c, :],
                        start=(c == 0), stop=(c == KC - 1), skip_group_check=True),
                        waits=[s3_toks[c]], inc="PE" if c == KC - 1 else None)
                    sq_pe3.append(p.emit("tensor", lambda e, c=c: e.matmul(
                        pb[1][0:1, 0:NTOK], ones_c[:, :], sqB[:, c % 2, :],
                        start=(c == 0), stop=(c == KC - 1), skip_group_check=True), waits=[tsqr],
                        inc="PE"))
                tmu = p.emit("vector", lambda e: e.tensor_scalar(
                    rows[0:1, 0:NTOK], pb[0][0:1, 0:NTOK], 1.0 / D, None,
                    ALU.mult), waits=[sq_pe3[-1], mm3_last], inc="V")
                p.emit("vector", lambda e: e.tensor_scalar(
                    rows[0:1, NTOK:2 * NTOK], pb[1][0:1, 0:NTOK], 1.0 / D,
                    None, ALU.mult), waits=[], inc="V")
                p.emit("vector", lambda e: e.tensor_tensor(
                    rows2[0:1, 0:NTOK], rows[0:1, 0:NTOK], rows[0:1, 0:NTOK],
                    ALU.mult), waits=[], inc="V")
                tvar = p.emit("vector", lambda e: e.tensor_tensor(
                    rows2[0:1, NTOK:2 * NTOK], rows[0:1, NTOK:2 * NTOK],
                    rows2[0:1, 0:NTOK], ALU.subtract), waits=[], inc="V")
                tsq = p.emit("scalar", lambda e: e.activation(
                    rows2[0:1, 0:NTOK], rows2[0:1, NTOK:2 * NTOK], ACT.Sqrt,
                    bias=eps1[:, :]), waits=[tvar], inc="S")
                trr = p.emit("vector", lambda e: e.reciprocal(
                    rows[0:1, NTOK:2 * NTOK], rows2[0:1, 0:NTOK]),
                    waits=[tsq], inc="V")
                tbc1 = p.emit("tensor", lambda e: e.matmul(
                    pb[0][:, 0:NTOK], ones_r[:, :], rows[0:1, 0:NTOK],
                    start=True, stop=True), waits=[tmu, trr], inc="PE")
                tbc2 = p.emit("tensor", lambda e: e.matmul(
                    pb[0][:, NTOK:2 * NTOK], ones_r[:, :],
                    rows[0:1, NTOK:2 * NTOK], start=True, stop=True),
                    waits=[trr], inc="PE")
                trbc = p.emit("vector", lambda e: e.tensor_copy(
                    rbcB[:, :], pb[0][:, NTOK:2 * NTOK]), waits=[tbc2], inc="V")
                for c in range(KC):
                    reg = pb[2] if c % 2 == 0 else pb[3]
                    p.emit("vector", lambda e, c=c, reg=reg: e.tensor_tensor(
                        reg[:, 0:NTOK], s3sb[:, c, :], pb[0][:, 0:NTOK],
                        ALU.subtract), waits=[tbc2, prev_out], inc="V")
                    otok = p.emit("vector", lambda e, c=c, reg=reg: e.tensor_tensor(
                        s3sb[:, c, :], reg[:, 0:NTOK], rbcB[:, :], ALU.mult),
                        waits=[trbc], inc="V")
                    prev_out = p.emit("gpsimd", lambda e, t=t, c=c: e.dma_start(
                        out=out_t[:, t, :].rearrange("(k p) n -> p k n", p=128)[
                            :, c, :],
                        in_=s3sb[:, c, :]), waits=[otok], inc="dG", inc_n=16)
                barB_V = p.ck("V")

            final_dg = p.ck("dG")
            p.emit("gpsimd", lambda e: e.engine_nop(), waits=[final_dg], inc="G")

        block = es.enter_context(nc.Block())
        p.build(block)

    return nc, dumps


def _host_inputs(core, L0_attn_x, L0_attn_WQ, L0_attn_bQ, L0_attn_WK, L0_attn_bK,
                 L0_attn_WV, L0_attn_bV, L0_attn_WO, L0_attn_bO, L0_attn_Wg,
                 L0_attn_bg, L0_attn_alpha_logit, L0_attn_ln_gamma,
                 L0_attn_ln_beta, L0_ffn_w_gate, L0_ffn_w_up, L0_ffn_w_down):
    xc = np.asarray(L0_attn_x)[4 * core:4 * core + 4]
    x_t = np.ascontiguousarray(
        xc.transpose(2, 1, 0).reshape(D, NT, C, BC).transpose(0, 1, 3, 2)
        .reshape(D, NT, NTOK))

    def colform(b):
        return np.ascontiguousarray(np.asarray(b, np.float32).reshape(KC, 128).T)
    a = np.asarray(L0_attn_alpha_logit, np.float32).reshape(KC, 2, 64)
    alpha_c = np.ascontiguousarray(a.transpose(1, 2, 0).reshape(128, KC))
    pp, ff = np.meshgrid(np.arange(128), np.arange(NTOK), indexing="ij")
    mask_t = ((pp % 64) <= (ff % 64)).astype(np.float32)
    return {
        "x_t": x_t,
        "wq_t": np.ascontiguousarray(np.asarray(L0_attn_WQ, np.float32).T),
        "wk_t": np.ascontiguousarray(np.asarray(L0_attn_WK, np.float32).T),
        "wv_t": np.ascontiguousarray(np.asarray(L0_attn_WV, np.float32).T),
        "wg_t": np.ascontiguousarray(np.asarray(L0_attn_Wg, np.float32).T),
        "wo_t": np.ascontiguousarray(np.asarray(L0_attn_WO, np.float32).T),
        "wg1_t": np.ascontiguousarray(np.asarray(L0_ffn_w_gate, np.float32).T),
        "wu1_t": np.ascontiguousarray(np.asarray(L0_ffn_w_up, np.float32).T),
        "wd1_t": np.ascontiguousarray(np.asarray(L0_ffn_w_down, np.float32).T),
        "bq_c": colform(L0_attn_bQ), "bk_c": colform(L0_attn_bK),
        "bv_c": colform(L0_attn_bV), "bg_c": colform(L0_attn_bg),
        "bo_c": colform(L0_attn_bO),
        "gamma_c": colform(L0_attn_ln_gamma), "beta_c": colform(L0_attn_ln_beta),
        "alpha_c": alpha_c,
        "bv_row": np.ascontiguousarray(
            np.asarray(L0_attn_bV, np.float32).reshape(1, DI)),
        "mask_t": mask_t,
        "ones_col": np.ones((128, 1), np.float32),
        "ones_row": np.ones((1, 128), np.float32),
        "ident64": np.eye(64, dtype=np.float32),
    }


def kernel(**inputs):
    nc, _ = build_program()
    in_maps = [_host_inputs(c, **inputs) for c in range(8)]
    res = run_bass_kernel_spmd(nc, in_maps, core_ids=list(range(8)))
    outs = []
    for c in range(8):
        ot = np.asarray(res.results[c]["out_t"])
        oc = (ot.reshape(D, NT, BC, C).transpose(0, 1, 3, 2)
              .reshape(D, T, BC).transpose(2, 1, 0))
        outs.append(oc)
    return np.ascontiguousarray(np.concatenate(outs, axis=0)).astype(np.float32)


# version 15
# speedup vs baseline: 1.0997x; 1.0424x over previous
import contextlib
import numpy as np

import concourse.bass as bass
import concourse.mybir as mybir
from concourse.bass_utils import run_bass_kernel_spmd

F32 = mybir.dt.float32
F16 = mybir.dt.float16
BF16 = mybir.dt.bfloat16
FP8 = mybir.dt.float8e4
ALU = mybir.AluOpType
ACT = mybir.ActivationFunctionType
AX = mybir.AxisListType

D = 1024
DI = 1024
BC = 4
T = 2048
C = 64
NT = T // C
NTOK = BC * C
KC = D // 128
FF = 4096
OC = FF // 128
LN_EPS = 1e-5

ENGINES = ("sync", "tensor", "vector", "scalar", "gpsimd")


class Plan:

    def __init__(self, nc):
        self.nc = nc
        self.ops = {e: [] for e in ENGINES}
        self.cnt = {}
        self.sem_objs = {}
        self.last_wait = {e: {} for e in ENGINES}
        self.own = {"sync": None, "tensor": "PE", "vector": "V",
                    "scalar": "S", "gpsimd": "G"}

    def add_sem(self, name, obj):
        self.sem_objs[name] = obj
        self.cnt[name] = 0

    def emit(self, eng, fn, waits=(), inc=None, inc_n=1):
        rw = []
        for w in waits:
            if w is None:
                continue
            s, v = w
            if v <= 0:
                continue
            assert v <= self.cnt[s], f"wait on future {s}:{v} > {self.cnt[s]}"
            if s == self.own[eng]:
                continue
            if self.last_wait[eng].get(s, -1) >= v:
                continue
            rw.append((s, v))
            self.last_wait[eng][s] = v
        tok = None
        if inc is not None:
            self.cnt[inc] += inc_n
            tok = (inc, self.cnt[inc])
        self.ops[eng].append((fn, rw, (inc, inc_n) if inc else None))
        return tok

    def ck(self, sem):
        return (sem, self.cnt[sem])

    def build(self, block):
        sem_objs = self.sem_objs
        for eng in ENGINES:
            ops = self.ops[eng]
            if not ops:
                continue

            def make(ops):
                def run(e):
                    for fn, waits, inc in ops:
                        for s, v in waits:
                            e.wait_ge(sem_objs[s], v)
                        r = fn(e)
                        if inc is not None:
                            s, n = inc
                            r.then_inc(sem_objs[s], n)
                return run
            getattr(block, eng)(make(ops))


def build_program(dump=(), nta=NT, ntb=NT, pha=99):
    nc = bass.Bass("TRN2", target_bir_lowering=False,
                   detect_race_conditions=False)
    dump = set(dump)

    x_t = nc.dram_tensor("x_t", [D, NT, NTOK], F32, kind="ExternalInput").ap()
    w_in = {}
    for w in ("wq", "wk", "wv", "wg", "wo"):
        w_in[w] = nc.dram_tensor(w + "_t", [D, DI], F32, kind="ExternalInput").ap()
    wg1_t = nc.dram_tensor("wg1_t", [D, FF], F32, kind="ExternalInput").ap()
    wu1_t = nc.dram_tensor("wu1_t", [D, FF], F32, kind="ExternalInput").ap()
    wd1_t = nc.dram_tensor("wd1_t", [FF, D], F32, kind="ExternalInput").ap()
    b_in = {}
    for b in ("bq", "bk", "bv", "bg", "bo"):
        b_in[b] = nc.dram_tensor(b + "_c", [128, KC], F32, kind="ExternalInput").ap()
    gamma_c = nc.dram_tensor("gamma_c", [128, KC], F32, kind="ExternalInput").ap()
    beta_c = nc.dram_tensor("beta_c", [128, KC], F32, kind="ExternalInput").ap()
    alpha_in = nc.dram_tensor("alpha_c", [128, KC], F32, kind="ExternalInput").ap()
    bv_row = nc.dram_tensor("bv_row", [1, DI], F32, kind="ExternalInput").ap()
    mask_in = nc.dram_tensor("mask_t", [128, NTOK], F32, kind="ExternalInput").ap()
    onesc_in = nc.dram_tensor("ones_col", [128, 1], F32, kind="ExternalInput").ap()
    onesr_in = nc.dram_tensor("ones_row", [1, 128], F32, kind="ExternalInput").ap()
    ident_in = nc.dram_tensor("ident64", [64, 64], F32, kind="ExternalInput").ap()
    out_t = nc.dram_tensor("out_t", [D, NT, NTOK], F32, kind="ExternalOutput").ap()

    whi_s = nc.dram_tensor("whi_s", [5, 128, KC, DI], F16, kind="Internal").ap()
    wlo_s = nc.dram_tensor("wlo_s", [5, 128, KC, DI], BF16, kind="Internal").ap()
    tg_s = nc.dram_tensor("tg_s", [128, KC, FF], FP8, kind="Internal").ap()
    tu_s = nc.dram_tensor("tu_s", [128, KC, FF], FP8, kind="Internal").ap()
    td_s = nc.dram_tensor("td_s", [128, OC, D], FP8, kind="Internal").ap()
    i1_s = nc.dram_tensor("i1_s", [NT, 128, KC, NTOK], BF16, kind="Internal").ap()
    x2_s = nc.dram_tensor("x2_s", [NT, 128, KC, NTOK], F32, kind="Internal").ap()
    iv1_s = nc.dram_tensor("iv1_s", [NT, 1, NTOK], F32, kind="Internal").ap()

    dumps = {}
    def dump_out(name, shape, dtype=F32):
        if name in dump and name not in dumps:
            dumps[name] = nc.dram_tensor("dbg_" + name, shape, dtype,
                                         kind="ExternalOutput").ap()
        return dumps.get(name)

    es = contextlib.ExitStack()
    with es:
        p = Plan(nc)
        for s in ("dS", "dG", "PE", "V", "S", "G"):
            p.add_sem(s, es.enter_context(nc.semaphore(s)))

        def sb(name, shape, dt=F32):
            return es.enter_context(nc.sbuf_tensor(name, shape, dt))

        pbW = [es.enter_context(nc.psum_tensor("pbw%d" % i, [128, 1024], F32))
               for i in range(4)]
        pb = [pbW[i // 2][:, (i % 2) * 512:(i % 2 + 1) * 512] for i in range(8)]
        bias = {b: sb(b, [128, KC]) for b in ("bq", "bk", "bv", "bg", "bo")}
        gam = sb("gam", [128, KC]); bet = sb("bet", [128, KC])
        alpha = sb("alpha", [128, KC])
        a64 = sb("a64", [128, KC])
        mghat = sb("mghat", [128, 3])
        mask = sb("mask", [128, NTOK])
        ones_c = sb("ones_c", [128, 1])
        ones_r = sb("ones_r", [1, 128])
        ident = sb("ident", [128, 64])
        eps1 = sb("eps1", [1, 1])
        rows = sb("rows", [1, 512])
        rows2 = sb("rows2", [1, 512])
        negc = sb("negc", [128, 8])
        aq = sb("aq", [128, KC, 64])
        ak = sb("ak", [128, KC, 64])
        bvbc = sb("bvbc", [128, DI])

        def dma(eng, out, in_, waits=()):
            return p.emit(eng, lambda e, o=out, i=in_: e.dma_start(out=o, in_=i),
                          waits=waits,
                          inc="dS" if eng in ("sync", "scalar") else "dG", inc_n=16)

        for dst, src in [(mask[:, :], mask_in[:, :]),
                         (ones_c[:, :], onesc_in[:, :]),
                         (ones_r[:, :], onesr_in[:, :]),
                         (ident[0:64, :], ident_in[:, :]),
                         (ident[64:128, :], ident_in[:, :]),
                         (gam[:, :], gamma_c[:, :]),
                         (bet[:, :], beta_c[:, :]),
                         (alpha[:, :], alpha_in[:, :])]:
            dma("sync", dst, src)
        for b in bias:
            dma("sync", bias[b][:, :], b_in[b][:, :])
        t_consts = p.ck("dS")

        p.emit("vector", lambda e: e.memset(eps1[:, :], LN_EPS), waits=[], inc="V")
        for k in range(8):
            cval = -(127.5 - 2.0 ** -17)
            p.emit("vector", lambda e, k=k, c=cval: e.memset(negc[:, k:k + 1], c),
                   waits=[], inc="V")
        tok_alpha = p.emit("scalar", lambda e: e.activation(
            alpha[:, :], alpha[:, :], ACT.Sigmoid), waits=[t_consts], inc="S")

        esp = contextlib.ExitStack()
        def sbp(name, shape, dt=F32):
            return esp.enter_context(nc.sbuf_tensor(name, shape, dt))
        apow = sbp("apow", [128, 64])
        apneg = sbp("apneg", [128, 64])
        acols = sbp("acols", [128, 16])
        def vdrain():
            p.emit("vector", lambda e: e.drain(), waits=[], inc=None)

        for c in range(KC):
            al = alpha[:, c:c + 1]
            def em(fn, waits=()):
                tok = p.emit("vector", fn, waits=waits, inc="V")
                vdrain()
                return tok
            w0 = [tok_alpha]
            em(lambda e, al=al: e.tensor_tensor(acols[:, 0:1], al, al, ALU.mult), w0)
            for j in range(1, 5):
                em(lambda e, j=j: e.tensor_tensor(
                    acols[:, j:j + 1], acols[:, j - 1:j], acols[:, j - 1:j], ALU.mult))
            em(lambda e, c=c: e.tensor_tensor(
                a64[:, c:c + 1], acols[:, 4:5], acols[:, 4:5], ALU.mult))
            em(lambda e, al=al: e.reciprocal(acols[:, 5:6], al), w0)
            for j in range(6, 11):
                em(lambda e, j=j: e.tensor_tensor(
                    acols[:, j:j + 1], acols[:, j - 1:j], acols[:, j - 1:j], ALU.mult))
            em(lambda e: e.memset(apow[:, 0:1], 1.0))
            em(lambda e, al=al: e.tensor_copy(apow[:, 1:2], al), w0)
            for j in range(1, 6):
                n = 2 ** j
                em(lambda e, n=n, j=j: e.tensor_scalar(
                    apow[:, n:2 * n], apow[:, 0:n], acols[:, j - 1:j], None, ALU.mult))
            em(lambda e: e.memset(apneg[:, 0:1], 1.0))
            em(lambda e: e.tensor_copy(apneg[:, 1:2], acols[:, 5:6]))
            for j in range(1, 6):
                n = 2 ** j
                em(lambda e, n=n, j=j: e.tensor_scalar(
                    apneg[:, n:2 * n], apneg[:, 0:n], acols[:, 5 + j:6 + j], None,
                    ALU.mult))
            em(lambda e, c=c: e.tensor_scalar(
                aq[:, c, :], apow[:, :], acols[:, 10:11], None, ALU.mult))
            em(lambda e, c=c: e.tensor_scalar(
                ak[:, c, :], apneg[:, :], acols[:, 4:5], None, ALU.mult))
        tok_atiles = p.ck("V")

        bvrow_sb = sbp("bvrow_sb", [1, DI])
        t_bv = dma("sync", bvrow_sb[:, :], bv_row[:, :])
        t1 = p.emit("tensor", lambda e: e.matmul(
            pb[0][:, 0:512], ones_r[:, :], bvrow_sb[:, 0:512],
            start=True, stop=True), waits=[t_bv, t_consts], inc="PE")
        t2 = p.emit("tensor", lambda e: e.matmul(
            pb[1][:, 0:512], ones_r[:, :], bvrow_sb[:, 512:1024],
            start=True, stop=True), waits=[t_bv], inc="PE")
        p.emit("vector", lambda e: e.tensor_copy(bvbc[:, 0:512], pb[0][:, 0:512]),
               waits=[t1], inc="V")
        t_bvbc = p.emit("vector", lambda e: e.tensor_copy(
            bvbc[:, 512:1024], pb[1][:, 0:512]), waits=[t2], inc="V")

        wtmp = sbp("wtmp", [128, 2, DI])
        whtmp = sbp("whtmp", [128, 2, DI], F16)
        wltmp = sbp("wltmp", [128, 2, DI], BF16)
        sfree = [None, None]
        for wi, w in enumerate(("wq", "wk", "wv", "wg", "wo")):
            for k in range(KC):
                par = (wi * KC + k) % 2
                tin = dma("sync", wtmp[:, par, :], w_in[w][k * 128:(k + 1) * 128, :],
                          waits=[sfree[par]])
                th = p.emit("vector", lambda e, par=par: e.tensor_copy(
                    whtmp[:, par, :], wtmp[:, par, :]), waits=[tin], inc="V")
                tl = p.emit("vector", lambda e, par=par: e.tensor_tensor(
                    wltmp[:, par, :], wtmp[:, par, :], whtmp[:, par, :],
                    ALU.subtract), waits=[], inc="V")
                dma("sync", whi_s[wi, :, k, :], whtmp[:, par, :], waits=[th])
                sfree[par] = dma("sync", wlo_s[wi, :, k, :], wltmp[:, par, :],
                                 waits=[tl])

        ftmp = sbp("ftmp", [128, 2, FF])
        fb1 = sbp("fb1", [128, FF])
        ftern = sbp("ftern", [128, FF], FP8)
        redc = sbp("redc", [128, 4])
        foldt = sbp("foldt", [64, NTOK])
        for wi, (src, dst, kn, cols) in enumerate(
                [(wg1_t, tg_s, KC, FF), (wu1_t, tu_s, KC, FF), (wd1_t, td_s, OC, D)]):
            tz = p.emit("vector", lambda e: e.memset(redc[:, 0:1], 0.0),
                        waits=[], inc="V")
            lin = [None, None]
            for k in range(kn):
                par = k % 2
                tin = dma("sync", ftmp[:, par, 0:cols],
                          src[k * 128:(k + 1) * 128, :], waits=[lin[par]])
                p.emit("vector", lambda e, par=par, cols=cols: e.tensor_reduce(
                    redc[:, 1:2], ftmp[:, par, 0:cols], AX.X, ALU.add,
                    apply_absolute_value=True), waits=[tin, tz], inc="V")
                vdrain()
                lin[par] = p.emit("vector", lambda e: e.tensor_tensor(
                    redc[:, 0:1], redc[:, 0:1], redc[:, 1:2], ALU.add),
                    waits=[], inc="V")
                vdrain()
            for half in (64, 32):
                p.emit("vector", lambda e, h=half: e.tensor_copy(
                    foldt[0:h, 0:1], redc[h:2 * h, 0:1]), waits=[], inc="V")
                vdrain()
                p.emit("vector", lambda e, h=half: e.tensor_tensor(
                    redc[0:h, 0:1], redc[0:h, 0:1], foldt[0:h, 0:1], ALU.add),
                    waits=[], inc="V")
                vdrain()
            for half in (16, 8, 4, 2, 1):
                msk = [(i + half) % 32 for i in range(32)]
                p.emit("vector", lambda e, m=msk: e.stream_shuffle(
                    foldt[0:32, 0:1], redc[0:32, 0:1], m), waits=[], inc="V")
                vdrain()
                p.emit("vector", lambda e, h=half: e.tensor_tensor(
                    redc[0:h, 0:1], redc[0:h, 0:1], foldt[0:h, 0:1], ALU.add),
                    waits=[], inc="V")
                vdrain()
            p.emit("vector", lambda e, n=float(kn * 128 * cols): e.tensor_scalar(
                rows[0:1, 0:1], redc[0:1, 0:1], 1.0 / n, 1e-5, ALU.mult, ALU.max),
                waits=[], inc="V")
            vdrain()
            tsw = p.emit("vector", lambda e: e.reciprocal(
                rows[0:1, 1:2], rows[0:1, 0:1]), waits=[], inc="V")
            vdrain()
            tm1 = p.emit("tensor", lambda e: e.matmul(
                pb[0][:, 0:1], ones_r[:, :], rows[0:1, 0:1], start=True, stop=True),
                waits=[tsw, t_bvbc], inc="PE")
            tm2 = p.emit("tensor", lambda e: e.matmul(
                pb[0][:, 1:2], ones_r[:, :], rows[0:1, 1:2], start=True, stop=True),
                waits=[], inc="PE")
            p.emit("vector", lambda e, wi=wi: e.tensor_copy(
                mghat[:, wi:wi + 1], pb[0][:, 0:1]), waits=[tm1, tm2], inc="V")
            tswc = p.emit("vector", lambda e: e.tensor_copy(
                redc[:, 2:3], pb[0][:, 1:2]), waits=[], inc="V")
            lin = [None, None]
            for k in range(kn):
                par = k % 2
                tin = dma("sync", ftmp[:, par, 0:cols],
                          src[k * 128:(k + 1) * 128, :], waits=[lin[par], tswc])
                p.emit("vector", lambda e, par=par, cols=cols: e.tensor_scalar(
                    fb1[:, 0:cols], ftmp[:, par, 0:cols], redc[:, 2:3], 0.5,
                    ALU.mult, ALU.is_gt), waits=[tin], inc="V")
                p.emit("vector", lambda e, par=par, cols=cols: e.tensor_scalar(
                    ftmp[:, par, 0:cols], ftmp[:, par, 0:cols], redc[:, 2:3], -0.5,
                    ALU.mult, ALU.is_lt), waits=[], inc="V")
                t3 = p.emit("vector", lambda e, par=par, cols=cols: e.tensor_tensor(
                    ftern[:, 0:cols], fb1[:, 0:cols], ftmp[:, par, 0:cols],
                    ALU.subtract), waits=[], inc="V")
                lin[par] = dma("sync", dst[:, k, :], ftern[:, 0:cols], waits=[t3])
        t_prelude_dS = p.ck("dS")
        esp.close()

        barrier = {s: p.ck(s) for s in ("PE", "V", "S", "dG")}

        with contextlib.ExitStack() as esa:
            def sba(name, shape, dt=F32):
                return esa.enter_context(nc.sbuf_tensor(name, shape, dt))

            xsb = sba("xsb", [128, 1, KC, NTOK])
            sq = sba("sq", [128, 2, NTOK])
            xnhi = sba("xnhi", [128, KC, NTOK], F16)
            xnlo = sba("xnlo", [128, KC, NTOK], BF16)
            qf = sba("qf", [128, KC, NTOK])
            kf = sba("kf", [128, KC, NTOK])
            gs = sba("gs", [128, KC, NTOK])
            vtm = sba("vtm", [128, 2, DI])
            ktm = sba("ktm", [128, 2, DI])
            pmk = sba("pmk", [128, NTOK])
            o1sb = sba("o1sb", [128, NTOK])
            srm = sba("srm", [128, NTOK])
            ohi = sba("ohi", [128, KC, NTOK], F16)
            olo = sba("olo", [128, KC, NTOK], BF16)
            ssb = sba("ssb", [128, KC, NTOK])
            x2sb = sba("x2sb", [128, 2, KC, NTOK])
            i1sb = sba("i1sb", [128, KC, NTOK], BF16)
            accp = sba("accp", [128, NTOK])
            accn = sba("accn", [128, NTOK])
            am = sba("am", [128, NTOK])
            amf = sba("amf", [64, NTOK])
            rbc_sb = sba("rbc_sb", [128, NTOK])
            bsb = sba("bsb", [128, 512])
            ufin = sba("ufin", [128, 512])
            hp = [sba("hp0", [128, KC, NTOK]), sba("hp1", [128, KC, NTOK])]
            wbh = sba("wbh", [128, 2, KC, DI], F16)
            wbl = sba("wbl", [128, 2, KC, DI], BF16)

            th0 = p.emit("vector", lambda e: e.memset(hp[0][:, :, :], 0.0),
                         waits=[t_prelude_dS], inc="V")
            p.emit("vector", lambda e: e.memset(hp[1][:, :, :], 0.0),
                   waits=[], inc="V")

            WROW = (2, 0, 1, 3, 4)
            w_free = {}
            w_ready = {}

            def prefetch_w(m):
                if m >= nta * 5 or m in w_ready:
                    return
                par = m % 2
                wi = WROW[m % 5]
                p.emit("sync", lambda e, wi=wi, par=par: e.dma_start(
                    out=wbh[:, par, :, :], in_=whi_s[wi, :, :, :]),
                    waits=[w_free.get(m - 2), t_prelude_dS], inc="dS", inc_n=16)
                w_ready[m] = p.emit("sync", lambda e, wi=wi, par=par: e.dma_start(
                    out=wbl[:, par, :, :], in_=wlo_s[wi, :, :, :]),
                    waits=[], inc="dS", inc_n=16)

            x_free = {}
            x_ready = {}

            def prefetch_x(t):
                if t >= nta or t in x_ready:
                    return
                x_ready[t] = p.emit("sync", lambda e, t=t: e.dma_start(
                    out=xsb[:, 0, :, :],
                    in_=x_t[:, t, :].rearrange("(k p) n -> p k n", p=128),
                    ), waits=[x_free.get(t - 1), t_prelude_dS],
                    inc="dS", inc_n=16)

            prefetch_w(0); prefetch_w(1)
            prefetch_x(0)

            prev_store = None
            defer_ctx = {}

            def emit_deferred(dt, piece=None):
                nonlocal prev_store, ti1_last, prev_iv, prev_st1
                ctx = defer_ctx[dt]
                dpar = ctx["par"]
                x2c = lambda c: x2sb[:, dpar, c, :]
                if piece is not None and piece > 0:
                    tbs1 = ctx["tbs1"]
                    pairs = {1: [0], 2: [2], 3: [4, 6]}[piece]
                    for cc in pairs:
                        emit_ladder_pair(dt, ctx, cc, x2c, tbs1)
                    return
                fold_last = None
                for c in range(KC):
                    if c == 0:
                        p.emit("vector", lambda e, c=c: e.tensor_copy(
                            accp[:, :], x2c(c)), waits=[ctx["x2_toks"][0]],
                            inc="V")
                        fold_last = p.emit("vector", lambda e, c=c: e.tensor_copy(
                            accn[:, :], x2c(c)), waits=[], inc="V")
                    else:
                        p.emit("vector", lambda e, c=c: e.tensor_tensor(
                            accp[:, :], accp[:, :], x2c(c), ALU.max),
                            waits=[ctx["x2_toks"][c]], inc="V")
                        fold_last = p.emit("vector", lambda e, c=c: e.tensor_tensor(
                            accn[:, :], accn[:, :], x2c(c), ALU.min),
                            waits=[], inc="V")
                p.emit("vector", lambda e: e.scalar_tensor_tensor(
                    am[:, :], accn[:, :], -1.0, accp[:, :], ALU.mult, ALU.max),
                    waits=[fold_last], inc="V")
                for half in (64, 32):
                    p.emit("vector", lambda e, h=half: e.tensor_copy(
                        amf[0:h, :], am[h:2 * h, :]), waits=[], inc="V")
                    p.emit("vector", lambda e, h=half: e.tensor_tensor(
                        am[0:h, :], am[0:h, :], amf[0:h, :], ALU.max),
                        waits=[], inc="V")
                for half in (16, 8, 4, 2, 1):
                    msk = [(i + half) % 32 for i in range(32)]
                    p.emit("vector", lambda e, m=msk: e.stream_shuffle(
                        amf[0:32, :], am[0:32, :], m), waits=[], inc="V")
                    p.emit("vector", lambda e, h=half: e.tensor_tensor(
                        am[0:h, :], am[0:h, :], amf[0:h, :], ALU.max),
                        waits=[], inc="V")
                p.emit("vector", lambda e: e.tensor_scalar(
                    rows[0:1, 0:NTOK], am[0:1, :], 1e-5, None, ALU.max),
                    waits=[], inc="V")
                p.emit("vector", lambda e: e.reciprocal(
                    rows[0:1, NTOK:2 * NTOK], rows[0:1, 0:NTOK]),
                    waits=[], inc="V")
                ts1 = p.emit("vector", lambda e: e.tensor_scalar(
                    rows2[0:1, 0:NTOK], rows[0:1, NTOK:2 * NTOK], 127.0, None,
                    ALU.mult), waits=[], inc="V")
                tiv = p.emit("vector", lambda e: e.tensor_scalar(
                    rows2[0:1, NTOK:2 * NTOK], rows[0:1, 0:NTOK], 1.0 / 127.0,
                    None, ALU.mult), waits=[], inc="V")
                tbs1 = p.emit("tensor", lambda e: e.matmul(
                    pb[1][:, 0:NTOK], ones_r[:, :], rows2[0:1, 0:NTOK],
                    start=True, stop=True), waits=[ts1], inc="PE")
                tivd = p.emit("gpsimd", lambda e, dt=dt: e.dma_start(
                    out=iv1_s[dt, :, :], in_=rows2[0:1, NTOK:2 * NTOK]),
                    waits=[tiv], inc="dG", inc_n=16)
                prev_iv = tivd
                ctx["tbs1"] = tbs1
                ctx["fold_last"] = fold_last
                if piece is None:
                    for cc in range(0, KC, 2):
                        emit_ladder_pair(dt, ctx, cc, x2c, tbs1)

            def emit_ladder_pair(dt, ctx, cc, x2c, tbs1):
                nonlocal prev_store, prev_st1, ti1_last
                u0v = pb[5][:, 0:512]
                pgA = pb[6][:, 0:512]
                pgB = pb[7][:, 0:512]
                p.emit("vector", lambda e, cc=cc, u0v=u0v: e.tensor_tensor(
                    u0v[:, 0:NTOK], x2c(cc), pb[1][:, 0:NTOK], ALU.mult),
                    waits=[tbs1, prev_st1 if cc == 0 else None], inc="V")
                p.emit("vector", lambda e, cc=cc, u0v=u0v: e.tensor_tensor(
                    u0v[:, NTOK:2 * NTOK], x2c(cc + 1), pb[1][:, 0:NTOK],
                    ALU.mult), waits=[], inc="V")
                p.emit("vector", lambda e, u0v=u0v: e.tensor_scalar(
                    u0v, u0v, 128.0, None, ALU.add), waits=[], inc="V")
                srcv = u0v
                for ki, k in enumerate(range(7, -1, -1)):
                    cth = float(2 ** k) - 0.5
                    p.emit("vector", lambda e, srcv=srcv, cth=cth, k=k:
                           e.tensor_scalar(
                        bsb[:, :], srcv, cth, -float(2 ** k),
                        ALU.is_ge, ALU.mult), waits=[], inc="V")
                    dstv = ufin[:, :] if k == 0 else (pgA if ki % 2 == 0 else pgB)
                    p.emit("vector", lambda e, srcv=srcv, dstv=dstv:
                           e.tensor_tensor(
                        dstv, bsb[:, :], srcv, ALU.add), waits=[], inc="V")
                    srcv = dstv
                p.emit("vector", lambda e, u0v=u0v: e.scalar_tensor_tensor(
                    bsb[:, :], ufin[:, :], -1.0, u0v, ALU.mult, ALU.add),
                    waits=[], inc="V")
                p.emit("vector", lambda e, cc=cc: e.tensor_scalar(
                    i1sb[:, cc, :], bsb[:, 0:NTOK], -128.0, None, ALU.add),
                    waits=[], inc="V")
                ti1 = p.emit("vector", lambda e, cc=cc: e.tensor_scalar(
                    i1sb[:, cc + 1, :], bsb[:, NTOK:2 * NTOK], -128.0, None,
                    ALU.add), waits=[], inc="V")
                ti1_last = ti1
                if cc == KC - 2:
                    defer_ctx.pop(dt)
                    prev_st1 = p.emit("gpsimd", lambda e, dt=dt: e.dma_start(
                        out=i1_s[dt, :, :, :], in_=i1sb[:, :, :]), waits=[ti1],
                        inc="dG", inc_n=16)
                    prev_store = p.emit("gpsimd", lambda e, dt=dt: e.dma_start(
                        out=x2_s[dt, :, :, :], in_=x2sb[:, ctx["par"], :, :]),
                        waits=[ctx["fold_last"]], inc="dG", inc_n=16)

            ti1_last = None
            prev_iv = None
            prev_st1 = None
            prev_sqpe = None
            pb01_free = None
            prev_S = None
            prev_tlo = None
            wo_v_ck = None
            for t in range(nta):
                par = t % 2
                bar = []
                xc = lambda c: xsb[:, 0, c, :]
                tx = x_ready[t]

                if pha < 1:
                    barrier = {s: p.ck(s) for s in ("PE", "V", "S", "dG")}
                    continue
                sq_pe = []
                mm1_last = None
                for c in range(KC):
                    tsqr = p.emit("scalar", lambda e, c=c: e.activation(
                        sq[:, c % 2, :], xc(c), ACT.Square),
                        waits=[tx, (sq_pe[c - 2] if c >= 2 else prev_sqpe)],
                        inc="S")
                    mm1_last = p.emit("tensor", lambda e, c=c: e.matmul(
                        pb[0][0:1, 0:NTOK], ones_c[:, :], xc(c),
                        start=(c == 0), stop=(c == KC - 1), skip_group_check=True),
                        waits=[tx, t_consts, pb01_free],
                        inc="PE" if c == KC - 1 else None)
                    sq_pe.append(p.emit("tensor", lambda e, c=c: e.matmul(
                        pb[7][0:1, 0:NTOK], ones_c[:, :], sq[:, c % 2, :],
                        start=(c == 0), stop=(c == KC - 1), skip_group_check=True), waits=[tsqr], inc="PE"))

                def ln_rows(tok_mm, prev_dma=None):
                    tmu = p.emit("vector", lambda e: e.tensor_scalar(
                        rows[0:1, 0:NTOK], pb[0][0:1, 0:NTOK], 1.0 / D, None,
                        ALU.mult), waits=[tok_mm, prev_dma], inc="V")
                    p.emit("vector", lambda e: e.tensor_scalar(
                        rows[0:1, NTOK:2 * NTOK], pb[7][0:1, 0:NTOK],
                        1.0 / D, None, ALU.mult), waits=[], inc="V")
                    p.emit("vector", lambda e: e.tensor_tensor(
                        rows2[0:1, 0:NTOK], rows[0:1, 0:NTOK], rows[0:1, 0:NTOK],
                        ALU.mult), waits=[], inc="V")
                    tvar = p.emit("vector", lambda e: e.tensor_tensor(
                        rows2[0:1, NTOK:2 * NTOK], rows[0:1, NTOK:2 * NTOK],
                        rows2[0:1, 0:NTOK], ALU.subtract), waits=[], inc="V")
                    tsq = p.emit("scalar", lambda e: e.activation(
                        rows2[0:1, 0:NTOK], rows2[0:1, NTOK:2 * NTOK], ACT.Sqrt,
                        bias=eps1[:, :]), waits=[tvar], inc="S")
                    trr = p.emit("vector", lambda e: e.reciprocal(
                        rows[0:1, NTOK:2 * NTOK], rows2[0:1, 0:NTOK]),
                        waits=[tsq], inc="V")
                    tbc1 = p.emit("tensor", lambda e: e.matmul(
                        pb[0][:, 0:NTOK], ones_r[:, :], rows[0:1, 0:NTOK],
                        start=True, stop=True), waits=[tmu, trr], inc="PE")
                    tbc2 = p.emit("tensor", lambda e: e.matmul(
                        pb[0][:, NTOK:2 * NTOK], ones_r[:, :],
                        rows[0:1, NTOK:2 * NTOK], start=True, stop=True),
                        waits=[trr], inc="PE")
                    trbc = p.emit("vector", lambda e: e.tensor_copy(
                        rbc_sb[:, :], pb[0][:, NTOK:2 * NTOK]), waits=[tbc2],
                        inc="V")
                    return tbc1, tbc2, trbc

                tbc1, tbc2, trbc = ln_rows(sq_pe[-1], prev_iv)
                if pha < 2:
                    barrier = {s: p.ck(s) for s in ("PE", "V", "S", "dG")}
                    continue

                for c in range(KC):
                    reg = pb[2] if c % 2 == 0 else pb[3]
                    p.emit("vector", lambda e, c=c, reg=reg: e.tensor_tensor(
                        reg[:, 0:NTOK], xc(c), pb[0][:, 0:NTOK], ALU.subtract),
                        waits=[tbc2], inc="V")
                    p.emit("vector", lambda e, reg=reg: e.tensor_tensor(
                        reg[:, NTOK:2 * NTOK], reg[:, 0:NTOK], rbc_sb[:, :],
                        ALU.mult), waits=[trbc], inc="V")
                    p.emit("vector", lambda e, c=c, reg=reg: e.tensor_scalar(
                        xnhi[:, c, :], reg[:, NTOK:2 * NTOK], gam[:, c:c + 1],
                        bet[:, c:c + 1], ALU.mult, ALU.add), waits=[], inc="V")
                    p.emit("vector", lambda e, c=c, reg=reg: e.tensor_scalar(
                        reg[:, 0:NTOK], reg[:, NTOK:2 * NTOK], gam[:, c:c + 1],
                        bet[:, c:c + 1], ALU.mult, ALU.add), waits=[], inc="V")
                    p.emit("vector", lambda e, c=c, reg=reg: e.tensor_tensor(
                        xnlo[:, c, :], reg[:, 0:NTOK], xnhi[:, c, :], ALU.subtract),
                        waits=[], inc="V")
                xn_done = p.ck("V")

                if pha < 3:
                    barrier = {s: p.ck(s) for s in ("PE", "V", "S", "dG")}
                    continue
                regions = [pb[2][:, 0:NTOK], pb[3][:, 0:NTOK],
                           pb[4][:, 0:NTOK]]
                regions2 = [pb[2][:, NTOK:2 * NTOK], pb[3][:, NTOK:2 * NTOK],
                            pb[4][:, NTOK:2 * NTOK]]
                reg_free = [None, None, None]
                reg_i = [0]

                def next_region():
                    i = reg_i[0] % 3
                    reg_i[0] += 1
                    return i, regions[i]

                def proj3(reg, stat_hi, stat_lo, mov_hi, mov_lo, first_waits):
                    last = None
                    for k in range(KC):
                        fst = (k == 0)
                        lst = (k == KC - 1)
                        p.emit("tensor", lambda e, k=k, reg=reg, sh=stat_hi,
                               mh=mov_hi, fst=fst: e.matmul(
                            reg, sh(k), mh(k), start=fst, stop=False),
                            waits=first_waits if fst else [], inc=None)
                        p.emit("tensor", lambda e, k=k, reg=reg, sh=stat_hi,
                               ml=mov_lo: e.matmul(
                            reg, sh(k), ml(k), start=False, stop=False),
                            waits=[], inc=None)
                        last = p.emit("tensor", lambda e, k=k, reg=reg, sl=stat_lo,
                                      mh=mov_hi, lst=lst: e.matmul(
                            reg, sl(k), mh(k), start=False, stop=lst),
                            waits=[], inc="PE" if lst else None)
                    return last

                m_v = t * 5 + 0
                wv_par, wv_tok = m_v % 2, w_ready[m_v]
                vt_ev = []
                for e2 in range(2):
                    for hf in range(2):
                        i = (e2 * 2 + hf) % 2
                        regfull = pb[4] if i == 0 else pb[5]
                        last = proj3(
                            regfull[:, 0:512],
                            lambda k, e2=e2: xnhi[:, k, e2 * 128:(e2 + 1) * 128],
                            lambda k, e2=e2: xnlo[:, k, e2 * 128:(e2 + 1) * 128],
                            lambda k, wv_par=wv_par, hf=hf:
                                wbh[:, wv_par, k, hf * 512:(hf + 1) * 512],
                            lambda k, wv_par=wv_par, hf=hf:
                                wbl[:, wv_par, k, hf * 512:(hf + 1) * 512],
                            [wv_tok, xn_done, wo_v_ck,
                             (vt_ev[-2] if len(vt_ev) >= 2 else None)])
                        vt_ev.append(p.emit("vector",
                            lambda e, e2=e2, hf=hf, regfull=regfull:
                            e.tensor_tensor(
                                vtm[:, e2, hf * 512:(hf + 1) * 512],
                                regfull[:, 0:512], bvbc[:, hf * 512:(hf + 1) * 512],
                                ALU.add), waits=[last, t_bvbc], inc="V"))
                vtm_done = p.ck("V")
                w_free[m_v] = p.ck("PE")
                prefetch_w(m_v + 2)

                if pha < 4:
                    barrier = {s: p.ck(s) for s in ("PE", "V", "S", "dG")}
                    continue
                reg_free[0] = reg_free[1] = xn_done
                reg_free[2] = vt_ev[3]
                reg_i[0] = 0
                ev_toks = {"q": [], "k": [], "g": []}
                for j, (nm, wn) in enumerate((("q", 1), ("k", 2), ("g", 3))):
                    m = t * 5 + wn
                    wpar, wtok = m % 2, w_ready[m]
                    for c in range(KC):
                        if nm == "q" and (t - 1) in defer_ctx:
                            if c == 0:
                                emit_deferred(t - 1, piece=0)
                            elif c in (2, 4, 6):
                                emit_deferred(t - 1, piece=c // 2)
                        ri, reg = next_region()
                        last = proj3(
                            reg,
                            lambda k, wpar=wpar, c=c:
                                wbh[:, wpar, k, c * 128:(c + 1) * 128],
                            lambda k, wpar=wpar, c=c:
                                wbl[:, wpar, k, c * 128:(c + 1) * 128],
                            lambda k: xnhi[:, k, :],
                            lambda k: xnlo[:, k, :],
                            [wtok, xn_done, reg_free[ri]])
                        if nm == "q":
                            for bb in range(BC):
                                tok = p.emit("vector", lambda e, c=c, reg=reg,
                                             bb=bb: e.scalar_tensor_tensor(
                                    qf[:, c, bb * 64:(bb + 1) * 64],
                                    reg[:, bb * 64:(bb + 1) * 64],
                                    bias["bq"][:, c:c + 1], aq[:, c, :],
                                    ALU.add, ALU.mult),
                                    waits=[last, tok_atiles], inc="V")
                        elif nm == "k":
                            for bb in range(BC):
                                tok = p.emit("vector", lambda e, c=c, reg=reg,
                                             bb=bb: e.scalar_tensor_tensor(
                                    kf[:, c, bb * 64:(bb + 1) * 64],
                                    reg[:, bb * 64:(bb + 1) * 64],
                                    bias["bk"][:, c:c + 1], ak[:, c, :],
                                    ALU.add, ALU.mult),
                                    waits=[last], inc="V")
                        else:
                            tok = p.emit("scalar", lambda e, c=c, reg=reg:
                                         e.activation(
                                gs[:, c, :], reg, ACT.Silu,
                                bias=bias["bg"][:, c:c + 1]),
                                waits=[last, prev_tlo], inc="S")
                        ev_toks[nm].append(tok)
                        reg_free[ri] = tok
                    w_free[m] = p.ck("PE")
                    prefetch_w(m + 2)
                    pass

                if pha < 5:
                    barrier = {s: p.ck(s) for s in ("PE", "V", "S", "dG")}
                    continue
                tr_banks = [pb[0], pb[6], pb[1], pb[7]]
                ktr_ev = []
                nidx = 0
                for c in range(KC):
                    for bp in range(2):
                        for hq in range(2):
                            bank = tr_banks[nidx % 4]
                            ttr = p.emit("tensor",
                                lambda e, c=c, bp=bp, hq=hq, bank=bank:
                                e.transpose(
                                    bank[0:128, NTOK:NTOK + 64],
                                    kf[hq * 64:hq * 64 + 64, c,
                                       bp * 128:(bp + 1) * 128],
                                    ident[hq * 64:hq * 64 + 64, :],
                                    tile_position=(hq * 64, 0)),
                                waits=[ev_toks["k"][c],
                                       (ktr_ev[-4] if len(ktr_ev) >= 4 else None)],
                                inc="PE")
                            ktr_ev.append(p.emit("vector",
                                lambda e, c=c, bp=bp, hq=hq, bank=bank:
                                e.tensor_copy(
                                    ktm[:, bp, c * 128 + hq * 64:
                                        c * 128 + hq * 64 + 64],
                                    bank[0:128, NTOK:NTOK + 64]),
                                waits=[ttr], inc="V"))
                            nidx += 1
                ktm_done = p.ck("V")

                if pha < 6:
                    barrier = {s: p.ck(s) for s in ("PE", "V", "S", "dG")}
                    continue
                prev_scan_v = None
                prev_maskv = None
                og_done = None
                o1_done = None
                tadd_done = None
                srm_done = None
                tlo_last = None
                for c in range(KC):
                    pmm = None
                    for b4 in range(BC):
                        for hq in range(2):
                            bank = pb[6] if hq == 0 else pb[7]
                            pmm = p.emit("tensor", lambda e, c=c, b4=b4, hq=hq,
                                         bank=bank: e.matmul(
                                bank[(b4 % 2) * 64:(b4 % 2) * 64 + 64,
                                     (b4 // 2) * 64:(b4 // 2) * 64 + 64],
                                kf[hq * 64:hq * 64 + 64, c, b4 * 64:(b4 + 1) * 64],
                                qf[hq * 64:hq * 64 + 64, c, b4 * 64:(b4 + 1) * 64],
                                start=True, stop=True,
                                tile_position=(hq * 64, (b4 % 2) * 64)),
                                waits=[ev_toks["q"][c], ev_toks["k"][c],
                                       prev_maskv, ktm_done], inc="PE")
                    pmkv = pmk[:, :].rearrange("p (a h t) -> p a h t", a=2, h=2)
                    maskv = mask[:, 0:128].rearrange("p (a t) -> p a t", a=2)
                    tpm0 = p.emit("vector", lambda e, pmkv=pmkv, maskv=maskv:
                                  e.tensor_tensor(
                        pmkv[:, :, 0, :],
                        pb[6][:, 0:128].rearrange("p (a t) -> p a t", a=2),
                        maskv, ALU.mult), waits=[pmm], inc="V")
                    tpm = p.emit("vector", lambda e, pmkv=pmkv, maskv=maskv:
                                 e.tensor_tensor(
                        pmkv[:, :, 1, :],
                        pb[7][:, 0:128].rearrange("p (a t) -> p a t", a=2),
                        maskv, ALU.mult), waits=[], inc="V")
                    prev_maskv = tpm
                    omm = None
                    for b4 in range(BC):
                        for hq in range(2):
                            p.emit("tensor", lambda e, c=c, b4=b4, hq=hq: e.matmul(
                                pb[2][(b4 % 2) * 64:(b4 % 2) * 64 + 64,
                                      (b4 // 2) * 128 + hq * 64:
                                      (b4 // 2) * 128 + hq * 64 + 64],
                                vtm[(b4 % 2) * 64:(b4 % 2) * 64 + 64, b4 // 2,
                                    c * 128 + hq * 64:c * 128 + hq * 64 + 64],
                                pmk[(b4 % 2) * 64:(b4 % 2) * 64 + 64,
                                    (b4 // 2) * 128 + hq * 64:
                                    (b4 // 2) * 128 + hq * 64 + 64],
                                start=True, stop=True,
                                tile_position=((b4 % 2) * 64, (b4 % 2) * 64)),
                                waits=[tpm, vtm_done, o1_done], inc=None)
                            omm = p.emit("tensor", lambda e, c=c, b4=b4, hq=hq,
                                         par=par: e.matmul(
                                pb[3][hq * 64:hq * 64 + 64, b4 * 64:(b4 + 1) * 64],
                                hp[par][hq * 64:hq * 64 + 64, c,
                                        b4 * 64:(b4 + 1) * 64],
                                qf[hq * 64:hq * 64 + 64, c, b4 * 64:(b4 + 1) * 64],
                                start=True, stop=True,
                                tile_position=(hq * 64, hq * 64)),
                                waits=[th0 if t == 0 else None, tadd_done],
                                inc="PE")
                    smm = None
                    for b4 in range(BC):
                        for hq in range(2):
                            smm = p.emit("tensor", lambda e, c=c, b4=b4, hq=hq:
                                         e.matmul(
                                pb[4][(b4 % 2) * 64:(b4 % 2) * 64 + 64,
                                      (b4 // 2) * 128 + hq * 64:
                                      (b4 // 2) * 128 + hq * 64 + 64],
                                ktm[(b4 % 2) * 64:(b4 % 2) * 64 + 64, b4 // 2,
                                    c * 128 + hq * 64:c * 128 + hq * 64 + 64],
                                vtm[(b4 % 2) * 64:(b4 % 2) * 64 + 64, b4 // 2,
                                    c * 128 + hq * 64:c * 128 + hq * 64 + 64],
                                start=True, stop=True,
                                tile_position=((b4 % 2) * 64, (b4 % 2) * 64)),
                                waits=[srm_done], inc="PE")
                    o1v = o1sb[:, :].rearrange("p (b t) -> p b t", b=BC)
                    cp = None
                    for hq in range(2):
                        for bpar in range(2):
                            cp = p.emit("vector", lambda e, hq=hq, bpar=bpar,
                                        o1v=o1v: e.tensor_copy(
                                o1v[hq * 64:hq * 64 + 64, bpar::2, :],
                                pb[2][bpar * 64:bpar * 64 + 64, 0:256].rearrange(
                                    "p (bb r) -> p bb r", bb=2)[:, :, hq * 64:
                                                               hq * 64 + 64]),
                                waits=[omm], inc="V")
                    o1_done = cp
                    srmv = srm[:, :].rearrange("p (b t) -> p b t", b=BC)
                    for hq in range(2):
                        for bpar in range(2):
                            srm_done = p.emit("vector", lambda e, hq=hq, bpar=bpar,
                                   srmv=srmv: e.tensor_copy(
                                srmv[hq * 64:hq * 64 + 64, bpar::2, :],
                                pb[4][bpar * 64:bpar * 64 + 64, 0:256].rearrange(
                                    "p (bb r) -> p bb r", bb=2)[:, :, hq * 64:
                                                               hq * 64 + 64]),
                                waits=[smm], inc="V")
                    tadd = p.emit("vector", lambda e: e.tensor_tensor(
                        pb[3][:, NTOK:2 * NTOK], o1sb[:, :], pb[3][:, 0:NTOK],
                        ALU.add), waits=[], inc="V")
                    tog = p.emit("vector", lambda e, c=c: e.tensor_tensor(
                        pb[5][:, 0:NTOK], pb[3][:, NTOK:2 * NTOK], gs[:, c, :],
                        ALU.mult), waits=[ev_toks["g"][c]], inc="V")
                    tadd_done = tog
                    p.emit("vector", lambda e, c=c: e.tensor_copy(
                        ohi[:, c, :], pb[5][:, 0:NTOK]), waits=[], inc="V")
                    tlo_last = p.emit("vector", lambda e, c=c: e.tensor_tensor(
                        olo[:, c, :], pb[5][:, 0:NTOK], ohi[:, c, :],
                        ALU.subtract), waits=[], inc="V")
                    og_done = tlo_last
                    p.emit("vector", lambda e, c=c, par=par: e.tensor_tensor(
                        pb[5][:, NTOK:2 * NTOK], hp[par][:, c, :], srm[:, :],
                        ALU.add), waits=[], inc="V")
                    prev_scan_v = p.emit("vector", lambda e, c=c, par=par:
                                         e.tensor_scalar(
                        hp[1 - par][:, c, :], pb[5][:, NTOK:2 * NTOK],
                        a64[:, c:c + 1], None, ALU.mult), waits=[], inc="V")
                scan_v_done = p.ck("V")

                m_o = t * 5 + 4
                wo_par, wo_tok = m_o % 2, w_ready[m_o]
                s_toks = []
                for mc in range(KC):
                    ri, reg = next_region()
                    reg2 = regions2[ri]
                    last = proj3(
                        reg,
                        lambda k, wo_par=wo_par, mc=mc:
                            wbh[:, wo_par, k, mc * 128:(mc + 1) * 128],
                        lambda k, wo_par=wo_par, mc=mc:
                            wbl[:, wo_par, k, mc * 128:(mc + 1) * 128],
                        lambda k: ohi[:, k, :],
                        lambda k: olo[:, k, :],
                        [wo_tok, tlo_last, reg_free[ri], scan_v_done])
                    tb = p.emit("vector", lambda e, mc=mc, reg=reg, reg2=reg2:
                                e.tensor_scalar(
                        reg2, reg, bias["bo"][:, mc:mc + 1], None, ALU.add),
                        waits=[last], inc="V")
                    stok = p.emit("vector", lambda e, mc=mc, reg2=reg2:
                                  e.scalar_tensor_tensor(
                        ssb[:, mc, :], xc(mc), 2.0, reg2, ALU.mult, ALU.add),
                        waits=[prev_S if mc == 0 else None], inc="V")
                    s_toks.append(stok)
                    reg_free[ri] = stok
                x_free[t] = s_toks[-1]
                wo_v_ck = p.ck("V")
                w_free[m_o] = p.ck("PE")
                prefetch_w(m_o + 2)
                prefetch_x(t + 1)

                if pha < 8:
                    barrier = {s: p.ck(s) for s in ("PE", "V", "S", "dG")}
                    continue
                sq_pe2 = []
                mm2_last = None
                for c in range(KC):
                    tsqr = p.emit("scalar", lambda e, c=c: e.activation(
                        sq[:, c % 2, :], ssb[:, c, :], ACT.Square),
                        waits=[s_toks[c], (sq_pe2[c - 2] if c >= 2 else None)],
                        inc="S")
                    mm2_last = p.emit("tensor", lambda e, c=c: e.matmul(
                        pb[0][0:1, 0:NTOK], ones_c[:, :], ssb[:, c, :],
                        start=(c == 0), stop=(c == KC - 1), skip_group_check=True),
                        waits=[s_toks[c]], inc="PE" if c == KC - 1 else None)
                    sq_pe2.append(p.emit("tensor", lambda e, c=c: e.matmul(
                        pb[7][0:1, 0:NTOK], ones_c[:, :], sq[:, c % 2, :],
                        start=(c == 0), stop=(c == KC - 1), skip_group_check=True), waits=[tsqr],
                        inc="PE"))
                tbc1, tbc2, trbc = ln_rows(sq_pe2[-1], prev_iv)

                x2_toks = []
                for c in range(KC):
                    reg = pb[2] if c % 2 == 0 else pb[3]
                    p.emit("vector", lambda e, c=c, reg=reg: e.tensor_tensor(
                        reg[:, 0:NTOK], ssb[:, c, :], pb[0][:, 0:NTOK],
                        ALU.subtract), waits=[tbc2], inc="V")
                    x2_toks.append(p.emit("vector", lambda e, c=c, reg=reg,
                                          par=par: e.tensor_tensor(
                        x2sb[:, par, c, :], reg[:, 0:NTOK], rbc_sb[:, :],
                        ALU.mult), waits=[trbc, prev_store], inc="V"))
                pb01_free = p.ck("V")
                defer_ctx[t] = dict(par=par, x2_toks=list(x2_toks))
                prev_sqpe = sq_pe2[-1]
                prev_S = p.ck("S")
                prev_tlo = tlo_last

                if t == 0 and dump:
                    taps = [("xnhi", xnhi, F16), ("xnlo", xnlo, BF16),
                            ("qf", qf, F32), ("kf", kf, F32), ("gs", gs, F32),
                            ("ohi", ohi, F16), ("olo", olo, BF16),
                            ("ssb", ssb, F32), ("x2sb", x2sb[:, 0], F32),
                            ("hp1", hp[1], F32)]
                    vck = p.ck("V")
                    for nm, tile, dt in taps:
                        dp = dump_out(nm, [128, KC, NTOK], dt)
                        if dp is not None:
                            p.emit("gpsimd", lambda e, dp=dp, tile=tile:
                                   e.dma_start(out=dp[:, :, :], in_=tile[:, :, :]),
                                   waits=[vck], inc="dG", inc_n=16)
                    for nm, tile in (("aq", aq), ("ak", ak)):
                        dp = dump_out(nm, [128, KC, 64])
                        if dp is not None:
                            p.emit("gpsimd", lambda e, dp=dp, tile=tile:
                                   e.dma_start(out=dp[:, :, :], in_=tile[:, :, :]),
                                   waits=[vck], inc="dG", inc_n=16)
                    for nm, tile in (("vtm", vtm), ("ktm", ktm)):
                        dp = dump_out(nm, [128, 2, DI])
                        if dp is not None:
                            p.emit("gpsimd", lambda e, dp=dp, tile=tile:
                                   e.dma_start(out=dp[:, :, :], in_=tile[:, :, :]),
                                   waits=[vck], inc="dG", inc_n=16)
                    p.emit("vector", lambda e: e.engine_nop(),
                           waits=[p.ck("dG")], inc="V")

                barrier = {s: p.ck(s) for s in ("PE", "V", "S", "dG")}

            if (nta - 1) in defer_ctx:
                emit_deferred(nta - 1)
            passA_dG = p.ck("dG")
            passA_V = p.ck("V")

        with contextlib.ExitStack() as esb:
            def sbb(name, shape, dt=F32):
                return esb.enter_context(nc.sbuf_tensor(name, shape, dt))

            tg_sb = sbb("tg_sb", [128, KC, FF], FP8)
            tu_sb = sbb("tu_sb", [128, KC, FF], FP8)
            td_sb = sbb("td_sb", [128, OC, D], FP8)
            i1b = sbb("i1b", [128, 1, KC, NTOK], BF16)
            x2b = sbb("x2b", [128, 1, KC, NTOK])
            iv1r = sbb("iv1r", [1, 1, NTOK])
            iv1bc2 = sbb("iv1bc2", [128, 512])
            iv2bc = sbb("iv2bc", [128, NTOK])
            g1 = sbb("g1", [128, 2, 512])
            gate = sbb("gate", [128, 512])
            hsb = sbb("hsb", [128, OC * NTOK])
            i2sb = sbb("i2sb", [128, OC * NTOK], BF16)
            s3sb = sbb("s3sb", [128, KC, NTOK])
            accpB = sbb("accpB", [128, NTOK])
            accpB2 = sbb("accpB2", [128, 512])
            accnB2 = sbb("accnB2", [128, 512])
            accnB = sbb("accnB", [128, NTOK])
            amB = sbb("amB", [128, NTOK])
            amfB = sbb("amfB", [64, NTOK])
            rbcB = sbb("rbcB", [128, NTOK])
            bsbB = sbb("bsbB", [128, 1024])
            ufinB = sbb("ufinB", [128, 1024])
            sqB = sbb("sqB", [128, 2, NTOK])

            passA_PE = p.ck("PE")
            passA_S = p.ck("S")
            for dst2, src2 in [(tg_sb[:, :, :], tg_s[:, :, :]),
                               (tu_sb[:, :, :], tu_s[:, :, :]),
                               (td_sb[:, :, :], td_s[:, :, :])]:
                dma("sync", dst2, src2, waits=[passA_dG, passA_PE, passA_V,
                                               passA_S])
            tT = p.ck("dS")

            xb_free = {}
            xb_ready = {}

            def prefetch_b(t):
                if t >= ntb or t in xb_ready:
                    return
                p.emit("sync", lambda e, t=t: e.dma_start(
                    out=i1b[:, 0, :, :], in_=i1_s[t, :, :, :]),
                    waits=[xb_free.get(t - 1), passA_dG, passA_PE, passA_S,
                           passA_V], inc="dS", inc_n=16)
                p.emit("sync", lambda e, t=t: e.dma_start(
                    out=x2b[:, 0, :, :], in_=x2_s[t, :, :, :]),
                    waits=[], inc="dS", inc_n=16)
                xb_ready[t] = p.emit("sync", lambda e, t=t: e.dma_start(
                    out=iv1r[:, 0, :], in_=iv1_s[t, :, :]),
                    waits=[], inc="dS", inc_n=16)

            prefetch_b(0)
            barB_V = passA_V
            prev_out = None

            for t in range(ntb):
                par = t % 2
                tin = xb_ready[t]
                tb = p.emit("tensor", lambda e, par=par: e.matmul(
                    pb[1][:, 0:NTOK], ones_r[:, :], iv1r[0:1, 0, :],
                    start=True, stop=True), waits=[tin, tT, barB_V], inc="PE")
                p.emit("vector", lambda e: e.tensor_copy(
                    iv1bc2[:, 0:NTOK], pb[1][:, 0:NTOK]),
                    waits=[tb, passA_dG], inc="V")
                tivc = p.emit("vector", lambda e: e.tensor_copy(
                    iv1bc2[:, NTOK:2 * NTOK], pb[1][:, 0:NTOK]), waits=[], inc="V")

                h_last = None
                h_toks = []
                for ocp in range(0, OC, 2):
                    pi = (ocp // 2) % 2
                    rg = pb[2 + pi * 2]
                    ru = pb[3 + pi * 2]
                    lastg = None
                    lastu = None
                    for half in range(2):
                        oc = ocp + half
                        for k in range(KC):
                            fst = (k == 0)
                            lst = (k == KC - 1)
                            lastg = p.emit("tensor", lambda e, k=k, oc=oc, rg=rg,
                                           half=half, par=par, fst=fst, lst=lst:
                                           e.matmul(
                                rg[:, half * NTOK:(half + 1) * NTOK],
                                tg_sb[:, k, oc * 128:(oc + 1) * 128],
                                i1b[:, 0, k, :], start=fst, stop=lst,
                                skip_group_check=True),
                                waits=([tin, (h_toks[-2] if len(h_toks) >= 2
                                              else None)] if fst else []),
                                inc="PE" if lst else None)
                    for half in range(2):
                        oc = ocp + half
                        for k in range(KC):
                            fst = (k == 0)
                            lst = (k == KC - 1)
                            lastu = p.emit("tensor", lambda e, k=k, oc=oc, ru=ru,
                                           half=half, par=par, fst=fst, lst=lst:
                                           e.matmul(
                                ru[:, half * NTOK:(half + 1) * NTOK],
                                tu_sb[:, k, oc * 128:(oc + 1) * 128],
                                i1b[:, 0, k, :], start=fst, stop=lst,
                                skip_group_check=True),
                                waits=[], inc="PE" if lst else None)
                    tg1 = p.emit("vector", lambda e, rg=rg:
                                 e.scalar_tensor_tensor(
                        g1[:, 0, :], rg[:, 0:512], mghat[:, 0:1], iv1bc2[:, :],
                        ALU.mult, ALU.mult), waits=[lastg, tivc], inc="V")
                    tgate = p.emit("scalar", lambda e: e.activation(
                        gate[:, :], g1[:, 0, :], ACT.Silu),
                        waits=[tg1], inc="S")
                    tu1 = p.emit("vector", lambda e, ru=ru:
                                 e.scalar_tensor_tensor(
                        g1[:, 1, :], ru[:, 0:512], mghat[:, 1:2], iv1bc2[:, :],
                        ALU.mult, ALU.mult), waits=[lastu], inc="V")
                    th = p.emit("vector", lambda e, ocp=ocp: e.tensor_tensor(
                        hsb[:, ocp * NTOK:(ocp + 2) * NTOK],
                        gate[:, :], g1[:, 1, :], ALU.mult),
                        waits=[tgate, tu1], inc="V")
                    if ocp == 0:
                        p.emit("vector", lambda e: e.tensor_copy(
                            accpB2[:, :], hsb[:, 0:2 * NTOK]),
                            waits=[], inc="V")
                        h_last = p.emit("vector", lambda e: e.tensor_copy(
                            accnB2[:, :], hsb[:, 0:2 * NTOK]),
                            waits=[], inc="V")
                    else:
                        p.emit("vector", lambda e, ocp=ocp: e.tensor_tensor(
                            accpB2[:, :], accpB2[:, :],
                            hsb[:, ocp * NTOK:(ocp + 2) * NTOK], ALU.max),
                            waits=[], inc="V")
                        h_last = p.emit("vector", lambda e, ocp=ocp: e.tensor_tensor(
                            accnB2[:, :], accnB2[:, :],
                            hsb[:, ocp * NTOK:(ocp + 2) * NTOK], ALU.min),
                            waits=[], inc="V")
                    h_toks.append(h_last)
                p.emit("vector", lambda e: e.tensor_tensor(
                    accpB[:, :], accpB2[:, 0:NTOK], accpB2[:, NTOK:2 * NTOK],
                    ALU.max), waits=[h_last], inc="V")
                h_last = p.emit("vector", lambda e: e.tensor_tensor(
                    accnB[:, :], accnB2[:, 0:NTOK], accnB2[:, NTOK:2 * NTOK],
                    ALU.min), waits=[], inc="V")

                p.emit("vector", lambda e: e.scalar_tensor_tensor(
                    amB[:, :], accnB[:, :], -1.0, accpB[:, :], ALU.mult, ALU.max),
                    waits=[h_last], inc="V")
                for half in (64, 32):
                    p.emit("vector", lambda e, h=half: e.tensor_copy(
                        amfB[0:h, :], amB[h:2 * h, :]), waits=[], inc="V")
                    p.emit("vector", lambda e, h=half: e.tensor_tensor(
                        amB[0:h, :], amB[0:h, :], amfB[0:h, :], ALU.max),
                        waits=[], inc="V")
                for half in (16, 8, 4, 2, 1):
                    msk = [(i + half) % 32 for i in range(32)]
                    p.emit("vector", lambda e, m=msk: e.stream_shuffle(
                        amfB[0:32, :], amB[0:32, :], m), waits=[], inc="V")
                    p.emit("vector", lambda e, h=half: e.tensor_tensor(
                        amB[0:h, :], amB[0:h, :], amfB[0:h, :], ALU.max),
                        waits=[], inc="V")
                p.emit("vector", lambda e: e.tensor_scalar(
                    rows[0:1, 0:NTOK], amB[0:1, :], 1e-5, None, ALU.max),
                    waits=[], inc="V")
                p.emit("vector", lambda e: e.reciprocal(
                    rows[0:1, NTOK:2 * NTOK], rows[0:1, 0:NTOK]), waits=[], inc="V")
                ts2 = p.emit("vector", lambda e: e.tensor_scalar(
                    rows2[0:1, 0:NTOK], rows[0:1, NTOK:2 * NTOK], 127.0, None,
                    ALU.mult), waits=[], inc="V")
                tiv2 = p.emit("vector", lambda e: e.tensor_scalar(
                    rows2[0:1, NTOK:2 * NTOK], rows[0:1, 0:NTOK], 1.0 / 127.0,
                    None, ALU.mult), waits=[], inc="V")
                tb1 = p.emit("tensor", lambda e: e.matmul(
                    pb[1][:, 0:NTOK], ones_r[:, :], rows2[0:1, 0:NTOK],
                    start=True, stop=True), waits=[ts2], inc="PE")
                tb1b = p.emit("tensor", lambda e: e.matmul(
                    pb[1][:, NTOK:2 * NTOK], ones_r[:, :], rows2[0:1, 0:NTOK],
                    start=True, stop=True), waits=[], inc="PE")
                tb2 = p.emit("tensor", lambda e: e.matmul(
                    pb[0][:, 0:NTOK], ones_r[:, :],
                    rows2[0:1, NTOK:2 * NTOK], start=True, stop=True),
                    waits=[tiv2], inc="PE")
                tiv2c = p.emit("vector", lambda e: e.tensor_copy(
                    iv2bc[:, :], pb[0][:, 0:NTOK]), waits=[tb2, tb1b],
                    inc="V")

                ti2 = None
                ts2c = tiv2c
                for cc in range(0, OC, 4):
                    u0v = pbW[1][:, 0:1024]
                    pgA = pbW[2][:, 0:1024]
                    pgB = pbW[3][:, 0:1024]
                    p.emit("vector", lambda e, cc=cc: e.tensor_tensor(
                        pbW[1][:, 0:512], hsb[:, cc * NTOK:(cc + 2) * NTOK],
                        pbW[0][:, 512:1024], ALU.mult), waits=[tiv2c, ts2c],
                        inc="V")
                    p.emit("vector", lambda e, cc=cc: e.tensor_tensor(
                        pbW[1][:, 512:1024],
                        hsb[:, (cc + 2) * NTOK:(cc + 4) * NTOK],
                        pbW[0][:, 512:1024], ALU.mult), waits=[], inc="V")
                    p.emit("vector", lambda e, u0v=u0v: e.tensor_scalar(
                        u0v, u0v, 128.0, None, ALU.add), waits=[], inc="V")
                    srcv = u0v
                    for ki, k in enumerate(range(7, -1, -1)):
                        cth = float(2 ** k) - 0.5
                        p.emit("vector", lambda e, srcv=srcv, cth=cth, k=k:
                               e.tensor_scalar(
                            bsbB[:, :], srcv, cth, -float(2 ** k),
                            ALU.is_ge, ALU.mult), waits=[], inc="V")
                        dstv = ufinB[:, :] if k == 0 else (pgA if ki % 2 == 0 else pgB)
                        p.emit("vector", lambda e, srcv=srcv, dstv=dstv:
                               e.tensor_tensor(
                            dstv, bsbB[:, :], srcv, ALU.add), waits=[], inc="V")
                        srcv = dstv
                    p.emit("vector", lambda e, u0v=u0v: e.scalar_tensor_tensor(
                        bsbB[:, :], ufinB[:, :], -1.0, u0v, ALU.mult, ALU.add),
                        waits=[], inc="V")
                    ti2 = p.emit("vector", lambda e, cc=cc: e.tensor_scalar(
                        i2sb[:, cc * NTOK:(cc + 4) * NTOK], bsbB[:, :], -128.0,
                        None, ALU.add), waits=[], inc="V")

                s3_toks = []
                for mc in range(KC):
                    reg = pb[4][:, 0:NTOK] if mc % 2 == 0 else pb[5][:, 0:NTOK]
                    reg2 = (pb[4][:, NTOK:2 * NTOK] if mc % 2 == 0
                            else pb[5][:, NTOK:2 * NTOK])
                    last = None
                    for k in range(OC):
                        fst = (k == 0)
                        lst = (k == OC - 1)
                        last = p.emit("tensor", lambda e, k=k, mc=mc, reg=reg,
                                      fst=fst, lst=lst: e.matmul(
                            reg, td_sb[:, k, mc * 128:(mc + 1) * 128],
                            i2sb[:, k * NTOK:(k + 1) * NTOK], start=fst, stop=lst),
                            waits=([ti2, (s3_toks[mc - 2] if mc >= 2 else None)]
                                   if fst else []), inc="PE" if lst else None)
                    p.emit("vector", lambda e, reg=reg, reg2=reg2:
                           e.scalar_tensor_tensor(
                        reg2, reg, mghat[:, 2:3], iv2bc[:, :], ALU.mult,
                        ALU.mult), waits=[last], inc="V")
                    s3_toks.append(p.emit("vector", lambda e, mc=mc, par=par,
                                          reg2=reg2: e.tensor_tensor(
                        s3sb[:, mc, :], x2b[:, 0, mc, :], reg2, ALU.add),
                        waits=[], inc="V"))
                xb_free[t] = s3_toks[-1]
                prefetch_b(t + 1)

                sq_pe3 = []
                mm3_last = None
                for c in range(KC):
                    tsqr = p.emit("scalar", lambda e, c=c: e.activation(
                        sqB[:, c % 2, :], s3sb[:, c, :], ACT.Square),
                        waits=[s3_toks[c], (sq_pe3[c - 2] if c >= 2 else None)],
                        inc="S")
                    mm3_last = p.emit("tensor", lambda e, c=c: e.matmul(
                        pb[0][0:1, 0:NTOK], ones_c[:, :], s3sb[:, c, :],
                        start=(c == 0), stop=(c == KC - 1), skip_group_check=True),
                        waits=[s3_toks[c]], inc="PE" if c == KC - 1 else None)
                    sq_pe3.append(p.emit("tensor", lambda e, c=c: e.matmul(
                        pb[1][0:1, 0:NTOK], ones_c[:, :], sqB[:, c % 2, :],
                        start=(c == 0), stop=(c == KC - 1), skip_group_check=True), waits=[tsqr],
                        inc="PE"))
                tmu = p.emit("vector", lambda e: e.tensor_scalar(
                    rows[0:1, 0:NTOK], pb[0][0:1, 0:NTOK], 1.0 / D, None,
                    ALU.mult), waits=[sq_pe3[-1], mm3_last], inc="V")
                p.emit("vector", lambda e: e.tensor_scalar(
                    rows[0:1, NTOK:2 * NTOK], pb[1][0:1, 0:NTOK], 1.0 / D,
                    None, ALU.mult), waits=[], inc="V")
                p.emit("vector", lambda e: e.tensor_tensor(
                    rows2[0:1, 0:NTOK], rows[0:1, 0:NTOK], rows[0:1, 0:NTOK],
                    ALU.mult), waits=[], inc="V")
                tvar = p.emit("vector", lambda e: e.tensor_tensor(
                    rows2[0:1, NTOK:2 * NTOK], rows[0:1, NTOK:2 * NTOK],
                    rows2[0:1, 0:NTOK], ALU.subtract), waits=[], inc="V")
                tsq = p.emit("scalar", lambda e: e.activation(
                    rows2[0:1, 0:NTOK], rows2[0:1, NTOK:2 * NTOK], ACT.Sqrt,
                    bias=eps1[:, :]), waits=[tvar], inc="S")
                trr = p.emit("vector", lambda e: e.reciprocal(
                    rows[0:1, NTOK:2 * NTOK], rows2[0:1, 0:NTOK]),
                    waits=[tsq], inc="V")
                tbc1 = p.emit("tensor", lambda e: e.matmul(
                    pb[0][:, 0:NTOK], ones_r[:, :], rows[0:1, 0:NTOK],
                    start=True, stop=True), waits=[tmu, trr], inc="PE")
                tbc2 = p.emit("tensor", lambda e: e.matmul(
                    pb[0][:, NTOK:2 * NTOK], ones_r[:, :],
                    rows[0:1, NTOK:2 * NTOK], start=True, stop=True),
                    waits=[trr], inc="PE")
                trbc = p.emit("vector", lambda e: e.tensor_copy(
                    rbcB[:, :], pb[0][:, NTOK:2 * NTOK]), waits=[tbc2], inc="V")
                for c in range(KC):
                    reg = pb[2] if c % 2 == 0 else pb[3]
                    p.emit("vector", lambda e, c=c, reg=reg: e.tensor_tensor(
                        reg[:, 0:NTOK], s3sb[:, c, :], pb[0][:, 0:NTOK],
                        ALU.subtract), waits=[tbc2, prev_out], inc="V")
                    otok = p.emit("vector", lambda e, c=c, reg=reg: e.tensor_tensor(
                        s3sb[:, c, :], reg[:, 0:NTOK], rbcB[:, :], ALU.mult),
                        waits=[trbc], inc="V")
                    prev_out = p.emit("gpsimd", lambda e, t=t, c=c: e.dma_start(
                        out=out_t[:, t, :].rearrange("(k p) n -> p k n", p=128)[
                            :, c, :],
                        in_=s3sb[:, c, :]), waits=[otok], inc="dG", inc_n=16)
                barB_V = p.ck("V")

            final_dg = p.ck("dG")
            p.emit("gpsimd", lambda e: e.engine_nop(), waits=[final_dg], inc="G")

        block = es.enter_context(nc.Block())
        p.build(block)

    return nc, dumps


def _host_inputs(core, L0_attn_x, L0_attn_WQ, L0_attn_bQ, L0_attn_WK, L0_attn_bK,
                 L0_attn_WV, L0_attn_bV, L0_attn_WO, L0_attn_bO, L0_attn_Wg,
                 L0_attn_bg, L0_attn_alpha_logit, L0_attn_ln_gamma,
                 L0_attn_ln_beta, L0_ffn_w_gate, L0_ffn_w_up, L0_ffn_w_down):
    xc = np.asarray(L0_attn_x)[4 * core:4 * core + 4]
    x_t = np.ascontiguousarray(
        xc.transpose(2, 1, 0).reshape(D, NT, C, BC).transpose(0, 1, 3, 2)
        .reshape(D, NT, NTOK))

    def colform(b):
        return np.ascontiguousarray(np.asarray(b, np.float32).reshape(KC, 128).T)
    a = np.asarray(L0_attn_alpha_logit, np.float32).reshape(KC, 2, 64)
    alpha_c = np.ascontiguousarray(a.transpose(1, 2, 0).reshape(128, KC))
    pp, ff = np.meshgrid(np.arange(128), np.arange(NTOK), indexing="ij")
    mask_t = ((pp % 64) <= (ff % 64)).astype(np.float32)
    return {
        "x_t": x_t,
        "wq_t": np.ascontiguousarray(np.asarray(L0_attn_WQ, np.float32).T),
        "wk_t": np.ascontiguousarray(np.asarray(L0_attn_WK, np.float32).T),
        "wv_t": np.ascontiguousarray(np.asarray(L0_attn_WV, np.float32).T),
        "wg_t": np.ascontiguousarray(np.asarray(L0_attn_Wg, np.float32).T),
        "wo_t": np.ascontiguousarray(np.asarray(L0_attn_WO, np.float32).T),
        "wg1_t": np.ascontiguousarray(np.asarray(L0_ffn_w_gate, np.float32).T),
        "wu1_t": np.ascontiguousarray(np.asarray(L0_ffn_w_up, np.float32).T),
        "wd1_t": np.ascontiguousarray(np.asarray(L0_ffn_w_down, np.float32).T),
        "bq_c": colform(L0_attn_bQ), "bk_c": colform(L0_attn_bK),
        "bv_c": colform(L0_attn_bV), "bg_c": colform(L0_attn_bg),
        "bo_c": colform(L0_attn_bO),
        "gamma_c": colform(L0_attn_ln_gamma), "beta_c": colform(L0_attn_ln_beta),
        "alpha_c": alpha_c,
        "bv_row": np.ascontiguousarray(
            np.asarray(L0_attn_bV, np.float32).reshape(1, DI)),
        "mask_t": mask_t,
        "ones_col": np.ones((128, 1), np.float32),
        "ones_row": np.ones((1, 128), np.float32),
        "ident64": np.eye(64, dtype=np.float32),
    }


def kernel(**inputs):
    nc, _ = build_program()
    in_maps = [_host_inputs(c, **inputs) for c in range(8)]
    res = run_bass_kernel_spmd(nc, in_maps, core_ids=list(range(8)))
    outs = []
    for c in range(8):
        ot = np.asarray(res.results[c]["out_t"])
        oc = (ot.reshape(D, NT, BC, C).transpose(0, 1, 3, 2)
              .reshape(D, T, BC).transpose(2, 1, 0))
        outs.append(oc)
    return np.ascontiguousarray(np.concatenate(outs, axis=0)).astype(np.float32)
